# version 01
# baseline (speedup 1.0000x reference)
import numpy as np
import ml_dtypes
from contextlib import ExitStack
import concourse.bass as bass
import concourse.mybir as mybir
from concourse.bass_utils import run_bass_kernel_spmd

F32 = mybir.dt.float32
BF16 = mybir.dt.bfloat16
I32 = mybir.dt.int32
AF = mybir.ActivationFunctionType
ALU = mybir.AluOpType
BF = ml_dtypes.bfloat16

D = 2048
TOK = 2048
NKT = 16
SEQ = 4096
NH = 16
HD = 128
SCALE0 = float(HD ** -0.5)
SCALE1 = float(1.0 / np.sqrt(HD))
LN_EPS = 1e-5
HF = 8192
RMS_EPS = float(np.finfo(np.float32).eps)
HB = 6
QCH = [64, 128, 128, 256, 256, 256] + [512] * 10
NQC = [TOK // n for n in QCH]
HQ0 = np.cumsum([0] + NQC)
ALL_PARTS = ("l0a", "l0f", "l1a", "l1f")
FUSED = True
TAP = None
L1A_SKIP = set()

M_CT = 0
M_CTB = 512
M_SLB = 1024
M_KC = 1040
M_KTI = 1072
M_C8 = 1104
M_CP = 1232
M_CS = 1264
M_X1 = 1280
M_SL = 1800


def _bt(w, mc=128):
    K, M = w.shape
    return np.ascontiguousarray(w.reshape(K // 128, 128, M // mc, mc).transpose(2, 0, 1, 3))


class Ctx:
    def __init__(self, nc, stack):
        self.nc, self.stack = nc, stack
        self.cnt, self.sems, self.t = {}, {}, {}

    def sem(self, n):
        self.sems[n] = self.stack.enter_context(self.nc.semaphore(n))
        self.cnt[n] = 0

    def inc(self, n, ins, step=1):
        ins.then_inc(self.sems[n], step)
        self.cnt[n] += step
        return self.cnt[n]

    def w(self, eng, n, val=None):
        v = self.cnt[n] if val is None else val
        if v > 0:
            eng.wait_ge(self.sems[n], v)

    def at(self, i, c0=0, c1=TOK):
        return self.t["arena_a"][:, i * TOK + c0:i * TOK + c1]

    def ab(self, i, c0=0, c1=TOK):
        return self.t["arena_b"][:, i * TOK + c0:i * TOK + c1]

    def ps(self, i, n=512, p=128):
        return self.t[f"ps{i}"][0:p, 0:n]

    def m(self, c0, c1, p=128):
        return self.t["misc"][0:p, c0:c1]

    def srow(self, c0, c1):
        return self.t["misc"][0:1, c0:c1]


SEMS = ["xl", "bl", "wl0", "wl1", "ql0", "ql1", "tt", "ss", "vv", "gg",
        "cc", "so", "sl", "pe", "ev", "qv", "kl", "pp", "sd0", "sd1", "sd2", "sd3"]


def build_program(parts):
    parts = [p for p in ALL_PARTS if p in parts]
    nc = bass.Bass("TRN2", target_bir_lowering=False)
    io = {}

    def di(name, shape, dtype=F32):
        io[name] = nc.dram_tensor(name, shape, dtype, kind="ExternalInput").ap()

    di("x_in", [D, TOK])
    di("biases_cat", [144, 128])
    di("coretab", [1, 512])
    di("corei", [1, 8], I32)
    if "l0a" in parts:
        di("wkv_t", [32, NKT, 128, 128])
        di("wq_t", [16, NKT, 128, 128])
        di("wo_t", [16, NKT, 128, 128])
    if "l0f" in parts:
        di("wgu_t", [128, NKT, 128, 128])
        di("wpj_t", [16, HF // 128, 128, 128])
    if "l1a" in parts:
        di("win_t", [32, NKT, 128, 128])
        di("wv1_t", [4, NKT, 128, 512])
        di("wou_t", [16, NKT, 128, 128])
        di("slopes", [1, 16])
        di("vbrow", [1, D])
        di("trimask", [16, 128, 512], BF16)
    if "l1f" in parts:
        di("wb_t", [16, NKT, 128, 128])
        di("rt_t", [NKT, 128, 8])
        di("at_t", [NKT, 128, 128])
        di("bwt_t", [128, D])
        di("rmat", [8, 128])
    y = nc.dram_tensor("y", [D, TOK], F32, kind="ExternalOutput").ap()

    cs_in = nc.dram_tensor("cs_in", [128, 16], F32)
    cs_out = nc.dram_tensor("cs_out", [8 * 128, 16], F32)
    kv_in = nc.dram_tensor("kv_in", [2 * TOK, D], BF16)
    kv_out = nc.dram_tensor("kv_out", [8 * 2 * TOK, D], BF16)
    kv_pair = nc.dram_tensor("kv_pair", [4 * TOK, D], BF16)

    with ExitStack() as st:
        E = st.enter_context
        c = Ctx(nc, st)
        t = c.t
        t["arena_a"] = E(nc.sbuf_tensor("arena_a", [128, NKT * TOK], BF16))
        t["arena_b"] = E(nc.sbuf_tensor("arena_b", [128, NKT * TOK], BF16))
        t["hblk"] = E(nc.sbuf_tensor("hblk", [128, HB * TOK], BF16))
        t["wb0"] = E(nc.sbuf_tensor("wb0", [128, NKT * 128], BF16))
        t["wb1"] = E(nc.sbuf_tensor("wb1", [128, NKT * 128], BF16))
        t["qb0"] = E(nc.sbuf_tensor("qb0", [128, NKT * 128], BF16))
        t["qb1"] = E(nc.sbuf_tensor("qb1", [128, NKT * 128], BF16))
        t["qsc"] = E(nc.sbuf_tensor("qsc", [128, NKT * 128], BF16))
        t["qta"] = E(nc.sbuf_tensor("qta", [128, NKT * 128], BF16))
        t["qw"] = E(nc.sbuf_tensor("qw", [128, NKT * 128], BF16))
        t["scr"] = E(nc.sbuf_tensor("scr", [128, 6144], BF16))
        t["misc"] = E(nc.sbuf_tensor("misc", [128, 2048], F32))
        t["bias"] = E(nc.sbuf_tensor("bias", [128, 144], F32))
        t["ones_bf"] = E(nc.sbuf_tensor("ones_bf", [128, 128], BF16))
        t["ident_bf"] = E(nc.sbuf_tensor("ident_bf", [128, 128], BF16))
        t["ident_f"] = E(nc.sbuf_tensor("ident_f", [128, 128], F32))
        t["selb"] = E(nc.sbuf_tensor("selb", [128, 128], BF16))
        t["onesf"] = E(nc.sbuf_tensor("onesf", [1, 128], F32))
        t["epsc"] = E(nc.sbuf_tensor("epsc", [128, 4], F32))
        t["pb0"] = E(nc.sbuf_tensor("pb0", [128, 512], BF16))
        t["pb1"] = E(nc.sbuf_tensor("pb1", [128, 512], BF16))
        if "l1a" in parts:
            t["kvb"] = E(nc.gpsimd.register("kvb"))
            t["kvt"] = E(nc.gpsimd.register("kvt"))
        for i in range(8):
            t[f"ps{i}"] = E(nc.psum_tensor(f"ps{i}", [128, 512], F32))
        for s in SEMS:
            c.sem(s)
        t["tmpA"] = t["scr"][:, 0:2048]
        t["tmpB"] = t["scr"][:, 2048:4096]
        scf = t["scr"].ap() if hasattr(t["scr"], "ap") else None
        t["scrF"] = t["scr"][:, 0:6144].bitcast(F32)
        t["chunkF"] = t["scr"][:, 4096:6144].bitcast(F32)

        block = E(nc.Block())
        _prolog(c, block, io, parts)
        last = parts[-1]
        if "l0a" in parts:
            _l0_attn(c, block, io, cs_in, cs_out, final=(last == "l0a"), y=y)
        if "l0f" in parts:
            _l0_ffn(c, block, io, final=(last == "l0f"), y=y)
        if "l1a" in parts:
            _l1_attn(c, block, io, kv_in, kv_out, kv_pair, final=(last == "l1a"), y=y)
        if "l1f" in parts:
            _l1_ffn(c, block, io, final=True, y=y)
        @block.sync
        def _(s):
            c.w(s, "so")
            c.w(s, "pp")
    return nc


def _prolog(c, block, io, parts):
    t = c.t

    @block.gpsimd
    def _(g):
        xr = io["x_in"].rearrange("(kt p) t -> kt p t", p=128)
        for i in range(NKT):
            c.inc("xl", g.dma_start(out=c.at(i), in_=xr[i]), 16)
        c.inc("gg", g.memset(t["ones_bf"][:], 1.0))
        c.inc("gg", g.memset(t["onesf"][:], 1.0))
        c.inc("gg", g.memset(t["epsc"][:, 0:1], LN_EPS))
        c.inc("gg", g.memset(t["epsc"][:, 1:2], RMS_EPS))
        c.inc("gg", g.memset(t["ident_bf"][:], 1.0))
        c.w(g, "gg")
        c.inc("gg", g.affine_select(out=t["ident_bf"][:], in_=t["ident_bf"][:],
                                    pattern=[[-1, 128]], compare_op=ALU.is_equal,
                                    fill=0.0, base=0, channel_multiplier=1))
        c.inc("gg", g.memset(t["ident_f"][:], 1.0))
        c.w(g, "gg")
        c.inc("gg", g.affine_select(out=t["ident_f"][:], in_=t["ident_f"][:],
                                    pattern=[[-1, 128]], compare_op=ALU.is_equal,
                                    fill=0.0, base=0, channel_multiplier=1))
        c.inc("gg", g.memset(t["selb"][:], 0.0))
        c.w(g, "gg")
        c.inc("gg", g.memset(t["selb"][0:64, 0:64], 1.0 / 64.0))
        c.inc("gg", g.memset(t["selb"][64:128, 64:128], 1.0 / 64.0))
        c.inc("gg", g.iota(c.m(M_KC, M_KC + 32), [[128, 32]], base=0, channel_multiplier=1,
                           allow_small_or_imprecise_dtypes=True))
        c.inc("gg", g.iota(c.m(M_KTI, M_KTI + 32), [[1, 32]], base=0, channel_multiplier=0,
                           allow_small_or_imprecise_dtypes=True))


    @block.sync
    def _(s):
        scrf = t["scrF"]
        c.inc("bl", s.dma_start(out=scrf[:, 0:128], in_=io["biases_cat"][0:128, :]), 16)
        c.inc("bl", s.dma_start(out=scrf[0:16, 128:256], in_=io["biases_cat"][128:144, :]), 16)
        c.inc("sl", s.dma_start(out=c.srow(M_CT, M_CT + 512), in_=io["coretab"]), 16)
        if "l1a" in parts:
            c.inc("sl", s.dma_start(out=c.srow(M_SL, M_SL + 16), in_=io["slopes"]), 16)
            c.inc("sl", s.dma_start(out=t["misc"][0:1, 1840:1848].bitcast(I32), in_=io["corei"]), 16)

    @block.tensor
    def _(tt):
        c.w(tt, "bl"); c.w(tt, "gg"); c.w(tt, "sl")
        tt.transpose(c.ps(0, 128), t["scrF"][:, 0:128], t["ident_f"][:])
        tt.transpose(c.ps(1, 16)[0:128, 0:16], t["scrF"][0:16, 128:256], t["ident_f"][0:16, 0:16])
        ins = tt.matmul(c.ps(2, 512), t["onesf"][0:1, :], c.srow(M_CT, M_CT + 512),
                        start=True, stop=True)
        if "l1a" in parts:
            ins = tt.matmul(c.ps(3, 16), t["onesf"][0:1, :], c.srow(M_SL, M_SL + 16),
                            start=True, stop=True)
        c.inc("tt", ins)

    @block.scalar
    def _(s):
        c.w(s, "tt")
        s.copy(out=t["bias"][:, 0:128], in_=c.ps(0, 128))
        s.copy(out=t["bias"][:, 128:144], in_=c.ps(1, 16)[0:128, 0:16])
        ins = s.copy(out=c.m(M_CTB, M_CTB + 512), in_=c.ps(2, 512))
        if "l1a" in parts:
            ins = s.copy(out=c.m(M_SLB, M_SLB + 16), in_=c.ps(3, 16))
        c.inc("ss", ins)


def _stream(c, g, dram, ci, buf_slot, nkt=NKT, guard=None):
    if guard is not None:
        c.w(g, guard[0], guard[1])
    buf = c.t["wb0"] if buf_slot == 0 else c.t["wb1"]
    c.inc(f"wl{buf_slot}",
          g.dma_start(out=buf[:, 0:nkt * 128].rearrange("p (kt m) -> p kt m", m=128),
                      in_=dram[ci].transpose([1, 0, 2])), 16)
    return buf


def _post_ln(c, block, u, out, final, y):
    t = c.t
    ut = c.at if u == "a" else c.ab
    ot = c.at if out == "a" else c.ab
    usq = t["tmpA"][:, 0:512]
    scf = t["scrF"]
    b_ss, b_vv, b_pe = c.cnt["ss"], c.cnt["vv"], c.cnt["pe"]

    @block.scalar
    def _(s):
        c.w(s, "vv")
        c.w(s, "tt")
        c.w(s, "ss")
        for i in range(NKT):
            for ch in range(4):
                if i * 4 + ch > 0:
                    c.w(s, "pe", b_pe + i * 4 + ch)
                c.inc("ss", s.activation(out=usq, in_=ut(i, ch * 512, ch * 512 + 512),
                                         func=AF.Square))

    @block.tensor
    def _(tt):
        for i in range(NKT):
            for ch in range(4):
                c.w(tt, "ss", b_ss + i * 4 + ch + 1)
                tt.matmul(c.ps(ch), t["ones_bf"][:], ut(i, ch * 512, ch * 512 + 512),
                          start=(i == 0), stop=(i == NKT - 1))
                ins = tt.matmul(c.ps(4 + ch), t["ones_bf"][:], usq,
                                start=(i == 0), stop=(i == NKT - 1))
                if i == NKT - 1 and ch == 3:
                    c.inc("tt", ins)
                else:
                    c.inc("pe", ins)

    b_tt2 = c.cnt["tt"]
    for ch in range(4):
        mloc = scf[:, 1024:1536]
        rloc = t["chunkF"][:, 512:1024]
        mrloc = t["chunkF"][:, 0:512]

        @block.vector
        def _(v):
            if ch == 0:
                c.w(v, "tt", b_tt2)
            else:
                c.w(v, "vv")
            c.inc("vv", v.tensor_scalar(out=mloc, in0=c.ps(ch), scalar1=1.0 / D, scalar2=None,
                            op0=ALU.mult))
            nm2 = t["chunkF"][:, 0:512]
            c.w(v, "vv")
            c.inc("vv", v.scalar_tensor_tensor(out=nm2, in0=mloc, scalar=-1.0, in1=mloc,
                                   op0=ALU.mult, op1=ALU.mult))
            var = t["chunkF"][:, 512:1024]
            c.w(v, "vv")
            c.inc("vv", v.scalar_tensor_tensor(out=var, in0=c.ps(4 + ch), scalar=1.0 / D, in1=nm2,
                                   op0=ALU.mult, op1=ALU.add))

        b_ss2 = c.cnt["ss"]

        @block.scalar
        def _(s):
            c.w(s, "vv")
            var = t["chunkF"][:, 512:1024]
            lnv = t["chunkF"][:, 0:512]
            c.inc("ss", s.activation(out=lnv, in_=var, func=AF.Ln, bias=t["epsc"][:, 0:1], scale=1.0))
            c.w(s, "ss")
            c.inc("ss", s.activation(out=rloc, in_=lnv, func=AF.Exp, scale=-0.5))

        @block.vector
        def _(v):
            c.w(v, "ss", b_ss2 + 2)
            c.inc("vv", v.tensor_mul(out=mrloc, in0=mloc, in1=rloc))
            tmp = scf[:, 1536:2048]
            for i in range(NKT):
                c.w(v, "vv")
                c.inc("vv", v.tensor_mul(out=tmp, in0=ut(i, ch * 512, ch * 512 + 512), in1=rloc))
                c.w(v, "vv")
                c.inc("vv", v.tensor_sub(out=ot(i, ch * 512, ch * 512 + 512), in0=tmp, in1=mrloc))
    if final:
        b_vv3 = c.cnt["vv"]

        @block.gpsimd
        def _(g):
            yr = y.rearrange("(kt p) t -> kt p t", p=128)
            for i in range(NKT):
                c.w(g, "vv", b_vv3 - 2 * (NKT - 1 - i))
                c.inc("pp", g.dma_start(out=yr[i], in_=ot(i)), 16)


def _phi_chain(c, emitters, et, ch, psbank, bias_col, dest, tok_tt):
    raise NotImplementedError


def _l0_attn(c, block, io, cs_in, cs_out, final, y):
    t = c.t
    temp1 = t["chunkF"][:, 0:512]
    temp2 = t["chunkF"][:, 512:1024]

    b_tt = c.cnt["tt"]
    b_wl0, b_wl1 = c.cnt["wl0"], c.cnt["wl1"]

    @block.gpsimd
    def _(g):
        c.w(g, "xl")
        for ci in range(32):
            _stream(c, g, io["wkv_t"], ci, ci % 2,
                    guard=("tt", b_tt + 4 * (ci - 1)) if ci >= 2 else None)

    b_ss_a = c.cnt["ss"]

    @block.tensor
    def _(tt):
        c.w(tt, "ss", b_ss_a)
        gidx = 0
        for et in range(16):
            for kv in range(2):
                ci = 2 * et + kv
                c.w(tt, f"wl{ci % 2}", (b_wl0 if ci % 2 == 0 else b_wl1) + 16 * (ci // 2 + 1))
                buf = t["wb0"] if ci % 2 == 0 else t["wb1"]
                for ch in range(4):
                    bank = gidx % 4
                    if gidx >= 4:
                        gp = gidx - 4
                        ep, rr = gp // 8, gp % 8
                        etok = (b_ss_a + ep * 12 + 2 * (rr % 4) + 1) if rr < 4 else                                (b_ss_a + ep * 12 + 8 + (rr % 4) + 1)
                        c.w(tt, "ss", etok)
                    for kt in range(NKT):
                        ins = tt.matmul(c.ps(bank), buf[:, kt * 128:kt * 128 + 128],
                                        c.at(kt, ch * 512, ch * 512 + 512),
                                        start=(kt == 0), stop=(kt == NKT - 1))
                    c.inc("tt", ins)
                    gidx += 1

    b_ss, b_vv = b_ss_a, c.cnt["vv"]
    A = lambda et, ch: b_ss + et * 12 + 2 * ch + 1
    C = lambda et, ch: b_ss + et * 12 + 2 * ch + 2
    V = lambda et, ch: b_ss + et * 12 + 8 + ch + 1
    Bt = lambda et, ch: b_vv + et * 10 + 2 * ch + 1
    Dt = lambda et, ch: b_vv + et * 10 + 2 * ch + 2
    KVt = lambda et: b_vv + et * 10 + 9
    CSt = lambda et: b_vv + et * 10 + 10

    @block.scalar
    def _(s):
        for et in range(16):
            for ch in range(4):
                gk = et * 8 + ch
                c.w(s, "tt", b_tt + gk + 1)
                if et > 0 and ch == 0:
                    c.w(s, "vv", CSt(et - 1))
                elif ch > 0:
                    c.w(s, "vv", Dt(et, ch - 1))
                ins = s.activation(out=temp1, in_=c.ps(gk % 4), func=AF.Identity,
                                   bias=t["bias"][:, 16 + et:17 + et], scale=1.0)
                c.inc("ss", ins)
                c.w(s, "vv", Bt(et, ch))
                c.inc("ss", s.activation(out=temp2, in_=t["tmpA"][:, ch * 512:ch * 512 + 512],
                                         func=AF.Exp))
            for ch in range(4):
                gv = et * 8 + 4 + ch
                c.w(s, "tt", b_tt + gv + 1)
                ins = s.activation(out=t["tmpB"][:, ch * 512:ch * 512 + 512],
                                   in_=c.ps(gv % 4), func=AF.Identity,
                                   bias=t["bias"][:, 32 + et:33 + et], scale=1.0)
                c.inc("ss", ins)

    @block.vector
    def _(v):
        for et in range(16):
            for ch in range(4):
                c.w(v, "ss", A(et, ch))
                c.inc("vv", v.tensor_scalar(out=t["tmpA"][:, ch * 512:ch * 512 + 512], in0=temp1,
                                scalar1=0.0, scalar2=None, op0=ALU.min))
                c.w(v, "ss", C(et, ch))
                c.inc("vv", v.scalar_tensor_tensor(out=t["tmpA"][:, ch * 512:ch * 512 + 512],
                                       in0=temp1, scalar=0.0, in1=temp2,
                                       op0=ALU.max, op1=ALU.add))
            c.w(v, "ss", V(et, 3))
            c.w(v, "vv", Dt(et, 3))
            c.inc("vv", v.tensor_mul(out=c.ab(et), in0=t["tmpA"], in1=t["tmpB"]))
            c.w(v, "vv", KVt(et))
            c.inc("vv", v.reduce_sum(out=c.m(M_CS + et, M_CS + et + 1), in_=c.ab(et),
                         axis=mybir.AxisListType.X))

    b_so = c.cnt["so"]

    @block.sync
    def _(s):
        c.w(s, "vv")
        c.inc("so", s.dma_start(out=cs_in[:, :], in_=c.m(M_CS, M_CS + 16)), 16)

    b_xl = c.cnt["xl"]

    @block.gpsimd
    def _(g):
        c.w(g, "so", b_so + 16)
        c.inc("cc", g.collective_compute(
            "AllGather", ALU.bypass, replica_groups=[list(range(8))],
            ins=[cs_in.ap().opt()], outs=[cs_out.ap().opt()]))
        c.w(g, "cc")
        c.inc("xl", g.dma_start(
            out=c.m(M_C8, M_C8 + 128).rearrange("p (r t) -> p r t", t=16),
            in_=cs_out.ap().rearrange("(r p) t -> r p t", p=128).transpose([1, 0, 2])), 16)

    b_vv2 = c.cnt["vv"]

    @block.vector
    def _(v):
        c.w(v, "xl", b_xl + 16)
        cp = [c.m(M_CP, M_CP + 16), c.m(M_CP + 16, M_CP + 32)]
        c.inc("vv", v.tensor_scalar(out=cp[0], in0=c.m(M_C8, M_C8 + 16),
                        scalar1=c.m(M_CTB, M_CTB + 1), scalar2=None, op0=ALU.mult))
        for r in range(1, 8):
            c.w(v, "vv")
            c.inc("vv", v.scalar_tensor_tensor(out=cp[r % 2], in0=c.m(M_C8 + 16 * r, M_C8 + 16 * r + 16),
                                   scalar=c.m(M_CTB + r, M_CTB + r + 1),
                                   in1=cp[(r + 1) % 2], op0=ALU.mult, op1=ALU.add))

    b_tt2 = c.cnt["tt"]
    b_ss2 = c.cnt["ss"]
    b_wl0, b_wl1 = c.cnt["wl0"], c.cnt["wl1"]

    @block.gpsimd
    def _(g):
        for ci in range(16):
            _stream(c, g, io["wq_t"], ci, ci % 2,
                    guard=("tt", b_tt2 + 4 * (ci - 1)) if ci >= 2 else ("tt", b_tt2))

    @block.tensor
    def _(tt):
        c.w(tt, "ss", b_ss2)
        gidx = 0
        for et in range(16):
            ci = et
            c.w(tt, f"wl{ci % 2}", (b_wl0 if ci % 2 == 0 else b_wl1) + 16 * (ci // 2 + 1))
            buf = t["wb0"] if ci % 2 == 0 else t["wb1"]
            for ch in range(4):
                bank = gidx % 4
                if gidx >= 4:
                    gp = gidx - 4
                    c.w(tt, "ss", b_ss2 + 2 * (gp % 4) + (gp // 4) * 8 + 1)
                for kt in range(NKT):
                    ins = tt.matmul(c.ps(bank), buf[:, kt * 128:kt * 128 + 128],
                                    c.at(kt, ch * 512, ch * 512 + 512),
                                    start=(kt == 0), stop=(kt == NKT - 1))
                c.inc("tt", ins)
                gidx += 1

    assert c.cnt["ss"] == b_ss2, (c.cnt["ss"], b_ss2)
    b_vv3 = c.cnt["vv"]
    A2 = lambda et, ch: b_ss2 + et * 8 + 2 * ch + 1
    C2 = lambda et, ch: b_ss2 + et * 8 + 2 * ch + 2
    B2 = lambda et, ch: b_vv3 + et * 10 + 2 * ch + 1
    D2 = lambda et, ch: b_vv3 + et * 10 + 2 * ch + 2
    SCt = lambda et: b_vv3 + et * 10 + 9
    CXt = lambda et: b_vv3 + et * 10 + 10

    @block.scalar
    def _(s):
        for et in range(16):
            for ch in range(4):
                gk = et * 4 + ch
                c.w(s, "tt", b_tt2 + gk + 1)
                if et == 0 and ch == 0:
                    c.w(s, "vv", b_vv3)
                elif ch == 0:
                    c.w(s, "vv", CXt(et - 1))
                else:
                    c.w(s, "vv", D2(et, ch - 1))
                ins = s.activation(out=temp1, in_=c.ps(gk % 4), func=AF.Identity,
                                   bias=t["bias"][:, 0 + et:1 + et], scale=1.0)
                c.inc("ss", ins)
                c.w(s, "vv", B2(et, ch))
                c.inc("ss", s.activation(out=temp2, in_=t["tmpA"][:, ch * 512:ch * 512 + 512],
                                         func=AF.Exp))

    @block.vector
    def _(v):
        c.w(v, "vv", b_vv2 + 8)
        for et in range(16):
            for ch in range(4):
                c.w(v, "ss", A2(et, ch))
                c.inc("vv", v.tensor_scalar(out=t["tmpA"][:, ch * 512:ch * 512 + 512], in0=temp1,
                                scalar1=0.0, scalar2=None, op0=ALU.min))
                c.w(v, "ss", C2(et, ch))
                c.inc("vv", v.scalar_tensor_tensor(out=t["tmpA"][:, ch * 512:ch * 512 + 512],
                                       in0=temp1, scalar=0.0, in1=temp2,
                                       op0=ALU.max, op1=ALU.add))
            c.w(v, "vv")
            c.inc("vv", v.tensor_tensor_scan(out=t["tmpB"], data0=c.ab(et), data1=c.ab(et),
                                 initial=c.m(M_CP + 16 + et, M_CP + 17 + et),
                                 op0=ALU.add, op1=ALU.bypass))
            c.w(v, "vv")
            c.inc("vv", v.scalar_tensor_tensor(out=c.ab(et), in0=t["tmpB"], scalar=SCALE0,
                                   in1=t["tmpA"], op0=ALU.mult, op1=ALU.mult))

    b_tt3 = c.cnt["tt"]
    b_ss3 = c.cnt["ss"]
    b_wl0, b_wl1 = c.cnt["wl0"], c.cnt["wl1"]

    @block.gpsimd
    def _(g):
        for ci in range(16):
            _stream(c, g, io["wo_t"], ci, ci % 2,
                    guard=("tt", b_tt3 + 4 * (ci - 1)) if ci >= 2 else ("tt", b_tt3))

    @block.tensor
    def _(tt):
        c.w(tt, "vv")
        c.w(tt, "ss", b_ss3)
        gidx = 0
        for ot in range(16):
            ci = ot
            c.w(tt, f"wl{ci % 2}", (b_wl0 if ci % 2 == 0 else b_wl1) + 16 * (ci // 2 + 1))
            buf = t["wb0"] if ci % 2 == 0 else t["wb1"]
            for ch in range(4):
                bank = gidx % 4
                if gidx >= 4:
                    c.w(tt, "ss", b_ss3 + gidx - 3)
                for kt in range(NKT):
                    tt.matmul(c.ps(bank), buf[:, kt * 128:kt * 128 + 128],
                              c.ab(kt, ch * 512, ch * 512 + 512),
                              start=(kt == 0), stop=False)
                ins = tt.matmul(c.ps(bank), t["ident_bf"][:],
                                c.at(ot, ch * 512, ch * 512 + 512),
                                start=False, stop=True)
                c.inc("tt", ins)
                gidx += 1

    @block.scalar
    def _(s):
        for ot in range(16):
            for ch in range(4):
                gk = ot * 4 + ch
                c.w(s, "tt", b_tt3 + gk + 1)
                ins = s.activation(out=c.at(ot, ch * 512, ch * 512 + 512),
                                   in_=c.ps(gk % 4), func=AF.Identity,
                                   bias=t["bias"][:, 48 + ot:49 + ot], scale=1.0)
                c.inc("ss", ins)

    _post_ln(c, block, "a", "a", final, y)


def _l0_ffn(c, block, io, final, y):
    t = c.t
    gsil = lambda ch: t["scr"][:, 4096 + ch * 512:4096 + ch * 512 + 512]
    qb = [t["qb0"], t["qb1"]]
    blocks = [list(range(i, min(i + HB, 64))) for i in range(0, 64, HB)]

    SC = []
    for bi, blk in enumerate(blocks):
        for local, pr in enumerate(blk):
            for gu in (0, 1):
                SC.append(("wchunk", bi, local, pr, gu))
        for dc in range(16):
            SC.append(("pchunk", bi, dc))

    cnt = {k: c.cnt[k] for k in ("ss", "vv", "tt", "pe", "ql0", "ql1")}
    tok = {}
    qn = 0
    scnt = 0
    gcnt = 0
    qchain = []
    projev = []
    siltok = {}
    hactok = {}
    rmmtok = {}
    acctok = {}

    def plan_quant(ncols):
        rec = {"qn": qn}
        cnt["ss"] += 1; rec["abs"] = cnt["ss"]
        nsb = (ncols + 511) // 512
        rec["smm"] = []
        rec["sev"] = []
        for sb in range(nsb):
            cnt["pe"] += 1; rec["smm"].append(cnt["pe"])
            cnt["ss"] += 1; rec["sev"].append(cnt["ss"])
        for op in ("a", "b", "q", "wh"):
            cnt["vv"] += 1; rec[op] = cnt["vv"]
        rec["nsb"] = nsb
        rec["ncols"] = ncols
        return rec

    plan = []
    for e in SC:
        if e[0] == "wchunk":
            bi, local, pr, gu = e[1:]
            rec = {"kind": "w", "bi": bi, "local": local, "pr": pr, "gu": gu,
                   "ci": 2 * pr + gu}
            cnt[f"ql{qn % 2}"] += 16
            rec["dmatok"] = cnt[f"ql{qn % 2}"]
            rec.update(plan_quant(2048))
            qchain.append(rec)
            rec["qidx"] = len(qchain) - 1
            rec["gmm"] = []
            rec["gev"] = []
            nonlocal_scnt = None
            for ch in range(4):
                cnt["tt"] += 1; rec["gmm"].append(cnt["tt"])
                if gu == 0:
                    cnt["ss"] += 1; rec["gev"].append(("ss", cnt["ss"]))
                    siltok[(pr, ch)] = cnt["ss"]
                else:
                    cnt["vv"] += 1; rec["gev"].append(("vv", cnt["vv"]))
                    hactok[(pr, ch)] = cnt["vv"]
                projev.append(rec["gev"][-1])
            if gu == 1:
                rec["rms"] = []
                for ch in range(4):
                    cnt["ss"] += 1
                    cnt["pe"] += 1
                    rec["rms"].append((cnt["ss"], cnt["pe"]))
                    rmmtok[(pr, ch)] = cnt["pe"]
            plan.append(rec)
        else:
            bi, dc = e[1], e[2]
            blk = blocks[bi]
            L = len(blk)
            rec = {"kind": "p", "bi": bi, "dc": dc, "L": L}
            cnt[f"ql{qn % 2}"] += 16
            rec["dmatok"] = cnt[f"ql{qn % 2}"]
            rec.update(plan_quant(L * 128))
            qchain.append(rec)
            rec["qidx"] = len(qchain) - 1
            rec["pmm"] = []
            rec["pev"] = []
            for ch in range(4):
                cnt["tt"] += 1; rec["pmm"].append(cnt["tt"])
                if bi == 0:
                    cnt["ss"] += 1; rec["pev"].append(("ss", cnt["ss"]))
                else:
                    cnt["vv"] += 1; rec["pev"].append(("vv", cnt["vv"]))
                acctok[(bi, dc, ch)] = rec["pev"][-1][1]
                projev.append(rec["pev"][-1])
            plan.append(rec)
        qn += 1

    si = 0
    gi = 0
    for rec in plan:
        rec["sbank"] = []
        for sb in range(rec["nsb"]):
            rec["sbank"].append(si % 2)
            rec.setdefault("sguard", []).append(si - 2)
            si += 1
        rec["gbank"] = []
        for ch in range(4):
            rec["gbank"].append(2 + gi % 2)
            rec.setdefault("gguard", []).append(gi - 2)
            gi += 1
    sevs = [tk for rec in plan for tk in rec["sev"]]

    b0 = {k: c.cnt[k] for k in ("ss", "vv", "tt", "pe")}

    @block.gpsimd
    def _(g):
        for rec in plan:
            i = rec["qidx"]
            if i >= 2:
                c.w(g, "vv", qchain[i - 2]["wh"])
            slot = i % 2
            if rec["kind"] == "w":
                src_ = io["wgu_t"][rec["ci"]].transpose([1, 0, 2])
                ncol = 2048
            else:
                blk = blocks[rec["bi"]]
                src_ = io["wpj_t"][rec["dc"], blk[0]:blk[0] + rec["L"]].transpose([1, 0, 2])
                ncol = rec["L"] * 128
            c.inc(f"ql{i % 2}",
                  g.dma_start(out=qb[slot][:, 0:ncol].rearrange("p (kt m) -> p kt m", m=128),
                              in_=src_), 16)

    @block.tensor
    def _(tt):
        c.w(tt, "ss", b0["ss"])
        for rec in plan:
            i = rec["qidx"]
            for sb in range(rec["nsb"]):
                n = min(512, rec["ncols"] - sb * 512)
                c.w(tt, "ss", rec["abs"])
                gidx = rec["sguard"][sb]
                if gidx >= 0:
                    c.w(tt, "ss", sevs[gidx])
                bank = rec["sbank"][sb]
                nkts = n // 128
                for k2 in range(nkts):
                    kt = sb * 4 + k2
                    ins = tt.matmul(c.ps(bank)[:, k2 * 128:k2 * 128 + 128], t["selb"][:],
                                    t["qta"][:, kt * 128:kt * 128 + 128],
                                    start=True, stop=True)
                c.inc("pe", ins)
            c.w(tt, "vv", rec["wh"])
            for ch in range(4):
                gidx = rec["gguard"][ch]
                if gidx >= 0:
                    c.w(tt, projev[gidx][0], projev[gidx][1])
                bank = rec["gbank"][ch]
                if rec["kind"] == "w":
                    for kt in range(NKT):
                        ins = tt.matmul(c.ps(bank), t["qw"][:, kt * 128:kt * 128 + 128],
                                        c.at(kt, ch * 512, ch * 512 + 512),
                                        start=(kt == 0), stop=(kt == NKT - 1))
                else:
                    for j in range(rec["L"]):
                        ins = tt.matmul(c.ps(bank), t["qw"][:, j * 128:j * 128 + 128],
                                        t["hblk"][:, j * TOK + ch * 512:j * TOK + ch * 512 + 512],
                                        start=(j == 0), stop=(j == rec["L"] - 1))
                c.inc("tt", ins)
            if rec["kind"] == "w" and rec["gu"] == 1:
                for ch in range(4):
                    c.w(tt, "ss", rec["rms"][ch][0])
                    ins = tt.matmul(c.ps(4 + ch), t["ones_bf"][:], gsil(ch),
                                    start=(rec["pr"] == 0), stop=(rec["pr"] == 63))
                    c.inc("pe", ins)

    @block.scalar
    def _(s):
        for rec in plan:
            i = rec["qidx"]
            slot = i % 2
            ncol = rec["ncols"]
            c.w(s, f"ql{i % 2}", rec["dmatok"])
            if i >= 1:
                c.w(s, "vv", qchain[i - 1]["q"])
            c.inc("ss", s.activation(out=t["qta"][:, 0:ncol], in_=qb[slot][:, 0:ncol],
                                     func=AF.Abs))
            for sb in range(rec["nsb"]):
                n = min(512, ncol - sb * 512)
                c.w(s, "pe", rec["smm"][sb])
                if i >= 1:
                    c.w(s, "vv", qchain[i - 1]["wh"])
                c.inc("ss", s.copy(out=t["qsc"][:, sb * 512:sb * 512 + n],
                                   in_=c.ps(rec["sbank"][sb])[:, 0:n]))
            if rec["kind"] == "w" and rec["gu"] == 0:
                pr = rec["pr"]
                for ch in range(4):
                    c.w(s, "tt", rec["gmm"][ch])
                    if pr > 0:
                        c.w(s, "vv", hactok[(pr - 1, ch)])
                    c.inc("ss", s.activation(out=gsil(ch), in_=c.ps(rec["gbank"][ch]),
                                             func=AF.Silu))
            if rec["kind"] == "w" and rec["gu"] == 1:
                pr, local = rec["pr"], rec["local"]
                for ch in range(4):
                    c.w(s, "vv", hactok[(pr, ch)])
                    if pr > 0:
                        c.w(s, "pe", rmmtok[(pr - 1, ch)])
                    c.inc("ss", s.activation(
                        out=gsil(ch),
                        in_=t["hblk"][:, local * TOK + ch * 512:local * TOK + ch * 512 + 512],
                        func=AF.Square))
            if rec["kind"] == "p" and rec["bi"] == 0:
                for ch in range(4):
                    c.w(s, "tt", rec["pmm"][ch])
                    c.inc("ss", s.copy(out=c.ab(rec["dc"], ch * 512, ch * 512 + 512),
                                       in_=c.ps(rec["gbank"][ch])))

    @block.vector
    def _(v):
        for rec in plan:
            i = rec["qidx"]
            slot = i % 2
            ncol = rec["ncols"]
            c.w(v, "ss", rec["sev"][-1])
            c.w(v, "pe", rec["smm"][-1])
            if TAP == "noq":
                for _r in range(2):
                    c.w(v, "vv")
                    c.inc("vv", v.tensor_copy(out=t["qta"][:, 0:ncol], in_=qb[slot][:, 0:ncol]))
                if i >= 1:
                    prev = qchain[i - 1]
                    c.w(v, "tt", prev["gmm"][-1] if "gmm" in prev else prev["pmm"][-1])
                for _r in range(2):
                    c.w(v, "vv")
                    c.inc("vv", v.tensor_copy(out=t["qw"][:, 0:ncol], in_=qb[slot][:, 0:ncol]))
            else:
                c.w(v, "vv")
                c.inc("vv", v.scalar_tensor_tensor(out=t["qta"][:, 0:ncol], in0=qb[slot][:, 0:ncol],
                                                   scalar=2.0, in1=t["qsc"][:, 0:ncol],
                                                   op0=ALU.mult, op1=ALU.is_gt))
                if i >= 1:
                    prev = qchain[i - 1]
                    c.w(v, "tt", prev["gmm"][-1] if "gmm" in prev else prev["pmm"][-1])
                c.w(v, "vv")
                c.inc("vv", v.scalar_tensor_tensor(out=t["qw"][:, 0:ncol], in0=qb[slot][:, 0:ncol],
                                                   scalar=-2.0, in1=t["qsc"][:, 0:ncol],
                                                   op0=ALU.mult, op1=ALU.is_gt))
                c.w(v, "vv")
                c.inc("vv", v.tensor_sub(out=qb[slot][:, 0:ncol], in0=t["qta"][:, 0:ncol],
                                         in1=t["qw"][:, 0:ncol]))
                c.w(v, "vv")
                c.inc("vv", v.tensor_mul(out=t["qw"][:, 0:ncol], in0=qb[slot][:, 0:ncol],
                                         in1=t["qsc"][:, 0:ncol]))
            if rec["kind"] == "w" and rec["gu"] == 1:
                pr, local = rec["pr"], rec["local"]
                for ch in range(4):
                    c.w(v, "tt", rec["gmm"][ch])
                    c.w(v, "ss", siltok[(pr, ch)])
                    c.w(v, "vv")
                    c.inc("vv", v.scalar_tensor_tensor(
                        out=t["hblk"][:, local * TOK + ch * 512:local * TOK + ch * 512 + 512],
                        in0=c.ps(rec["gbank"][ch]), scalar=1.0, in1=gsil(ch),
                        op0=ALU.mult, op1=ALU.mult))
            if rec["kind"] == "p" and rec["bi"] > 0:
                for ch in range(4):
                    c.w(v, "tt", rec["pmm"][ch])
                    c.w(v, "vv")
                    c.inc("vv", v.scalar_tensor_tensor(
                        out=c.ab(rec["dc"], ch * 512, ch * 512 + 512),
                        in0=c.ps(rec["gbank"][ch]), scalar=1.0,
                        in1=c.ab(rec["dc"], ch * 512, ch * 512 + 512),
                        op0=ALU.mult, op1=ALU.add))

    if TAP in ("accraw", "noq"):
        @block.gpsimd
        def _(g):
            yr = y.rearrange("(kt p) t -> kt p t", p=128)
            c.w(g, "vv")
            c.w(g, "ss")
            for i in range(NKT):
                c.inc("pp", g.dma_start(out=yr[i], in_=c.ab(i)), 16)
        return

    rstdF = t["scrF"][:, 0:2048]
    lntmp = t["chunkF"][:, 512:1024]
    b_ss_r = c.cnt["ss"]

    @block.scalar
    def _(s):
        c.w(s, "ss")
        for ch in range(4):
            c.w(s, "pe", rmmtok[(63, ch)])
            c.inc("ss", s.activation(out=lntmp, in_=c.ps(4 + ch), func=AF.Ln,
                                     bias=t["epsc"][:, 1:2], scale=1.0 / HF))
            c.w(s, "ss")
            c.inc("ss", s.activation(out=rstdF[:, ch * 512:ch * 512 + 512], in_=lntmp,
                                     func=AF.Exp, scale=-0.5))

    @block.vector
    def _(v):
        t1 = t["chunkF"][:, 0:512]
        for dc in range(16):
            for ch in range(4):
                c.w(v, "ss", b_ss_r + 2 * ch + 2)
                c.w(v, "vv")
                c.inc("vv", v.tensor_mul(out=t1, in0=c.ab(dc, ch * 512, ch * 512 + 512),
                                         in1=rstdF[:, ch * 512:ch * 512 + 512]))
                c.w(v, "vv")
                c.inc("vv", v.tensor_add(out=c.ab(dc, ch * 512, ch * 512 + 512),
                                         in0=t1, in1=c.at(dc, ch * 512, ch * 512 + 512)))

    if TAP == "u":
        @block.gpsimd
        def _(g):
            yr = y.rearrange("(kt p) t -> kt p t", p=128)
            c.w(g, "vv")
            for i in range(NKT):
                c.inc("pp", g.dma_start(out=yr[i], in_=c.ab(i)), 16)
        return
    _post_ln(c, block, "b", "a", final, y)


def KTLIM(sw):
    return min(31, (2048 + sw * 1024 + 1023) // 128)


def _ktlim_chunk(sw, qc, N):
    return min(KTLIM(sw), (2048 + sw * 1024 + (qc + 1) * N - 1) // 128)


def _maskslots(sw, qc, kt, N):
    out = []
    for sp in (0, 1):
        j = kt - 16 * sp - (sw * 1024 + qc * N) // 128
        if 0 <= j < max(1, N // 128):
            if N == 64:
                off = (qc * 64) % 128
                out.append(sp * 8 + (0 if off == 64 else 1))
            else:
                out.append(sp * 8 + {128: 1, 256: 2, 512: 4}[N] + j)
    return out


def _l1_attn(c, block, io, kv_in, kv_out, kv_pair, final, y):
    t = c.t
    vbufs = [t["qb0"], t["qb1"], t["qsc"], t["qta"]]

    b_tt = c.cnt["tt"]
    b_wl0, b_wl1 = c.cnt["wl0"], c.cnt["wl1"]
    b_ss = c.cnt["ss"]
    b_so = c.cnt["so"]

    @block.gpsimd
    def _(g):
        for ci in range(16):
            _stream(c, g, io["win_t"], 16 + ci, ci % 2,
                    guard=("tt", b_tt + 4 * (ci - 1)) if ci >= 2 else ("tt", b_tt))

    @block.tensor
    def _(tt):
        c.w(tt, "ss", b_ss)
        c.w(tt, "vv")
        c.w(tt, "xl")
        gidx = 0
        for et in range(16):
            c.w(tt, f"wl{et % 2}", (b_wl0 if et % 2 == 0 else b_wl1) + 16 * (et // 2 + 1))
            buf = t["wb0"] if et % 2 == 0 else t["wb1"]
            for ch in range(4):
                if gidx >= 4:
                    c.w(tt, "ss", b_ss + gidx - 3)
                for kt in range(NKT):
                    ins = tt.matmul(c.ps(gidx % 4), buf[:, kt * 128:kt * 128 + 128],
                                    c.at(kt, ch * 512, ch * 512 + 512),
                                    start=(kt == 0), stop=(kt == NKT - 1))
                c.inc("tt", ins)
                gidx += 1

    b_sd = [c.cnt["sd0"], c.cnt["sd1"]]

    @block.scalar
    def _(s):
        for et in range(16):
            ksc = t["tmpA"] if et % 2 == 0 else t["tmpB"]
            for ch in range(4):
                g = et * 4 + ch
                c.w(s, "tt", b_tt + g + 1)
                if et >= 2 and ch == 0:
                    c.w(s, f"sd{et % 2}", b_sd[et % 2] + 16 * ((et - 2) // 2 + 1))
                c.inc("ss", s.activation(out=ksc[:, ch * 512:ch * 512 + 512],
                                         in_=c.ps(g % 4), func=AF.Identity,
                                         bias=t["bias"][:, 80 + et:81 + et], scale=1.0))

    @block.sync
    def _(s):
        for et in range(16):
            ksc = t["tmpA"] if et % 2 == 0 else t["tmpB"]
            c.w(s, "ss", b_ss + et * 4 + 4)
            c.inc(f"sd{et % 2}", s.dma_start(out=kv_in[et * 128:et * 128 + 128, :], in_=ksc[:]), 16)

    b_kl = c.cnt["kl"]
    b_tt_v = c.cnt["tt"]
    b_ss_v = c.cnt["ss"]
    b_so_v = c.cnt["so"]
    vmm_last_plan = {}
    _ttp = c.cnt["tt"]
    for vg in range(4):
        _ttp += 16
        vmm_last_plan[vg] = _ttp

    @block.gpsimd
    def _(g):
        c.w(g, "sd0")
        c.w(g, "sd1")
        vbb = t["scr"][0:1, 2048:4096]
        c.inc("kl", g.dma_start(out=vbb, in_=io["vbrow"]), 16)
        for vg in range(4):
            if vg >= 1:
                c.w(g, "tt", vmm_last_plan[vg - 1])
            for kq in range(4):
                c.inc("kl", g.dma_start(
                    out=vbufs[kq][:, 0:2048].rearrange("p (kt m) -> p kt m", m=512),
                    in_=io["wv1_t"][vg, kq * 4:kq * 4 + 4].transpose([1, 0, 2])), 16)

    @block.tensor
    def _(tt):
        vbb = t["scr"][0:1, 2048:4096]
        i = 0
        for vg in range(4):
            c.w(tt, "kl", b_kl + 16 + 64 * (vg + 1))
            for tt_ in range(16):
                if i >= 2:
                    c.w(tt, "ss", b_ss_v + i - 1)
                bank = 4 + i % 2
                for kt in range(NKT):
                    tt.matmul(c.ps(bank), c.at(kt, tt_ * 128, tt_ * 128 + 128),
                              vbufs[kt // 4][:, (kt % 4) * 512:(kt % 4) * 512 + 512],
                              start=(kt == 0), stop=False)
                ins = tt.matmul(c.ps(bank), t["ones_bf"][0:1, 0:128],
                                vbb[0:1, vg * 512:vg * 512 + 512],
                                start=False, stop=True)
                c.inc("tt", ins)
                i += 1

    b_sd4 = [c.cnt[f"sd{j}"] for j in range(4)]

    @block.scalar
    def _(s):
        i = 0
        for vg in range(4):
            for tt_ in range(16):
                c.w(s, "tt", b_tt_v + i + 1)
                if i >= 4:
                    c.w(s, f"sd{i % 4}", b_sd4[i % 4] + 64 * ((i - 4) // 4 + 1))
                slot = i % 4
                c.inc("ss", s.copy(out=t["tmpA"][:, slot * 512:slot * 512 + 512],
                                   in_=c.ps(4 + i % 2)))
                i += 1

    @block.sync
    def _(s):
        i = 0
        for vg in range(4):
            for tt_ in range(16):
                c.w(s, "ss", b_ss_v + i + 1)
                slot = i % 4
                for sub in range(4):
                    dh_tile = vg * 4 + sub
                    base = 2048 + dh_tile * 128 + tt_ * 8
                    c.inc(f"sd{i % 4}", s.dma_start(
                        out=kv_in[base:base + 8, :].rearrange("r (t dh) -> (r t) dh", dh=128),
                        in_=t["tmpA"][:, slot * 512 + sub * 128:slot * 512 + sub * 128 + 128]), 16)
                i += 1

    b_so_ag = c.cnt["so"]
    b_tt_tm = c.cnt["tt"]

    @block.gpsimd
    def _(g):
        for j in range(4):
            c.w(g, f"sd{j}")
        c.inc("cc", g.collective_compute(
            "AllGather", ALU.bypass, replica_groups=[list(range(8))],
            ins=[kv_in.ap().opt()], outs=[kv_out.ap().opt()]))
        c.w(g, "cc")
        c.w(g, "sl")
        g.reg_load(t["kvb"], t["misc"][0:1, 1840:1841].bitcast(I32))
        c.inc("kl", g.dma_start(
            out=kv_pair[0:2 * TOK, :].rearrange("(r w) d -> r w d", w=TOK),
            in_=bass.AP(kv_out, t["kvb"], [[SEQ * D, 2], [D, TOK], [1, D]])), 16)
        g.reg_add(t["kvt"], t["kvb"], TOK * D)
        c.inc("kl", g.dma_start(
            out=kv_pair[2 * TOK:4 * TOK, :].rearrange("(r w) d -> r w d", w=TOK),
            in_=bass.AP(kv_out, t["kvt"], [[SEQ * D, 2], [D, TOK], [1, D]])), 16)
        c.w(g, "tt", b_tt_tm)
        tm = io["trimask"]
        for q4 in range(4):
            c.inc("kl", g.dma_start(
                out=vbufs[q4][:, 0:2048].rearrange("p (s n) -> p s n", n=512),
                in_=tm[q4 * 4:q4 * 4 + 4].transpose([1, 0, 2])), 16)
    tri_tok = c.cnt["kl"]
    b_kl_pair = tri_tok

    def tri_slice(slot, N):
        return vbufs[slot // 4][:, (slot % 4) * 512:(slot % 4) * 512 + N]

    qh = t["hblk"][:, 0:2048]
    kh = t["hblk"][:, 2048:6144]
    vh = t["hblk"][:, 6144:10240]
    rlb = t["qw"][:, 0:1024]
    rlrow = t["qw"][0:1, 1024:2048]
    btab = t["chunkF"]

    P = {"tt": c.cnt["tt"], "ss": c.cnt["ss"], "vv": c.cnt["vv"], "kl": c.cnt["kl"]}
    heads = []
    for h in range(NH):
        N = QCH[h]
        nqh = 1024 // N
        hd = {"h": h, "N": N, "nqh": nqh}
        hd["kl"] = [P["kl"] + 16 * (i + 1) for i in range(4)]
        P["kl"] += 64
        hd["qmm"] = []
        hd["qev"] = []
        for gch in range(4):
            P["tt"] += 1; hd["qmm"].append(P["tt"])
            P["ss"] += 1; hd["qev"].append(P["ss"])
        P["vv"] += 2
        hd["kch"] = P["vv"]
        hd["bt"] = []
        for qf in range(NQC[h]):
            for _ in range(6):
                P["vv"] += 1
            hd["bt"].append(P["vv"])
        hd["sweeps"] = []
        for sw in (0, 1):
            swd = {"slots": []}
            seen_l, seen_pv = set(), set()
            for kt in range(KTLIM(sw) + 1):
                for qc in range(nqh):
                    if kt > _ktlim_chunk(sw, qc, N):
                        continue
                    col = qc * N
                    sl = {"kt": kt, "qc": qc,
                          "lfirst": (col // 512) not in seen_l,
                          "pvfirst": (col // 512) not in seen_pv}
                    seen_l.add(col // 512)
                    seen_pv.add(col // 512)
                    P["tt"] += 1; sl["smm"] = P["tt"]
                    P["ss"] += 1; sl["exp"] = P["ss"]
                    ms = _maskslots(sw, qc, kt, N)
                    sl["mask"] = None
                    if ms:
                        P["vv"] += 1; sl["mask"] = P["vv"]
                        sl["mslots"] = ms
                    P["tt"] += 1; sl["lmm"] = P["tt"]
                    P["tt"] += 1; sl["pvmm"] = P["tt"]
                    swd["slots"].append(sl)
            P["vv"] += 2; swd["lev"] = P["vv"]
            P["ss"] += 2; swd["rl"] = P["ss"]
            swd["bc"] = []
            for j in range(2):
                P["tt"] += 1
                P["ss"] += 1
                swd["bc"].append((P["tt"], P["ss"]))
            P["vv"] += 2; swd["aev"] = P["vv"]
            hd["sweeps"].append(swd)
        heads.append(hd)

    b_wl0q, b_wl1q = c.cnt["wl0"], c.cnt["wl1"]

    @block.gpsimd
    def _(g):
        c.w(g, "kl", b_kl_pair)
        for h in range(NH):
            hd = heads[h]
            guard = ("tt", heads[h - 2]["qmm"][-1]) if h >= 2 else ("tt", b_tt_tm)
            _stream(c, g, io["win_t"], h, h % 2, guard=guard)
            if h >= 1:
                c.w(g, "tt", heads[h - 1]["sweeps"][1]["slots"][-1]["pvmm"])
            for r in range(2):
                c.inc("kl", g.dma_start(
                    out=kh[:, r * 2048:r * 2048 + 2048],
                    in_=kv_pair[r * TOK + h * 128:r * TOK + h * 128 + 128, :]), 16)
                base = 2 * TOK + r * TOK + h * 128
                c.inc("kl", g.dma_start(
                    out=vh[:, r * 2048:r * 2048 + 2048].rearrange("p (kt dh) -> p kt dh", dh=128),
                    in_=bass.AP(kv_pair, base * D, [[128, 128], [16384, 16], [1, 128]])), 16)

    @block.tensor
    def _(tt):
        for h in range(NH):
            hd = heads[h]
            N = hd["N"]
            c.w(tt, f"wl{h % 2}", (b_wl0q if h % 2 == 0 else b_wl1q) + 16 * (h // 2 + 1))
            buf = t["wb0"] if h % 2 == 0 else t["wb1"]
            for gch in range(4):
                if gch == 0 and h >= 1:
                    c.w(tt, "ss", heads[h - 1]["qev"][3])
                elif gch >= 1:
                    c.w(tt, "ss", hd["qev"][gch - 1])
                for kt in range(NKT):
                    ins = tt.matmul(c.ps(7), buf[:, kt * 128:kt * 128 + 128],
                                    c.at(kt, gch * 512, gch * 512 + 512),
                                    start=(kt == 0), stop=(kt == NKT - 1))
                c.inc("tt", ins)
            c.w(tt, "kl", hd["kl"][-1])
            for sw in (0, 1):
                swd = hd["sweeps"][sw]
                for i, sl in enumerate(swd["slots"]):
                    kt, qc = sl["kt"], sl["qc"]
                    col = qc * N
                    ktl = _ktlim_chunk(sw, qc, N)
                    if i >= 2:
                        c.w(tt, "ss", swd["slots"][i - 2]["exp"])
                    elif sw == 1 and len(hd["sweeps"][0]["slots"]) >= 2 - i:
                        c.w(tt, "ss", hd["sweeps"][0]["slots"][i - 2]["exp"])
                    ins = tt.matmul(c.ps(2 + i % 2, N),
                                    kh[:, kt * 128:kt * 128 + 128],
                                    qh[:, sw * 1024 + col:sw * 1024 + col + N],
                                    start=True, stop=True)
                    c.inc("tt", ins)
                    ptok = ("vv", sl["mask"]) if sl["mask"] else ("ss", sl["exp"])
                    c.w(tt, ptok[0], ptok[1])
                    pb = t["pb0"] if i % 2 == 0 else t["pb1"]
                    if sl["kt"] == 0 and sw == 1:
                        c.w(tt, "vv", hd["sweeps"][0]["lev"])
                    elif sl["kt"] == 0 and sw == 0 and h >= 1:
                        c.w(tt, "vv", heads[h - 1]["sweeps"][1]["lev"])
                    ins = tt.matmul(c.ps(4 + col // 512, 512)[0:1, col % 512:col % 512 + N],
                                    t["ones_bf"][:, 0:1], pb[:, 0:N],
                                    start=sl["lfirst"], stop=(kt == ktl),
                                    skip_group_check=True)
                    c.inc("tt", ins)
                    if sl["kt"] == 0 and sw == 1:
                        c.w(tt, "vv", hd["sweeps"][0]["aev"])
                    elif sl["kt"] == 0 and sw == 0 and h >= 1:
                        c.w(tt, "vv", heads[h - 1]["sweeps"][1]["aev"])
                    ins = tt.matmul(c.ps(col // 512, 512)[:, col % 512:col % 512 + N],
                                    vh[:, kt * 128:kt * 128 + 128], pb[:, 0:N],
                                    start=sl["pvfirst"], stop=(kt == ktl),
                                    skip_group_check=True)
                    c.inc("tt", ins)
                for j in range(2):
                    c.w(tt, "ss", swd["rl"])
                    if j == 1:
                        c.w(tt, "ss", swd["bc"][0][1])
                    elif sw == 1:
                        c.w(tt, "ss", hd["sweeps"][0]["bc"][1][1])
                    elif h >= 1:
                        c.w(tt, "ss", heads[h - 1]["sweeps"][1]["bc"][1][1])
                    ins = tt.matmul(c.ps(6, 512), t["ones_bf"][0:1, 0:128],
                                    rlrow[0:1, j * 512:j * 512 + 512],
                                    start=True, stop=True)
                    c.inc("tt", ins)

    @block.scalar
    def _(s):
        for h in range(NH):
            hd = heads[h]
            N = hd["N"]
            nqh = hd["nqh"]
            for gch in range(4):
                c.w(s, "tt", hd["qmm"][gch])
                c.inc("ss", s.activation(out=qh[:, gch * 512:gch * 512 + 512],
                                         in_=c.ps(7), func=AF.Identity,
                                         bias=t["bias"][:, 64 + h:65 + h], scale=1.0))
            for sw in (0, 1):
                swd = hd["sweeps"][sw]
                for i, sl in enumerate(swd["slots"]):
                    c.w(s, "tt", sl["smm"])
                    if i == 0 and sw == 0:
                        c.w(s, "vv", hd["bt"][-1])
                    if i >= 2:
                        c.w(s, "tt", swd["slots"][i - 2]["pvmm"])
                    elif sw == 1:
                        nsl = hd["sweeps"][0]["slots"]
                        if len(nsl) >= 2 - i:
                            c.w(s, "tt", nsl[i - 2]["pvmm"])
                    pb = t["pb0"] if i % 2 == 0 else t["pb1"]
                    qfull = sw * nqh + sl["qc"]
                    c.inc("ss", s.activation(
                        out=pb[:, 0:N], in_=c.ps(2 + i % 2, N),
                        func=AF.Exp, bias=btab[:, qfull * 32 + sl["kt"]:qfull * 32 + sl["kt"] + 1],
                        scale=SCALE1))
                c.w(s, "vv", swd["lev"])
                lrow = t["scrF"][0:1, 1024:2048]
                lnl = t["scrF"][0:1, 0:1024]
                c.inc("ss", s.activation(out=lnl, in_=lrow, func=AF.Ln,
                                         scale=float(2.0 ** -6)))
                c.w(s, "ss")
                c.inc("ss", s.activation(out=rlrow[0:1, 0:1024], in_=lnl, func=AF.Exp,
                                         scale=-1.0))
                for j in range(2):
                    c.w(s, "tt", swd["bc"][j][0])
                    c.inc("ss", s.copy(out=rlb[:, j * 512:j * 512 + 512], in_=c.ps(6, 512)))

    @block.vector
    def _(v):
        for h in range(NH):
            hd = heads[h]
            N = hd["N"]
            nqh = hd["nqh"]
            c.w(v, "ss", hd["qev"][0])
            kch = t["scrF"][:, 0:32]
            capc = t["scrF"][:, 97:98]
            c.w(v, "vv")
            c.inc("vv", v.tensor_scalar(out=kch, in0=c.m(M_KC, M_KC + 32),
                                        scalar1=c.m(M_SLB + h, M_SLB + h + 1),
                                        scalar2=None, op0=ALU.mult))
            c.w(v, "vv")
            c.inc("vv", v.tensor_scalar(out=capc, in0=c.m(M_SLB + h, M_SLB + h + 1),
                                        scalar1=(N - 1) / 2.0, scalar2=None, op0=ALU.mult))
            for qf in range(NQC[h]):
                slot = int(HQ0[h]) + qf
                sqt = t["scrF"][:, 96:97]
                vm = t["scrF"][:, 32:64]
                pen = t["scrF"][:, 64:96]
                bsl = btab[:, qf * 32:qf * 32 + 32]
                c.w(v, "vv")
                c.inc("vv", v.tensor_scalar(out=sqt, in0=c.m(M_CTB + 16 + slot, M_CTB + 17 + slot),
                                            scalar1=c.m(M_SLB + h, M_SLB + h + 1),
                                            scalar2=None, op0=ALU.mult))
                c.w(v, "vv")
                c.inc("vv", v.tensor_scalar(out=bsl, in0=kch, scalar1=sqt,
                                            scalar2=None, op0=ALU.subtract))
                c.w(v, "vv")
                c.inc("vv", v.tensor_scalar(out=bsl, in0=bsl, scalar1=capc,
                                            scalar2=None, op0=ALU.min))
                c.w(v, "vv")
                c.inc("vv", v.tensor_scalar(out=vm, in0=c.m(M_KTI, M_KTI + 32),
                                            scalar1=c.m(M_CTB + 160 + slot, M_CTB + 161 + slot),
                                            scalar2=None, op0=ALU.subtract))
                c.w(v, "vv")
                c.inc("vv", v.tensor_scalar(out=pen, in0=vm, scalar1=0.0, scalar2=-1e30,
                                            op0=ALU.max, op1=ALU.mult))
                c.w(v, "vv")
                c.inc("vv", v.tensor_tensor(out=bsl, in0=bsl, in1=pen, op=ALU.add))
            for sw in (0, 1):
                swd = hd["sweeps"][sw]
                for i, sl in enumerate(swd["slots"]):
                    if sl["mask"] is not None:
                        pb = t["pb0"] if i % 2 == 0 else t["pb1"]
                        c.w(v, "ss", sl["exp"])
                        c.w(v, "kl", tri_tok)
                        ins = None
                        for ms in sl["mslots"]:
                            c.w(v, "vv")
                            ins = v.tensor_tensor(out=pb[:, 0:N], in0=pb[:, 0:N],
                                                  in1=tri_slice(ms, N), op=ALU.mult)
                        c.inc("vv", ins)
                c.w(v, "tt", swd["slots"][-1]["lmm"])
                lrow = t["scrF"][0:1, 1024:2048]
                for j in range(2):
                    c.w(v, "vv")
                    c.inc("vv", v.tensor_copy(out=lrow[0:1, j * 512:j * 512 + 512],
                                              in_=c.ps(4 + j, 512)[0:1, 0:512]))
                for j in range(2):
                    c.w(v, "ss", swd["bc"][j][1])
                    c.w(v, "tt", swd["slots"][-1]["pvmm"])
                    c.w(v, "vv")
                    c.inc("vv", v.scalar_tensor_tensor(
                        out=c.ab(h, sw * 1024 + j * 512, sw * 1024 + j * 512 + 512),
                        in0=c.ps(j, 512), scalar=float(2.0 ** -6),
                        in1=rlb[:, j * 512:j * 512 + 512],
                        op0=ALU.mult,
                        op1=(ALU.bypass if TAP == "l1a_pv" else ALU.mult)))

    for nm in ("tt", "ss", "vv", "kl"):
        assert c.cnt[nm] == P[nm], (nm, c.cnt[nm], P[nm])

    if TAP in ("l1a_att", "l1a_pv"):
        @block.gpsimd
        def _(g):
            yr = y.rearrange("(kt p) t -> kt p t", p=128)
            c.w(g, "vv")
            for i in range(NKT):
                c.inc("pp", g.dma_start(out=yr[i], in_=c.ab(i)), 16)
        return

    b_tt3 = c.cnt["tt"]
    b_ss3 = c.cnt["ss"]
    b_wl0o, b_wl1o = c.cnt["wl0"], c.cnt["wl1"]

    @block.gpsimd
    def _(g):
        for ci in range(16):
            _stream(c, g, io["wou_t"], ci, ci % 2,
                    guard=("tt", b_tt3 + 4 * (ci - 1)) if ci >= 2 else ("tt", b_tt3))

    @block.tensor
    def _(tt):
        c.w(tt, "vv")
        c.w(tt, "ss", b_ss3)
        gidx = 0
        for ot in range(16):
            c.w(tt, f"wl{ot % 2}", (b_wl0o if ot % 2 == 0 else b_wl1o) + 16 * (ot // 2 + 1))
            buf = t["wb0"] if ot % 2 == 0 else t["wb1"]
            for ch in range(4):
                if gidx >= 4:
                    c.w(tt, "ss", b_ss3 + gidx - 3)
                for kt in range(NKT):
                    tt.matmul(c.ps(gidx % 4), buf[:, kt * 128:kt * 128 + 128],
                              c.ab(kt, ch * 512, ch * 512 + 512),
                              start=(kt == 0), stop=False)
                ins = tt.matmul(c.ps(gidx % 4), t["ident_bf"][:],
                                c.at(ot, ch * 512, ch * 512 + 512),
                                start=False, stop=True)
                c.inc("tt", ins)
                gidx += 1

    @block.scalar
    def _(s):
        for ot in range(16):
            for ch in range(4):
                gk = ot * 4 + ch
                c.w(s, "tt", b_tt3 + gk + 1)
                c.inc("ss", s.activation(out=c.at(ot, ch * 512, ch * 512 + 512),
                                         in_=c.ps(gk % 4), func=AF.Identity,
                                         bias=t["bias"][:, 112 + ot:113 + ot], scale=1.0))

    _post_ln(c, block, "a", "a", final, y)


def _l1_ffn(c, block, io, final, y):
    t = c.t
    rtb = t["qsc"][:, 0:128]
    rmb = t["qsc"][0:8, 128:256]
    atb = t["qb0"][:, 0:2048]
    bwb = t["qb1"][:, 0:2048]
    ohbc = t["qw"][:, 0:2048]
    hm = t["qta"][:, 0:2048]
    lgsb = t["scrF"][0:8, 0:2048]
    lgT = c.m(M_X1, M_X1 + 128)
    mx = c.m(M_X1 + 128, M_X1 + 144)
    ohT = c.m(M_X1 + 144, M_X1 + 272)
    dsc = c.m(M_X1 + 272, M_X1 + 280)
    oh8 = t["scr"][0:8, 4096:6144]

    b_kl = c.cnt["kl"]

    @block.gpsimd
    def _(g):
        c.w(g, "vv")
        c.w(g, "tt")
        c.inc("kl", g.dma_start(out=rtb.rearrange("p (kt m) -> p kt m", m=8),
                                in_=io["rt_t"].transpose([1, 0, 2])), 16)
        c.inc("kl", g.dma_start(out=rmb, in_=io["rmat"]), 16)
        c.inc("kl", g.dma_start(out=atb.rearrange("p (kt m) -> p kt m", m=128),
                                in_=io["at_t"].transpose([1, 0, 2])), 16)
        c.inc("kl", g.dma_start(out=bwb, in_=io["bwt_t"]), 16)

    @block.tensor
    def _(tt):
        c.w(tt, "kl", b_kl + 64)
        c.w(tt, "ss")
        c.w(tt, "vv")
        for ch in range(4):
            for kt in range(NKT):
                ins = tt.matmul(c.ps(ch % 2, 512, 8), rtb[:, kt * 8:kt * 8 + 8],
                                c.at(kt, ch * 512, ch * 512 + 512),
                                start=(kt == 0), stop=(kt == NKT - 1))
            c.inc("tt", ins)
            if ch >= 1:
                c.w(tt, "ss", c.cnt["ss"] - 1)

    b_tt1 = c.cnt["tt"]

    @block.scalar
    def _(s):
        for ch in range(4):
            c.w(s, "tt", b_tt1 - 4 + ch + 1)
            c.inc("ss", s.copy(out=lgsb[0:8, ch * 512:ch * 512 + 512], in_=c.ps(ch % 2, 512, 8)))

    @block.tensor
    def _(tt):
        c.w(tt, "ss")
        for tt_ in range(16):
            ins = tt.transpose(c.ps(2, 128)[0:128, tt_ * 8:tt_ * 8 + 8],
                               lgsb[0:8, tt_ * 128:tt_ * 128 + 128],
                               t["ident_f"][0:8, 0:8])
        c.inc("tt", ins)

    @block.scalar
    def _(s):
        c.w(s, "tt")
        c.inc("ss", s.copy(out=lgT, in_=c.ps(2, 128)))

    @block.vector
    def _(v):
        c.w(v, "ss")
        for tt_ in range(16):
            c.w(v, "vv")
            c.inc("vv", v.reduce_max(out=mx[:, tt_:tt_ + 1], in_=lgT[:, tt_ * 8:tt_ * 8 + 8],
                                     axis=mybir.AxisListType.X))
            c.w(v, "vv")
            c.inc("vv", v.tensor_scalar(out=ohT[:, tt_ * 8:tt_ * 8 + 8],
                                        in0=lgT[:, tt_ * 8:tt_ * 8 + 8],
                                        scalar1=mx[:, tt_:tt_ + 1], scalar2=None,
                                        op0=ALU.subtract))
            c.w(v, "vv")
            c.inc("vv", v.tensor_scalar(out=ohT[:, tt_ * 8:tt_ * 8 + 8],
                                        in0=ohT[:, tt_ * 8:tt_ * 8 + 8],
                                        scalar1=1e30, scalar2=1.0, op0=ALU.mult, op1=ALU.add))
            c.w(v, "vv")
            c.inc("vv", v.tensor_scalar(out=ohT[:, tt_ * 8:tt_ * 8 + 8],
                                        in0=ohT[:, tt_ * 8:tt_ * 8 + 8],
                                        scalar1=0.0, scalar2=1.0, op0=ALU.max, op1=ALU.min))

    b_tt2 = c.cnt["tt"]
    b_ss2 = c.cnt["ss"]

    @block.tensor
    def _(tt):
        c.w(tt, "vv")
        for tt_ in range(16):
            if tt_ >= 1:
                c.w(tt, "ss", b_ss2 + tt_)
            ins = tt.transpose(c.ps(3, 128)[0:8, 0:128], ohT[:, tt_ * 8:tt_ * 8 + 8],
                               t["ident_f"][:])
            c.inc("tt", ins)

    @block.scalar
    def _(s):
        for tt_ in range(16):
            c.w(s, "tt", b_tt2 + tt_ + 1)
            c.inc("ss", s.copy(out=oh8[0:8, tt_ * 128:tt_ * 128 + 128],
                               in_=c.ps(3, 128)[0:8, 0:128]))

    b_tt3 = c.cnt["tt"]
    b_ss3 = c.cnt["ss"]

    @block.tensor
    def _(tt):
        c.w(tt, "ss", b_ss3)
        for ch in range(4):
            if ch >= 2:
                c.w(tt, "ss", b_ss3 + ch - 1)
            ins = tt.matmul(c.ps(ch % 2, 512), rmb[0:8, 0:128],
                            oh8[0:8, ch * 512:ch * 512 + 512], start=True, stop=True)
            c.inc("tt", ins)

    @block.scalar
    def _(s):
        for ch in range(4):
            c.w(s, "tt", b_tt3 + ch + 1)
            c.inc("ss", s.copy(out=ohbc[:, ch * 512:ch * 512 + 512], in_=c.ps(ch % 2, 512)))

    b_tt4 = c.cnt["tt"]
    b_vv4 = c.cnt["vv"]

    @block.tensor
    def _(tt):
        c.w(tt, "ss")
        for ch in range(4):
            if ch >= 2:
                c.w(tt, "vv", b_vv4 + ch - 1)
            for kt in range(NKT):
                ins = tt.matmul(c.ps(2 + ch % 2, 512), atb[:, kt * 128:kt * 128 + 128],
                                c.at(kt, ch * 512, ch * 512 + 512),
                                start=(kt == 0), stop=(kt == NKT - 1))
            c.inc("tt", ins)

    @block.vector
    def _(v):
        for ch in range(4):
            c.w(v, "tt", b_tt4 + ch + 1)
            c.w(v, "vv")
            c.inc("vv", v.scalar_tensor_tensor(
                out=hm[:, ch * 512:ch * 512 + 512], in0=c.ps(2 + ch % 2, 512),
                scalar=2.0, in1=ohbc[:, ch * 512:ch * 512 + 512],
                op0=ALU.mult, op1=ALU.mult))

    b_tt5 = c.cnt["tt"]
    b_ss5 = c.cnt["ss"]
    b_wl0f, b_wl1f = c.cnt["wl0"], c.cnt["wl1"]

    @block.gpsimd
    def _(g):
        for ci in range(16):
            _stream(c, g, io["wb_t"], ci, ci % 2,
                    guard=("tt", b_tt5 + 4 * (ci - 1)) if ci >= 2 else None)

    @block.tensor
    def _(tt):
        c.w(tt, "vv")
        gidx = 0
        for ot in range(16):
            c.w(tt, f"wl{ot % 2}", (b_wl0f if ot % 2 == 0 else b_wl1f) + 16 * (ot // 2 + 1))
            buf = t["wb0"] if ot % 2 == 0 else t["wb1"]
            for ch in range(4):
                if gidx >= 4:
                    c.w(tt, "ss", b_ss5 + gidx - 3)
                for kt in range(NKT):
                    tt.matmul(c.ps(gidx % 4), buf[:, kt * 128:kt * 128 + 128],
                              c.at(kt, ch * 512, ch * 512 + 512),
                              start=(kt == 0), stop=False)
                tt.matmul(c.ps(gidx % 4), bwb[:, ot * 128:ot * 128 + 128],
                          hm[:, ch * 512:ch * 512 + 512], start=False, stop=False)
                ins = tt.matmul(c.ps(gidx % 4), t["ident_bf"][:],
                                c.at(ot, ch * 512, ch * 512 + 512),
                                start=False, stop=True)
                c.inc("tt", ins)
                gidx += 1

    @block.scalar
    def _(s):
        for ot in range(16):
            for ch in range(4):
                gk = ot * 4 + ch
                c.w(s, "tt", b_tt5 + gk + 1)
                c.inc("ss", s.activation(out=c.ab(ot, ch * 512, ch * 512 + 512),
                                         in_=c.ps(gk % 4), func=AF.Identity,
                                         bias=t["bias"][:, 128 + ot:129 + ot], scale=1.0))

    _post_ln(c, block, "b", "a", final, y)



def _np32(a):
    return np.ascontiguousarray(np.asarray(a), dtype=np.float32)


def _host_shared(kw):
    g = {}
    wq, wk, wv, wo = (_np32(kw[k]) for k in
                      ["L0_attn_Wq", "L0_attn_Wk", "L0_attn_Wv", "L0_attn_Wo"])
    bkq = _bt(wq.T)
    bkk = _bt(wk.T)
    bkv = _bt(wv.T)
    wkv = np.empty((32, NKT, 128, 128), np.float32)
    wkv[0::2] = bkk
    wkv[1::2] = bkv
    g["wkv_t"] = wkv
    g["wq_t"] = bkq
    g["wo_t"] = _bt(wo.T)
    wg = _np32(kw["L0_ffn_w_gate_up"])
    perm = np.empty((2 * HF,), np.int64)
    for j in range(HF // 128):
        perm[256 * j:256 * j + 128] = np.arange(128 * j, 128 * j + 128)
        perm[256 * j + 128:256 * j + 256] = np.arange(HF + 128 * j, HF + 128 * j + 128)
    g["wgu_t"] = _bt(np.ascontiguousarray(wg[perm].T))
    wp = _np32(kw["L0_ffn_w_proj"])
    g["wpj_t"] = _bt(np.ascontiguousarray(wp.T))
    win = _np32(kw["L1_attn_in_proj"])
    winT = np.ascontiguousarray(win.T)
    g["win_t"] = _bt(winT[:, 0:2 * D])
    g["wv1_t"] = _bt(winT[:, 2 * D:3 * D], 512)
    g["wou_t"] = _bt(np.ascontiguousarray(_np32(kw["L1_attn_out_w"]).T))
    g["wb_t"] = _bt(np.ascontiguousarray(_np32(kw["L1_ffn_W_base"]).T))
    rt = np.ascontiguousarray(_np32(kw["L1_ffn_router_W"]).T)
    g["rt_t"] = np.ascontiguousarray(rt.reshape(NKT, 128, 8))
    at = np.ascontiguousarray(_np32(kw["L1_ffn_A"]).reshape(128, D).T)
    g["at_t"] = np.ascontiguousarray(at.reshape(NKT, 128, 128))
    g["bwt_t"] = np.ascontiguousarray(
        _np32(kw["L1_ffn_Bw"]).transpose(0, 2, 1).reshape(128, D))
    g["rmat"] = (np.arange(8)[:, None] == (np.arange(128)[None, :] // 16)).astype(np.float32)
    g["biases_cat"] = np.concatenate([
        _np32(kw["L0_attn_bq"]).reshape(16, 128),
        _np32(kw["L0_attn_bk"]).reshape(16, 128),
        _np32(kw["L0_attn_bv"]).reshape(16, 128),
        _np32(kw["L0_attn_bo"]).reshape(16, 128),
        _np32(kw["L1_attn_in_proj_bias"]).reshape(48, 128),
        _np32(kw["L1_attn_out_b"]).reshape(16, 128),
        _np32(kw["L1_ffn_b_base"]).reshape(16, 128)], 0)
    g["slopes"] = _np32(kw["L1_attn_alibi_slopes"]).reshape(1, 16)
    g["vbrow"] = _np32(kw["L1_attn_in_proj_bias"])[2 * D:3 * D].reshape(1, D)
    cls_off = [64, 0, 0, -128, 0, -128, -256, -384]
    tms = {}
    p_ = np.arange(128)[:, None]
    n_ = np.arange(512)[None, :]
    for par in (0, 1):
        tm = np.ones((16, 128, 512), np.float32)
        for sp in (0, 1):
            if sp != par:
                continue
            for ic, off in enumerate(cls_off):
                tm[sp * 8 + ic] = (p_ <= n_ + off).astype(np.float32)
        tms[par] = tm.astype(BF)
    g["trimask"] = tms
    return g


def _host_core(core):
    seq, half = core // 2, core % 2
    q0 = half * TOK
    ct = np.zeros((1, 512), np.float32)
    if half == 1:
        ct[0, core - 1] = 1.0
    for h in range(NH):
        N = QCH[h]
        for qf in range(NQC[h]):
            ct[0, 16 + HQ0[h] + qf] = q0 + qf * N + (N - 1) / 2.0
            ct[0, 160 + HQ0[h] + qf] = (q0 + (qf + 1) * N - 1) // 128
    ci = np.zeros((1, 8), np.int32)
    ci[0, 0] = 2 * seq * SEQ * D
    return ct, ci


def _prep(kw):
    g = _host_shared(kw)
    x = _np32(kw["L0_attn_x"])
    per = []
    for core in range(8):
        seq, half = core // 2, core % 2
        ct, ci = _host_core(core)
        d = {"x_in": np.ascontiguousarray(x[seq, half * TOK:(half + 1) * TOK, :].T),
             "coretab": ct, "corei": ci}
        per.append(d)
    return g, per


_IN_BY_PART = {
    "l0a": ["wkv_t", "wq_t", "wo_t"],
    "l0f": ["wgu_t", "wpj_t"],
    "l1a": ["win_t", "wv1_t", "wou_t", "slopes", "vbrow"],
    "l1f": ["wb_t", "rt_t", "at_t", "bwt_t", "rmat"],
}


def _run(parts, g, per, x_override=None):
    nc = build_program(parts)
    maps = []
    for core in range(8):
        d = dict(per[core])
        if x_override is not None:
            d["x_in"] = np.ascontiguousarray(x_override[core])
        d["biases_cat"] = g["biases_cat"]
        for p in parts:
            for n in _IN_BY_PART[p]:
                d[n] = g[n]
            if p == "l1a":
                d["trimask"] = np.asarray(g["trimask"][core % 2])
        maps.append(d)
    res = run_bass_kernel_spmd(nc, maps, core_ids=list(range(8)))
    return [res.results[i]["y"] for i in range(8)]


def kernel(**kw):
    g, per = _prep(kw)
    if FUSED:
        ys = _run(list(ALL_PARTS), g, per)
    else:
        x_ov = None
        for p in ALL_PARTS:
            ys = _run([p], g, per, x_override=x_ov)
            x_ov = ys
    out = np.empty((4, SEQ, D), np.float32)
    for core in range(8):
        seq, half = core // 2, core % 2
        out[seq, half * TOK:(half + 1) * TOK, :] = ys[core].T
    return out


# version 02
# speedup vs baseline: 1.2643x; 1.2643x over previous
import numpy as np
import ml_dtypes
from contextlib import ExitStack
import concourse.bass as bass
import concourse.mybir as mybir
from concourse.bass_utils import run_bass_kernel_spmd

F32 = mybir.dt.float32
BF16 = mybir.dt.bfloat16
I32 = mybir.dt.int32
AF = mybir.ActivationFunctionType
ALU = mybir.AluOpType
BF = ml_dtypes.bfloat16

D = 2048
TOK = 2048
NKT = 16
SEQ = 4096
NH = 16
HD = 128
SCALE0 = float(HD ** -0.5)
SCALE1 = float(1.0 / np.sqrt(HD))
LN_EPS = 1e-5
HF = 8192
RMS_EPS = float(np.finfo(np.float32).eps)
HB = 6
QCH = [64, 128, 128, 256, 256, 256] + [512] * 10
NQC = [TOK // n for n in QCH]
HQ0 = np.cumsum([0] + NQC)
ALL_PARTS = ("l0a", "l0f", "l1a", "l1f")
FUSED = True
TAP = None
L1A_SKIP = set()

M_CT = 0
M_CTB = 512
M_SLB = 1024
M_KC = 1040
M_KTI = 1072
M_C8 = 1104
M_CP = 1232
M_CS = 1264
M_X1 = 1280
M_SL = 1800


def _bt(w, mc=128):
    K, M = w.shape
    return np.ascontiguousarray(w.reshape(K // 128, 128, M // mc, mc).transpose(2, 0, 1, 3))


class Ctx:
    def __init__(self, nc, stack):
        self.nc, self.stack = nc, stack
        self.cnt, self.sems, self.t = {}, {}, {}

    def sem(self, n):
        self.sems[n] = self.stack.enter_context(self.nc.semaphore(n))
        self.cnt[n] = 0

    def inc(self, n, ins, step=1):
        ins.then_inc(self.sems[n], step)
        self.cnt[n] += step
        return self.cnt[n]

    def w(self, eng, n, val=None):
        v = self.cnt[n] if val is None else val
        if v > 0:
            eng.wait_ge(self.sems[n], v)

    def at(self, i, c0=0, c1=TOK):
        return self.t["arena_a"][:, i * TOK + c0:i * TOK + c1]

    def ab(self, i, c0=0, c1=TOK):
        return self.t["arena_b"][:, i * TOK + c0:i * TOK + c1]

    def ps(self, i, n=512, p=128):
        return self.t[f"ps{i}"][0:p, 0:n]

    def m(self, c0, c1, p=128):
        return self.t["misc"][0:p, c0:c1]

    def srow(self, c0, c1):
        return self.t["misc"][0:1, c0:c1]


SEMS = ["xl", "bl", "wl0", "wl1", "ql0", "ql1", "tt", "ss", "vv", "gg",
        "cc", "so", "sl", "pe", "ev", "qv", "kl", "pp", "sd0", "sd1", "sd2", "sd3"]


def build_program(parts):
    parts = [p for p in ALL_PARTS if p in parts]
    nc = bass.Bass("TRN2", target_bir_lowering=False)
    io = {}

    def di(name, shape, dtype=F32):
        io[name] = nc.dram_tensor(name, shape, dtype, kind="ExternalInput").ap()

    di("x_in", [D, TOK])
    di("biases_cat", [144, 128])
    di("coretab", [1, 512])
    di("corei", [1, 8], I32)
    if "l0a" in parts:
        di("wkv_t", [32, NKT, 128, 128])
        di("wq_t", [16, NKT, 128, 128])
        di("wo_t", [16, NKT, 128, 128])
    if "l0f" in parts:
        di("wgu_t", [128, NKT, 128, 128])
        di("wpj_t", [16, HF // 128, 128, 128])
    if "l1a" in parts:
        di("win_t", [32, NKT, 128, 128])
        di("wv1_t", [4, NKT, 128, 512])
        di("wou_t", [16, NKT, 128, 128])
        di("slopes", [1, 16])
        di("vbrow", [1, D])
        di("trimask", [16, 128, 512], BF16)
    if "l1f" in parts:
        di("wb_t", [16, NKT, 128, 128])
        di("rt_t", [NKT, 128, 8])
        di("at_t", [NKT, 128, 128])
        di("bwt_t", [128, D])
        di("rmat", [8, 128])
    y = nc.dram_tensor("y", [D, TOK], F32, kind="ExternalOutput").ap()

    cs_in = nc.dram_tensor("cs_in", [128, 16], F32)
    cs_out = nc.dram_tensor("cs_out", [8 * 128, 16], F32, addr_space="Shared")
    kv_in = nc.dram_tensor("kv_in", [2 * TOK, D], BF16)
    kv_out = nc.dram_tensor("kv_out", [8 * 2 * TOK, D], BF16, addr_space="Shared")
    kv_pair = nc.dram_tensor("kv_pair", [4 * TOK, D], BF16)

    with ExitStack() as st:
        E = st.enter_context
        c = Ctx(nc, st)
        t = c.t
        t["arena_a"] = E(nc.sbuf_tensor("arena_a", [128, NKT * TOK], BF16))
        t["arena_b"] = E(nc.sbuf_tensor("arena_b", [128, NKT * TOK], BF16))
        t["hblk"] = E(nc.sbuf_tensor("hblk", [128, HB * TOK], BF16))
        t["wb0"] = E(nc.sbuf_tensor("wb0", [128, NKT * 128], BF16))
        t["wb1"] = E(nc.sbuf_tensor("wb1", [128, NKT * 128], BF16))
        t["qb0"] = E(nc.sbuf_tensor("qb0", [128, NKT * 128], BF16))
        t["qb1"] = E(nc.sbuf_tensor("qb1", [128, NKT * 128], BF16))
        t["qsc"] = E(nc.sbuf_tensor("qsc", [128, NKT * 128], BF16))
        t["qta"] = E(nc.sbuf_tensor("qta", [128, NKT * 128], BF16))
        t["qw"] = E(nc.sbuf_tensor("qw", [128, NKT * 128], BF16))
        t["scr"] = E(nc.sbuf_tensor("scr", [128, 6144], BF16))
        t["misc"] = E(nc.sbuf_tensor("misc", [128, 2048], F32))
        t["bias"] = E(nc.sbuf_tensor("bias", [128, 144], F32))
        t["ones_bf"] = E(nc.sbuf_tensor("ones_bf", [128, 128], BF16))
        t["ident_bf"] = E(nc.sbuf_tensor("ident_bf", [128, 128], BF16))
        t["ident_f"] = E(nc.sbuf_tensor("ident_f", [128, 128], F32))
        t["selb"] = E(nc.sbuf_tensor("selb", [128, 128], BF16))
        t["onesf"] = E(nc.sbuf_tensor("onesf", [1, 128], F32))
        t["epsc"] = E(nc.sbuf_tensor("epsc", [128, 4], F32))
        t["pb0"] = E(nc.sbuf_tensor("pb0", [128, 512], BF16))
        t["pb1"] = E(nc.sbuf_tensor("pb1", [128, 512], BF16))
        if "l1a" in parts:
            t["kvb"] = E(nc.gpsimd.register("kvb"))
            t["kvt"] = E(nc.gpsimd.register("kvt"))
        for i in range(8):
            t[f"ps{i}"] = E(nc.psum_tensor(f"ps{i}", [128, 512], F32))
        for s in SEMS:
            c.sem(s)
        t["tmpA"] = t["scr"][:, 0:2048]
        t["tmpB"] = t["scr"][:, 2048:4096]
        scf = t["scr"].ap() if hasattr(t["scr"], "ap") else None
        t["scrF"] = t["scr"][:, 0:6144].bitcast(F32)
        t["chunkF"] = t["scr"][:, 4096:6144].bitcast(F32)

        block = E(nc.Block())
        _prolog(c, block, io, parts)
        last = parts[-1]
        if "l0a" in parts:
            _l0_attn(c, block, io, cs_in, cs_out, final=(last == "l0a"), y=y)
        if "l0f" in parts:
            _l0_ffn(c, block, io, final=(last == "l0f"), y=y)
        if "l1a" in parts:
            _l1_attn(c, block, io, kv_in, kv_out, kv_pair, final=(last == "l1a"), y=y)
        if "l1f" in parts:
            _l1_ffn(c, block, io, final=True, y=y)
        @block.sync
        def _(s):
            c.w(s, "so")
            c.w(s, "pp")
    return nc


def _prolog(c, block, io, parts):
    t = c.t

    @block.gpsimd
    def _(g):
        xr = io["x_in"].rearrange("(kt p) t -> kt p t", p=128)
        for i in range(NKT):
            c.inc("xl", g.dma_start(out=c.at(i), in_=xr[i]), 16)
        c.inc("gg", g.memset(t["ones_bf"][:], 1.0))
        c.inc("gg", g.memset(t["onesf"][:], 1.0))
        c.inc("gg", g.memset(t["epsc"][:, 0:1], LN_EPS))
        c.inc("gg", g.memset(t["epsc"][:, 1:2], RMS_EPS))
        c.inc("gg", g.memset(t["ident_bf"][:], 1.0))
        c.w(g, "gg")
        c.inc("gg", g.affine_select(out=t["ident_bf"][:], in_=t["ident_bf"][:],
                                    pattern=[[-1, 128]], compare_op=ALU.is_equal,
                                    fill=0.0, base=0, channel_multiplier=1))
        c.inc("gg", g.memset(t["ident_f"][:], 1.0))
        c.w(g, "gg")
        c.inc("gg", g.affine_select(out=t["ident_f"][:], in_=t["ident_f"][:],
                                    pattern=[[-1, 128]], compare_op=ALU.is_equal,
                                    fill=0.0, base=0, channel_multiplier=1))
        c.inc("gg", g.memset(t["selb"][:], 0.0))
        c.w(g, "gg")
        c.inc("gg", g.memset(t["selb"][0:64, 0:64], 1.0 / 64.0))
        c.inc("gg", g.memset(t["selb"][64:128, 64:128], 1.0 / 64.0))
        c.inc("gg", g.iota(c.m(M_KC, M_KC + 32), [[128, 32]], base=0, channel_multiplier=1,
                           allow_small_or_imprecise_dtypes=True))
        c.inc("gg", g.iota(c.m(M_KTI, M_KTI + 32), [[1, 32]], base=0, channel_multiplier=0,
                           allow_small_or_imprecise_dtypes=True))


    @block.sync
    def _(s):
        scrf = t["scrF"]
        c.inc("bl", s.dma_start(out=scrf[:, 0:128], in_=io["biases_cat"][0:128, :]), 16)
        c.inc("bl", s.dma_start(out=scrf[0:16, 128:256], in_=io["biases_cat"][128:144, :]), 16)
        c.inc("sl", s.dma_start(out=c.srow(M_CT, M_CT + 512), in_=io["coretab"]), 16)
        if "l1a" in parts:
            c.inc("sl", s.dma_start(out=c.srow(M_SL, M_SL + 16), in_=io["slopes"]), 16)
            c.inc("sl", s.dma_start(out=t["misc"][0:1, 1840:1848].bitcast(I32), in_=io["corei"]), 16)

    @block.tensor
    def _(tt):
        c.w(tt, "bl"); c.w(tt, "gg"); c.w(tt, "sl")
        tt.transpose(c.ps(0, 128), t["scrF"][:, 0:128], t["ident_f"][:])
        tt.transpose(c.ps(1, 16)[0:128, 0:16], t["scrF"][0:16, 128:256], t["ident_f"][0:16, 0:16])
        ins = tt.matmul(c.ps(2, 512), t["onesf"][0:1, :], c.srow(M_CT, M_CT + 512),
                        start=True, stop=True)
        if "l1a" in parts:
            ins = tt.matmul(c.ps(3, 16), t["onesf"][0:1, :], c.srow(M_SL, M_SL + 16),
                            start=True, stop=True)
        c.inc("tt", ins)

    @block.scalar
    def _(s):
        c.w(s, "tt")
        s.copy(out=t["bias"][:, 0:128], in_=c.ps(0, 128))
        s.copy(out=t["bias"][:, 128:144], in_=c.ps(1, 16)[0:128, 0:16])
        ins = s.copy(out=c.m(M_CTB, M_CTB + 512), in_=c.ps(2, 512))
        if "l1a" in parts:
            ins = s.copy(out=c.m(M_SLB, M_SLB + 16), in_=c.ps(3, 16))
        c.inc("ss", ins)


def _stream(c, g, dram, ci, buf_slot, nkt=NKT, guard=None):
    if guard is not None:
        c.w(g, guard[0], guard[1])
    buf = c.t["wb0"] if buf_slot == 0 else c.t["wb1"]
    c.inc(f"wl{buf_slot}",
          g.dma_start(out=buf[:, 0:nkt * 128].rearrange("p (kt m) -> p kt m", m=128),
                      in_=dram[ci].transpose([1, 0, 2])), 16)
    return buf


def _post_ln(c, block, u, out, final, y):
    t = c.t
    ut = c.at if u == "a" else c.ab
    ot = c.at if out == "a" else c.ab
    usq = t["tmpA"][:, 0:512]
    scf = t["scrF"]
    b_ss, b_vv, b_pe = c.cnt["ss"], c.cnt["vv"], c.cnt["pe"]

    @block.scalar
    def _(s):
        c.w(s, "vv")
        c.w(s, "tt")
        c.w(s, "ss")
        for i in range(NKT):
            for ch in range(4):
                if i * 4 + ch > 0:
                    c.w(s, "pe", b_pe + i * 4 + ch)
                c.inc("ss", s.activation(out=usq, in_=ut(i, ch * 512, ch * 512 + 512),
                                         func=AF.Square))

    @block.tensor
    def _(tt):
        for i in range(NKT):
            for ch in range(4):
                c.w(tt, "ss", b_ss + i * 4 + ch + 1)
                tt.matmul(c.ps(ch), t["ones_bf"][:], ut(i, ch * 512, ch * 512 + 512),
                          start=(i == 0), stop=(i == NKT - 1))
                ins = tt.matmul(c.ps(4 + ch), t["ones_bf"][:], usq,
                                start=(i == 0), stop=(i == NKT - 1))
                if i == NKT - 1 and ch == 3:
                    c.inc("tt", ins)
                else:
                    c.inc("pe", ins)

    b_tt2 = c.cnt["tt"]
    for ch in range(4):
        mloc = scf[:, 1024:1536]
        rloc = t["chunkF"][:, 512:1024]
        mrloc = t["chunkF"][:, 0:512]

        @block.vector
        def _(v):
            if ch == 0:
                c.w(v, "tt", b_tt2)
            else:
                c.w(v, "vv")
            c.inc("vv", v.tensor_scalar(out=mloc, in0=c.ps(ch), scalar1=1.0 / D, scalar2=None,
                            op0=ALU.mult))
            nm2 = t["chunkF"][:, 0:512]
            c.w(v, "vv")
            c.inc("vv", v.scalar_tensor_tensor(out=nm2, in0=mloc, scalar=-1.0, in1=mloc,
                                   op0=ALU.mult, op1=ALU.mult))
            var = t["chunkF"][:, 512:1024]
            c.w(v, "vv")
            c.inc("vv", v.scalar_tensor_tensor(out=var, in0=c.ps(4 + ch), scalar=1.0 / D, in1=nm2,
                                   op0=ALU.mult, op1=ALU.add))

        b_ss2 = c.cnt["ss"]

        @block.scalar
        def _(s):
            c.w(s, "vv")
            var = t["chunkF"][:, 512:1024]
            lnv = t["chunkF"][:, 0:512]
            c.inc("ss", s.activation(out=lnv, in_=var, func=AF.Ln, bias=t["epsc"][:, 0:1], scale=1.0))
            c.w(s, "ss")
            c.inc("ss", s.activation(out=rloc, in_=lnv, func=AF.Exp, scale=-0.5))

        @block.vector
        def _(v):
            c.w(v, "ss", b_ss2 + 2)
            c.inc("vv", v.tensor_mul(out=mrloc, in0=mloc, in1=rloc))
            tmp = scf[:, 1536:2048]
            for i in range(NKT):
                c.w(v, "vv")
                c.inc("vv", v.tensor_mul(out=tmp, in0=ut(i, ch * 512, ch * 512 + 512), in1=rloc))
                c.w(v, "vv")
                c.inc("vv", v.tensor_sub(out=ot(i, ch * 512, ch * 512 + 512), in0=tmp, in1=mrloc))
    if final:
        b_vv3 = c.cnt["vv"]

        @block.gpsimd
        def _(g):
            yr = y.rearrange("(kt p) t -> kt p t", p=128)
            for i in range(NKT):
                c.w(g, "vv", b_vv3 - 2 * (NKT - 1 - i))
                c.inc("pp", g.dma_start(out=yr[i], in_=ot(i)), 16)


def _phi_chain(c, emitters, et, ch, psbank, bias_col, dest, tok_tt):
    raise NotImplementedError


def _l0_attn(c, block, io, cs_in, cs_out, final, y):
    t = c.t
    temp1 = t["chunkF"][:, 0:512]
    temp2 = t["chunkF"][:, 512:1024]

    b_tt = c.cnt["tt"]
    b_wl0, b_wl1 = c.cnt["wl0"], c.cnt["wl1"]

    @block.gpsimd
    def _(g):
        c.w(g, "xl")
        for ci in range(32):
            _stream(c, g, io["wkv_t"], ci, ci % 2,
                    guard=("tt", b_tt + 4 * (ci - 1)) if ci >= 2 else None)

    b_ss_a = c.cnt["ss"]

    @block.tensor
    def _(tt):
        c.w(tt, "ss", b_ss_a)
        gidx = 0
        for et in range(16):
            for kv in range(2):
                ci = 2 * et + kv
                c.w(tt, f"wl{ci % 2}", (b_wl0 if ci % 2 == 0 else b_wl1) + 16 * (ci // 2 + 1))
                buf = t["wb0"] if ci % 2 == 0 else t["wb1"]
                for ch in range(4):
                    bank = gidx % 4
                    if gidx >= 4:
                        gp = gidx - 4
                        ep, rr = gp // 8, gp % 8
                        etok = (b_ss_a + ep * 12 + 2 * (rr % 4) + 1) if rr < 4 else                                (b_ss_a + ep * 12 + 8 + (rr % 4) + 1)
                        c.w(tt, "ss", etok)
                    for kt in range(NKT):
                        ins = tt.matmul(c.ps(bank), buf[:, kt * 128:kt * 128 + 128],
                                        c.at(kt, ch * 512, ch * 512 + 512),
                                        start=(kt == 0), stop=(kt == NKT - 1))
                    c.inc("tt", ins)
                    gidx += 1

    b_ss, b_vv = b_ss_a, c.cnt["vv"]
    A = lambda et, ch: b_ss + et * 12 + 2 * ch + 1
    C = lambda et, ch: b_ss + et * 12 + 2 * ch + 2
    V = lambda et, ch: b_ss + et * 12 + 8 + ch + 1
    Bt = lambda et, ch: b_vv + et * 10 + 2 * ch + 1
    Dt = lambda et, ch: b_vv + et * 10 + 2 * ch + 2
    KVt = lambda et: b_vv + et * 10 + 9
    CSt = lambda et: b_vv + et * 10 + 10

    @block.scalar
    def _(s):
        for et in range(16):
            for ch in range(4):
                gk = et * 8 + ch
                c.w(s, "tt", b_tt + gk + 1)
                if et > 0 and ch == 0:
                    c.w(s, "vv", CSt(et - 1))
                elif ch > 0:
                    c.w(s, "vv", Dt(et, ch - 1))
                ins = s.activation(out=temp1, in_=c.ps(gk % 4), func=AF.Identity,
                                   bias=t["bias"][:, 16 + et:17 + et], scale=1.0)
                c.inc("ss", ins)
                c.w(s, "vv", Bt(et, ch))
                c.inc("ss", s.activation(out=temp2, in_=t["tmpA"][:, ch * 512:ch * 512 + 512],
                                         func=AF.Exp))
            for ch in range(4):
                gv = et * 8 + 4 + ch
                c.w(s, "tt", b_tt + gv + 1)
                ins = s.activation(out=t["tmpB"][:, ch * 512:ch * 512 + 512],
                                   in_=c.ps(gv % 4), func=AF.Identity,
                                   bias=t["bias"][:, 32 + et:33 + et], scale=1.0)
                c.inc("ss", ins)

    @block.vector
    def _(v):
        for et in range(16):
            for ch in range(4):
                c.w(v, "ss", A(et, ch))
                c.inc("vv", v.tensor_scalar(out=t["tmpA"][:, ch * 512:ch * 512 + 512], in0=temp1,
                                scalar1=0.0, scalar2=None, op0=ALU.min))
                c.w(v, "ss", C(et, ch))
                c.inc("vv", v.scalar_tensor_tensor(out=t["tmpA"][:, ch * 512:ch * 512 + 512],
                                       in0=temp1, scalar=0.0, in1=temp2,
                                       op0=ALU.max, op1=ALU.add))
            c.w(v, "ss", V(et, 3))
            c.w(v, "vv", Dt(et, 3))
            c.inc("vv", v.tensor_mul(out=c.ab(et), in0=t["tmpA"], in1=t["tmpB"]))
            c.w(v, "vv", KVt(et))
            c.inc("vv", v.reduce_sum(out=c.m(M_CS + et, M_CS + et + 1), in_=c.ab(et),
                         axis=mybir.AxisListType.X))

    b_so = c.cnt["so"]

    @block.sync
    def _(s):
        c.w(s, "vv")
        c.inc("so", s.dma_start(out=cs_in[:, :], in_=c.m(M_CS, M_CS + 16)), 16)

    b_xl = c.cnt["xl"]

    @block.gpsimd
    def _(g):
        c.w(g, "so", b_so + 16)
        c.inc("cc", g.collective_compute(
            "AllGather", ALU.bypass, replica_groups=[list(range(8))],
            ins=[cs_in.ap().opt()], outs=[cs_out.ap().opt()]))
        c.w(g, "cc")
        c.inc("xl", g.dma_start(
            out=c.m(M_C8, M_C8 + 128).rearrange("p (r t) -> p r t", t=16),
            in_=cs_out.ap().rearrange("(r p) t -> r p t", p=128).transpose([1, 0, 2])), 16)

    b_vv2 = c.cnt["vv"]

    @block.vector
    def _(v):
        c.w(v, "xl", b_xl + 16)
        cp = [c.m(M_CP, M_CP + 16), c.m(M_CP + 16, M_CP + 32)]
        c.inc("vv", v.tensor_scalar(out=cp[0], in0=c.m(M_C8, M_C8 + 16),
                        scalar1=c.m(M_CTB, M_CTB + 1), scalar2=None, op0=ALU.mult))
        for r in range(1, 8):
            c.w(v, "vv")
            c.inc("vv", v.scalar_tensor_tensor(out=cp[r % 2], in0=c.m(M_C8 + 16 * r, M_C8 + 16 * r + 16),
                                   scalar=c.m(M_CTB + r, M_CTB + r + 1),
                                   in1=cp[(r + 1) % 2], op0=ALU.mult, op1=ALU.add))

    b_tt2 = c.cnt["tt"]
    b_ss2 = c.cnt["ss"]
    b_wl0, b_wl1 = c.cnt["wl0"], c.cnt["wl1"]

    @block.gpsimd
    def _(g):
        for ci in range(16):
            _stream(c, g, io["wq_t"], ci, ci % 2,
                    guard=("tt", b_tt2 + 4 * (ci - 1)) if ci >= 2 else ("tt", b_tt2))

    @block.tensor
    def _(tt):
        c.w(tt, "ss", b_ss2)
        gidx = 0
        for et in range(16):
            ci = et
            c.w(tt, f"wl{ci % 2}", (b_wl0 if ci % 2 == 0 else b_wl1) + 16 * (ci // 2 + 1))
            buf = t["wb0"] if ci % 2 == 0 else t["wb1"]
            for ch in range(4):
                bank = gidx % 4
                if gidx >= 4:
                    gp = gidx - 4
                    c.w(tt, "ss", b_ss2 + 2 * (gp % 4) + (gp // 4) * 8 + 1)
                for kt in range(NKT):
                    ins = tt.matmul(c.ps(bank), buf[:, kt * 128:kt * 128 + 128],
                                    c.at(kt, ch * 512, ch * 512 + 512),
                                    start=(kt == 0), stop=(kt == NKT - 1))
                c.inc("tt", ins)
                gidx += 1

    assert c.cnt["ss"] == b_ss2, (c.cnt["ss"], b_ss2)
    b_vv3 = c.cnt["vv"]
    A2 = lambda et, ch: b_ss2 + et * 8 + 2 * ch + 1
    C2 = lambda et, ch: b_ss2 + et * 8 + 2 * ch + 2
    B2 = lambda et, ch: b_vv3 + et * 10 + 2 * ch + 1
    D2 = lambda et, ch: b_vv3 + et * 10 + 2 * ch + 2
    SCt = lambda et: b_vv3 + et * 10 + 9
    CXt = lambda et: b_vv3 + et * 10 + 10

    @block.scalar
    def _(s):
        for et in range(16):
            for ch in range(4):
                gk = et * 4 + ch
                c.w(s, "tt", b_tt2 + gk + 1)
                if et == 0 and ch == 0:
                    c.w(s, "vv", b_vv3)
                elif ch == 0:
                    c.w(s, "vv", CXt(et - 1))
                else:
                    c.w(s, "vv", D2(et, ch - 1))
                ins = s.activation(out=temp1, in_=c.ps(gk % 4), func=AF.Identity,
                                   bias=t["bias"][:, 0 + et:1 + et], scale=1.0)
                c.inc("ss", ins)
                c.w(s, "vv", B2(et, ch))
                c.inc("ss", s.activation(out=temp2, in_=t["tmpA"][:, ch * 512:ch * 512 + 512],
                                         func=AF.Exp))

    @block.vector
    def _(v):
        c.w(v, "vv", b_vv2 + 8)
        for et in range(16):
            for ch in range(4):
                c.w(v, "ss", A2(et, ch))
                c.inc("vv", v.tensor_scalar(out=t["tmpA"][:, ch * 512:ch * 512 + 512], in0=temp1,
                                scalar1=0.0, scalar2=None, op0=ALU.min))
                c.w(v, "ss", C2(et, ch))
                c.inc("vv", v.scalar_tensor_tensor(out=t["tmpA"][:, ch * 512:ch * 512 + 512],
                                       in0=temp1, scalar=0.0, in1=temp2,
                                       op0=ALU.max, op1=ALU.add))
            c.w(v, "vv")
            c.inc("vv", v.tensor_tensor_scan(out=t["tmpB"], data0=c.ab(et), data1=c.ab(et),
                                 initial=c.m(M_CP + 16 + et, M_CP + 17 + et),
                                 op0=ALU.add, op1=ALU.bypass))
            c.w(v, "vv")
            c.inc("vv", v.scalar_tensor_tensor(out=c.ab(et), in0=t["tmpB"], scalar=SCALE0,
                                   in1=t["tmpA"], op0=ALU.mult, op1=ALU.mult))

    b_tt3 = c.cnt["tt"]
    b_ss3 = c.cnt["ss"]
    b_wl0, b_wl1 = c.cnt["wl0"], c.cnt["wl1"]

    @block.gpsimd
    def _(g):
        for ci in range(16):
            _stream(c, g, io["wo_t"], ci, ci % 2,
                    guard=("tt", b_tt3 + 4 * (ci - 1)) if ci >= 2 else ("tt", b_tt3))

    @block.tensor
    def _(tt):
        c.w(tt, "vv")
        c.w(tt, "ss", b_ss3)
        gidx = 0
        for ot in range(16):
            ci = ot
            c.w(tt, f"wl{ci % 2}", (b_wl0 if ci % 2 == 0 else b_wl1) + 16 * (ci // 2 + 1))
            buf = t["wb0"] if ci % 2 == 0 else t["wb1"]
            for ch in range(4):
                bank = gidx % 4
                if gidx >= 4:
                    c.w(tt, "ss", b_ss3 + gidx - 3)
                for kt in range(NKT):
                    tt.matmul(c.ps(bank), buf[:, kt * 128:kt * 128 + 128],
                              c.ab(kt, ch * 512, ch * 512 + 512),
                              start=(kt == 0), stop=False)
                ins = tt.matmul(c.ps(bank), t["ident_bf"][:],
                                c.at(ot, ch * 512, ch * 512 + 512),
                                start=False, stop=True)
                c.inc("tt", ins)
                gidx += 1

    @block.scalar
    def _(s):
        for ot in range(16):
            for ch in range(4):
                gk = ot * 4 + ch
                c.w(s, "tt", b_tt3 + gk + 1)
                ins = s.activation(out=c.at(ot, ch * 512, ch * 512 + 512),
                                   in_=c.ps(gk % 4), func=AF.Identity,
                                   bias=t["bias"][:, 48 + ot:49 + ot], scale=1.0)
                c.inc("ss", ins)

    _post_ln(c, block, "a", "a", final, y)


def _l0_ffn(c, block, io, final, y):
    t = c.t
    gsil = lambda ch: t["scr"][:, 4096 + ch * 512:4096 + ch * 512 + 512]
    qb = [t["qb0"], t["qb1"]]
    blocks = [list(range(i, min(i + HB, 64))) for i in range(0, 64, HB)]

    SC = []
    for bi, blk in enumerate(blocks):
        for local, pr in enumerate(blk):
            for gu in (0, 1):
                SC.append(("wchunk", bi, local, pr, gu))
        for dc in range(16):
            SC.append(("pchunk", bi, dc))

    cnt = {k: c.cnt[k] for k in ("ss", "vv", "tt", "pe", "ql0", "ql1")}
    tok = {}
    qn = 0
    scnt = 0
    gcnt = 0
    qchain = []
    projev = []
    siltok = {}
    hactok = {}
    rmmtok = {}
    acctok = {}

    def plan_quant(ncols):
        rec = {"qn": qn}
        cnt["ss"] += 1; rec["abs"] = cnt["ss"]
        nsb = (ncols + 511) // 512
        rec["smm"] = []
        rec["sev"] = []
        for sb in range(nsb):
            cnt["pe"] += 1; rec["smm"].append(cnt["pe"])
            cnt["ss"] += 1; rec["sev"].append(cnt["ss"])
        for op in ("a", "b", "q", "wh"):
            cnt["vv"] += 1; rec[op] = cnt["vv"]
        rec["nsb"] = nsb
        rec["ncols"] = ncols
        return rec

    plan = []
    for e in SC:
        if e[0] == "wchunk":
            bi, local, pr, gu = e[1:]
            rec = {"kind": "w", "bi": bi, "local": local, "pr": pr, "gu": gu,
                   "ci": 2 * pr + gu}
            cnt[f"ql{qn % 2}"] += 16
            rec["dmatok"] = cnt[f"ql{qn % 2}"]
            rec.update(plan_quant(2048))
            qchain.append(rec)
            rec["qidx"] = len(qchain) - 1
            rec["gmm"] = []
            rec["gev"] = []
            nonlocal_scnt = None
            for ch in range(4):
                cnt["tt"] += 1; rec["gmm"].append(cnt["tt"])
                if gu == 0:
                    cnt["ss"] += 1; rec["gev"].append(("ss", cnt["ss"]))
                    siltok[(pr, ch)] = cnt["ss"]
                else:
                    cnt["vv"] += 1; rec["gev"].append(("vv", cnt["vv"]))
                    hactok[(pr, ch)] = cnt["vv"]
                projev.append(rec["gev"][-1])
            if gu == 1:
                rec["rms"] = []
                for ch in range(4):
                    cnt["ss"] += 1
                    cnt["pe"] += 1
                    rec["rms"].append((cnt["ss"], cnt["pe"]))
                    rmmtok[(pr, ch)] = cnt["pe"]
            plan.append(rec)
        else:
            bi, dc = e[1], e[2]
            blk = blocks[bi]
            L = len(blk)
            rec = {"kind": "p", "bi": bi, "dc": dc, "L": L}
            cnt[f"ql{qn % 2}"] += 16
            rec["dmatok"] = cnt[f"ql{qn % 2}"]
            rec.update(plan_quant(L * 128))
            qchain.append(rec)
            rec["qidx"] = len(qchain) - 1
            rec["pmm"] = []
            rec["pev"] = []
            for ch in range(4):
                cnt["tt"] += 1; rec["pmm"].append(cnt["tt"])
                if bi == 0:
                    cnt["ss"] += 1; rec["pev"].append(("ss", cnt["ss"]))
                else:
                    cnt["vv"] += 1; rec["pev"].append(("vv", cnt["vv"]))
                acctok[(bi, dc, ch)] = rec["pev"][-1][1]
                projev.append(rec["pev"][-1])
            plan.append(rec)
        qn += 1

    si = 0
    gi = 0
    for rec in plan:
        rec["sbank"] = []
        for sb in range(rec["nsb"]):
            rec["sbank"].append(si % 2)
            rec.setdefault("sguard", []).append(si - 2)
            si += 1
        rec["gbank"] = []
        for ch in range(4):
            rec["gbank"].append(2 + gi % 2)
            rec.setdefault("gguard", []).append(gi - 2)
            gi += 1
    sevs = [tk for rec in plan for tk in rec["sev"]]

    b0 = {k: c.cnt[k] for k in ("ss", "vv", "tt", "pe")}

    @block.gpsimd
    def _(g):
        for rec in plan:
            i = rec["qidx"]
            if i >= 2:
                c.w(g, "vv", qchain[i - 2]["wh"])
            slot = i % 2
            if rec["kind"] == "w":
                src_ = io["wgu_t"][rec["ci"]].transpose([1, 0, 2])
                ncol = 2048
            else:
                blk = blocks[rec["bi"]]
                src_ = io["wpj_t"][rec["dc"], blk[0]:blk[0] + rec["L"]].transpose([1, 0, 2])
                ncol = rec["L"] * 128
            c.inc(f"ql{i % 2}",
                  g.dma_start(out=qb[slot][:, 0:ncol].rearrange("p (kt m) -> p kt m", m=128),
                              in_=src_), 16)

    @block.tensor
    def _(tt):
        c.w(tt, "ss", b0["ss"])
        for rec in plan:
            i = rec["qidx"]
            for sb in range(rec["nsb"]):
                n = min(512, rec["ncols"] - sb * 512)
                c.w(tt, "ss", rec["abs"])
                gidx = rec["sguard"][sb]
                if gidx >= 0:
                    c.w(tt, "ss", sevs[gidx])
                bank = rec["sbank"][sb]
                nkts = n // 128
                for k2 in range(nkts):
                    kt = sb * 4 + k2
                    ins = tt.matmul(c.ps(bank)[:, k2 * 128:k2 * 128 + 128], t["selb"][:],
                                    t["qta"][:, kt * 128:kt * 128 + 128],
                                    start=True, stop=True)
                c.inc("pe", ins)
            c.w(tt, "vv", rec["wh"])
            for ch in range(4):
                gidx = rec["gguard"][ch]
                if gidx >= 0:
                    c.w(tt, projev[gidx][0], projev[gidx][1])
                bank = rec["gbank"][ch]
                if rec["kind"] == "w":
                    for kt in range(NKT):
                        ins = tt.matmul(c.ps(bank), t["qw"][:, kt * 128:kt * 128 + 128],
                                        c.at(kt, ch * 512, ch * 512 + 512),
                                        start=(kt == 0), stop=(kt == NKT - 1))
                else:
                    for j in range(rec["L"]):
                        ins = tt.matmul(c.ps(bank), t["qw"][:, j * 128:j * 128 + 128],
                                        t["hblk"][:, j * TOK + ch * 512:j * TOK + ch * 512 + 512],
                                        start=(j == 0), stop=(j == rec["L"] - 1))
                c.inc("tt", ins)
            if rec["kind"] == "w" and rec["gu"] == 1:
                for ch in range(4):
                    c.w(tt, "ss", rec["rms"][ch][0])
                    ins = tt.matmul(c.ps(4 + ch), t["ones_bf"][:], gsil(ch),
                                    start=(rec["pr"] == 0), stop=(rec["pr"] == 63))
                    c.inc("pe", ins)

    @block.scalar
    def _(s):
        for rec in plan:
            i = rec["qidx"]
            slot = i % 2
            ncol = rec["ncols"]
            c.w(s, f"ql{i % 2}", rec["dmatok"])
            if i >= 1:
                c.w(s, "vv", qchain[i - 1]["q"])
            c.inc("ss", s.activation(out=t["qta"][:, 0:ncol], in_=qb[slot][:, 0:ncol],
                                     func=AF.Abs))
            for sb in range(rec["nsb"]):
                n = min(512, ncol - sb * 512)
                c.w(s, "pe", rec["smm"][sb])
                if i >= 1:
                    c.w(s, "vv", qchain[i - 1]["wh"])
                c.inc("ss", s.copy(out=t["qsc"][:, sb * 512:sb * 512 + n],
                                   in_=c.ps(rec["sbank"][sb])[:, 0:n]))
            if rec["kind"] == "w" and rec["gu"] == 0:
                pr = rec["pr"]
                for ch in range(4):
                    c.w(s, "tt", rec["gmm"][ch])
                    if pr > 0:
                        c.w(s, "vv", hactok[(pr - 1, ch)])
                    c.inc("ss", s.activation(out=gsil(ch), in_=c.ps(rec["gbank"][ch]),
                                             func=AF.Silu))
            if rec["kind"] == "w" and rec["gu"] == 1:
                pr, local = rec["pr"], rec["local"]
                for ch in range(4):
                    c.w(s, "vv", hactok[(pr, ch)])
                    if pr > 0:
                        c.w(s, "pe", rmmtok[(pr - 1, ch)])
                    c.inc("ss", s.activation(
                        out=gsil(ch),
                        in_=t["hblk"][:, local * TOK + ch * 512:local * TOK + ch * 512 + 512],
                        func=AF.Square))
            if rec["kind"] == "p" and rec["bi"] == 0:
                for ch in range(4):
                    c.w(s, "tt", rec["pmm"][ch])
                    c.inc("ss", s.copy(out=c.ab(rec["dc"], ch * 512, ch * 512 + 512),
                                       in_=c.ps(rec["gbank"][ch])))

    @block.vector
    def _(v):
        for rec in plan:
            i = rec["qidx"]
            slot = i % 2
            ncol = rec["ncols"]
            c.w(v, "ss", rec["sev"][-1])
            c.w(v, "pe", rec["smm"][-1])
            if TAP == "noq":
                for _r in range(2):
                    c.w(v, "vv")
                    c.inc("vv", v.tensor_copy(out=t["qta"][:, 0:ncol], in_=qb[slot][:, 0:ncol]))
                if i >= 1:
                    prev = qchain[i - 1]
                    c.w(v, "tt", prev["gmm"][-1] if "gmm" in prev else prev["pmm"][-1])
                for _r in range(2):
                    c.w(v, "vv")
                    c.inc("vv", v.tensor_copy(out=t["qw"][:, 0:ncol], in_=qb[slot][:, 0:ncol]))
            else:
                c.w(v, "vv")
                c.inc("vv", v.scalar_tensor_tensor(out=t["qta"][:, 0:ncol], in0=qb[slot][:, 0:ncol],
                                                   scalar=2.0, in1=t["qsc"][:, 0:ncol],
                                                   op0=ALU.mult, op1=ALU.is_gt))
                if i >= 1:
                    prev = qchain[i - 1]
                    c.w(v, "tt", prev["gmm"][-1] if "gmm" in prev else prev["pmm"][-1])
                c.w(v, "vv")
                c.inc("vv", v.scalar_tensor_tensor(out=t["qw"][:, 0:ncol], in0=qb[slot][:, 0:ncol],
                                                   scalar=-2.0, in1=t["qsc"][:, 0:ncol],
                                                   op0=ALU.mult, op1=ALU.is_gt))
                c.w(v, "vv")
                c.inc("vv", v.tensor_sub(out=qb[slot][:, 0:ncol], in0=t["qta"][:, 0:ncol],
                                         in1=t["qw"][:, 0:ncol]))
                c.w(v, "vv")
                c.inc("vv", v.tensor_mul(out=t["qw"][:, 0:ncol], in0=qb[slot][:, 0:ncol],
                                         in1=t["qsc"][:, 0:ncol]))
            if rec["kind"] == "w" and rec["gu"] == 1:
                pr, local = rec["pr"], rec["local"]
                for ch in range(4):
                    c.w(v, "tt", rec["gmm"][ch])
                    c.w(v, "ss", siltok[(pr, ch)])
                    c.w(v, "vv")
                    c.inc("vv", v.scalar_tensor_tensor(
                        out=t["hblk"][:, local * TOK + ch * 512:local * TOK + ch * 512 + 512],
                        in0=c.ps(rec["gbank"][ch]), scalar=1.0, in1=gsil(ch),
                        op0=ALU.mult, op1=ALU.mult))
            if rec["kind"] == "p" and rec["bi"] > 0:
                for ch in range(4):
                    c.w(v, "tt", rec["pmm"][ch])
                    c.w(v, "vv")
                    c.inc("vv", v.scalar_tensor_tensor(
                        out=c.ab(rec["dc"], ch * 512, ch * 512 + 512),
                        in0=c.ps(rec["gbank"][ch]), scalar=1.0,
                        in1=c.ab(rec["dc"], ch * 512, ch * 512 + 512),
                        op0=ALU.mult, op1=ALU.add))

    if TAP in ("accraw", "noq"):
        @block.gpsimd
        def _(g):
            yr = y.rearrange("(kt p) t -> kt p t", p=128)
            c.w(g, "vv")
            c.w(g, "ss")
            for i in range(NKT):
                c.inc("pp", g.dma_start(out=yr[i], in_=c.ab(i)), 16)
        return

    rstdF = t["scrF"][:, 0:2048]
    lntmp = t["chunkF"][:, 512:1024]
    b_ss_r = c.cnt["ss"]

    @block.scalar
    def _(s):
        c.w(s, "ss")
        for ch in range(4):
            c.w(s, "pe", rmmtok[(63, ch)])
            c.inc("ss", s.activation(out=lntmp, in_=c.ps(4 + ch), func=AF.Ln,
                                     bias=t["epsc"][:, 1:2], scale=1.0 / HF))
            c.w(s, "ss")
            c.inc("ss", s.activation(out=rstdF[:, ch * 512:ch * 512 + 512], in_=lntmp,
                                     func=AF.Exp, scale=-0.5))

    @block.vector
    def _(v):
        t1 = t["chunkF"][:, 0:512]
        for dc in range(16):
            for ch in range(4):
                c.w(v, "ss", b_ss_r + 2 * ch + 2)
                c.w(v, "vv")
                c.inc("vv", v.tensor_mul(out=t1, in0=c.ab(dc, ch * 512, ch * 512 + 512),
                                         in1=rstdF[:, ch * 512:ch * 512 + 512]))
                c.w(v, "vv")
                c.inc("vv", v.tensor_add(out=c.ab(dc, ch * 512, ch * 512 + 512),
                                         in0=t1, in1=c.at(dc, ch * 512, ch * 512 + 512)))

    if TAP == "u":
        @block.gpsimd
        def _(g):
            yr = y.rearrange("(kt p) t -> kt p t", p=128)
            c.w(g, "vv")
            for i in range(NKT):
                c.inc("pp", g.dma_start(out=yr[i], in_=c.ab(i)), 16)
        return
    _post_ln(c, block, "b", "a", final, y)


def KTLIM(sw):
    return min(31, (2048 + sw * 1024 + 1023) // 128)


def _ktlim_chunk(sw, qc, N):
    return min(KTLIM(sw), (2048 + sw * 1024 + (qc + 1) * N - 1) // 128)


def _maskslots(sw, qc, kt, N):
    out = []
    for sp in (0, 1):
        j = kt - 16 * sp - (sw * 1024 + qc * N) // 128
        if 0 <= j < max(1, N // 128):
            if N == 64:
                off = (qc * 64) % 128
                out.append(sp * 8 + (0 if off == 64 else 1))
            else:
                out.append(sp * 8 + {128: 1, 256: 2, 512: 4}[N] + j)
    return out


def _l1_attn(c, block, io, kv_in, kv_out, kv_pair, final, y):
    t = c.t
    vbufs = [t["qb0"], t["qb1"], t["qsc"], t["qta"]]

    b_tt = c.cnt["tt"]
    b_wl0, b_wl1 = c.cnt["wl0"], c.cnt["wl1"]
    b_ss = c.cnt["ss"]
    b_so = c.cnt["so"]

    @block.gpsimd
    def _(g):
        for ci in range(16):
            _stream(c, g, io["win_t"], 16 + ci, ci % 2,
                    guard=("tt", b_tt + 4 * (ci - 1)) if ci >= 2 else ("tt", b_tt))

    @block.tensor
    def _(tt):
        c.w(tt, "ss", b_ss)
        c.w(tt, "vv")
        c.w(tt, "xl")
        gidx = 0
        for et in range(16):
            c.w(tt, f"wl{et % 2}", (b_wl0 if et % 2 == 0 else b_wl1) + 16 * (et // 2 + 1))
            buf = t["wb0"] if et % 2 == 0 else t["wb1"]
            for ch in range(4):
                if gidx >= 4:
                    c.w(tt, "ss", b_ss + gidx - 3)
                for kt in range(NKT):
                    ins = tt.matmul(c.ps(gidx % 4), buf[:, kt * 128:kt * 128 + 128],
                                    c.at(kt, ch * 512, ch * 512 + 512),
                                    start=(kt == 0), stop=(kt == NKT - 1))
                c.inc("tt", ins)
                gidx += 1

    b_sd = [c.cnt["sd0"], c.cnt["sd1"]]

    @block.scalar
    def _(s):
        for et in range(16):
            ksc = t["tmpA"] if et % 2 == 0 else t["tmpB"]
            for ch in range(4):
                g = et * 4 + ch
                c.w(s, "tt", b_tt + g + 1)
                if et >= 2 and ch == 0:
                    c.w(s, f"sd{et % 2}", b_sd[et % 2] + 16 * ((et - 2) // 2 + 1))
                c.inc("ss", s.activation(out=ksc[:, ch * 512:ch * 512 + 512],
                                         in_=c.ps(g % 4), func=AF.Identity,
                                         bias=t["bias"][:, 80 + et:81 + et], scale=1.0))

    @block.sync
    def _(s):
        for et in range(16):
            ksc = t["tmpA"] if et % 2 == 0 else t["tmpB"]
            c.w(s, "ss", b_ss + et * 4 + 4)
            c.inc(f"sd{et % 2}", s.dma_start(out=kv_in[et * 128:et * 128 + 128, :], in_=ksc[:]), 16)

    b_kl = c.cnt["kl"]
    b_tt_v = c.cnt["tt"]
    b_ss_v = c.cnt["ss"]
    b_so_v = c.cnt["so"]
    vmm_last_plan = {}
    _ttp = c.cnt["tt"]
    for vg in range(4):
        _ttp += 16
        vmm_last_plan[vg] = _ttp

    @block.gpsimd
    def _(g):
        c.w(g, "sd0")
        c.w(g, "sd1")
        vbb = t["scr"][0:1, 2048:4096]
        c.inc("kl", g.dma_start(out=vbb, in_=io["vbrow"]), 16)
        for vg in range(4):
            if vg >= 1:
                c.w(g, "tt", vmm_last_plan[vg - 1])
            for kq in range(4):
                c.inc("kl", g.dma_start(
                    out=vbufs[kq][:, 0:2048].rearrange("p (kt m) -> p kt m", m=512),
                    in_=io["wv1_t"][vg, kq * 4:kq * 4 + 4].transpose([1, 0, 2])), 16)

    @block.tensor
    def _(tt):
        vbb = t["scr"][0:1, 2048:4096]
        i = 0
        for vg in range(4):
            c.w(tt, "kl", b_kl + 16 + 64 * (vg + 1))
            for tt_ in range(16):
                if i >= 2:
                    c.w(tt, "ss", b_ss_v + i - 1)
                bank = 4 + i % 2
                for kt in range(NKT):
                    tt.matmul(c.ps(bank), c.at(kt, tt_ * 128, tt_ * 128 + 128),
                              vbufs[kt // 4][:, (kt % 4) * 512:(kt % 4) * 512 + 512],
                              start=(kt == 0), stop=False)
                ins = tt.matmul(c.ps(bank), t["ones_bf"][0:1, 0:128],
                                vbb[0:1, vg * 512:vg * 512 + 512],
                                start=False, stop=True)
                c.inc("tt", ins)
                i += 1

    b_sd4 = [c.cnt[f"sd{j}"] for j in range(4)]

    @block.scalar
    def _(s):
        i = 0
        for vg in range(4):
            for tt_ in range(16):
                c.w(s, "tt", b_tt_v + i + 1)
                if i >= 4:
                    c.w(s, f"sd{i % 4}", b_sd4[i % 4] + 64 * ((i - 4) // 4 + 1))
                slot = i % 4
                c.inc("ss", s.copy(out=t["tmpA"][:, slot * 512:slot * 512 + 512],
                                   in_=c.ps(4 + i % 2)))
                i += 1

    @block.sync
    def _(s):
        i = 0
        for vg in range(4):
            for tt_ in range(16):
                c.w(s, "ss", b_ss_v + i + 1)
                slot = i % 4
                for sub in range(4):
                    dh_tile = vg * 4 + sub
                    base = 2048 + dh_tile * 128 + tt_ * 8
                    c.inc(f"sd{i % 4}", s.dma_start(
                        out=kv_in[base:base + 8, :].rearrange("r (t dh) -> (r t) dh", dh=128),
                        in_=t["tmpA"][:, slot * 512 + sub * 128:slot * 512 + sub * 128 + 128]), 16)
                i += 1

    b_so_ag = c.cnt["so"]
    b_tt_tm = c.cnt["tt"]

    @block.gpsimd
    def _(g):
        for j in range(4):
            c.w(g, f"sd{j}")
        c.inc("cc", g.collective_compute(
            "AllGather", ALU.bypass, replica_groups=[list(range(8))],
            ins=[kv_in.ap().opt()], outs=[kv_out.ap().opt()]))
        c.w(g, "cc")
        c.w(g, "sl")
        g.reg_load(t["kvb"], t["misc"][0:1, 1840:1841].bitcast(I32))
        c.inc("kl", g.dma_start(
            out=kv_pair[0:2 * TOK, :].rearrange("(r w) d -> r w d", w=TOK),
            in_=bass.AP(kv_out, t["kvb"], [[SEQ * D, 2], [D, TOK], [1, D]])), 16)
        g.reg_add(t["kvt"], t["kvb"], TOK * D)
        c.inc("kl", g.dma_start(
            out=kv_pair[2 * TOK:4 * TOK, :].rearrange("(r w) d -> r w d", w=TOK),
            in_=bass.AP(kv_out, t["kvt"], [[SEQ * D, 2], [D, TOK], [1, D]])), 16)
        c.w(g, "tt", b_tt_tm)
        tm = io["trimask"]
        for q4 in range(4):
            c.inc("kl", g.dma_start(
                out=vbufs[q4][:, 0:2048].rearrange("p (s n) -> p s n", n=512),
                in_=tm[q4 * 4:q4 * 4 + 4].transpose([1, 0, 2])), 16)
    tri_tok = c.cnt["kl"]
    b_kl_pair = tri_tok

    def tri_slice(slot, N):
        return vbufs[slot // 4][:, (slot % 4) * 512:(slot % 4) * 512 + N]

    qh = t["hblk"][:, 0:2048]
    kh = t["hblk"][:, 2048:6144]
    vh = t["hblk"][:, 6144:10240]
    rlb = t["qw"][:, 0:1024]
    rlrow = t["qw"][0:1, 1024:2048]
    btab = t["chunkF"]

    P = {"tt": c.cnt["tt"], "ss": c.cnt["ss"], "vv": c.cnt["vv"], "kl": c.cnt["kl"]}
    heads = []
    for h in range(NH):
        N = QCH[h]
        nqh = 1024 // N
        hd = {"h": h, "N": N, "nqh": nqh}
        hd["kl"] = [P["kl"] + 16 * (i + 1) for i in range(4)]
        P["kl"] += 64
        hd["qmm"] = []
        hd["qev"] = []
        for gch in range(4):
            P["tt"] += 1; hd["qmm"].append(P["tt"])
            P["ss"] += 1; hd["qev"].append(P["ss"])
        P["vv"] += 2
        hd["kch"] = P["vv"]
        hd["bt"] = []
        for qf in range(NQC[h]):
            for _ in range(6):
                P["vv"] += 1
            hd["bt"].append(P["vv"])
        hd["sweeps"] = []
        for sw in (0, 1):
            swd = {"slots": []}
            seen_l, seen_pv = set(), set()
            for kt in range(KTLIM(sw) + 1):
                for qc in range(nqh):
                    if kt > _ktlim_chunk(sw, qc, N):
                        continue
                    col = qc * N
                    sl = {"kt": kt, "qc": qc,
                          "lfirst": (col // 512) not in seen_l,
                          "pvfirst": (col // 512) not in seen_pv}
                    seen_l.add(col // 512)
                    seen_pv.add(col // 512)
                    P["ss"] += 1; sl["exp"] = P["ss"]
                    ms = _maskslots(sw, qc, kt, N)
                    sl["mask"] = None
                    if ms:
                        P["vv"] += 1; sl["mask"] = P["vv"]
                        sl["mslots"] = ms
                    swd["slots"].append(sl)
            nsl = len(swd["slots"])
            if nsl:
                P["tt"] += 1; swd["slots"][0]["smm"] = P["tt"]
            for i in range(nsl):
                if i + 1 < nsl:
                    P["tt"] += 1; swd["slots"][i + 1]["smm"] = P["tt"]
                P["tt"] += 1; swd["slots"][i]["lmm"] = P["tt"]
                P["tt"] += 1; swd["slots"][i]["pvmm"] = P["tt"]
            P["vv"] += 2; swd["lev"] = P["vv"]
            P["ss"] += 2; swd["rl"] = P["ss"]
            swd["bc"] = []
            for j in range(2):
                P["tt"] += 1
                P["ss"] += 1
                swd["bc"].append((P["tt"], P["ss"]))
            P["vv"] += 2; swd["aev"] = P["vv"]
            hd["sweeps"].append(swd)
        heads.append(hd)

    b_wl0q, b_wl1q = c.cnt["wl0"], c.cnt["wl1"]

    @block.gpsimd
    def _(g):
        c.w(g, "kl", b_kl_pair)
        for h in range(NH):
            hd = heads[h]
            guard = ("tt", heads[h - 2]["qmm"][-1]) if h >= 2 else ("tt", b_tt_tm)
            _stream(c, g, io["win_t"], h, h % 2, guard=guard)
            if h >= 1:
                c.w(g, "tt", heads[h - 1]["sweeps"][1]["slots"][-1]["pvmm"])
            for r in range(2):
                c.inc("kl", g.dma_start(
                    out=kh[:, r * 2048:r * 2048 + 2048],
                    in_=kv_pair[r * TOK + h * 128:r * TOK + h * 128 + 128, :]), 16)
                base = 2 * TOK + r * TOK + h * 128
                c.inc("kl", g.dma_start(
                    out=vh[:, r * 2048:r * 2048 + 2048].rearrange("p (kt dh) -> p kt dh", dh=128),
                    in_=bass.AP(kv_pair, base * D, [[128, 128], [16384, 16], [1, 128]])), 16)

    @block.tensor
    def _(tt):
        for h in range(NH):
            hd = heads[h]
            N = hd["N"]
            c.w(tt, f"wl{h % 2}", (b_wl0q if h % 2 == 0 else b_wl1q) + 16 * (h // 2 + 1))
            buf = t["wb0"] if h % 2 == 0 else t["wb1"]
            for gch in range(4):
                if gch == 0 and h >= 1:
                    c.w(tt, "ss", heads[h - 1]["qev"][3])
                elif gch >= 1:
                    c.w(tt, "ss", hd["qev"][gch - 1])
                for kt in range(NKT):
                    ins = tt.matmul(c.ps(7), buf[:, kt * 128:kt * 128 + 128],
                                    c.at(kt, gch * 512, gch * 512 + 512),
                                    start=(kt == 0), stop=(kt == NKT - 1))
                c.inc("tt", ins)
            c.w(tt, "kl", hd["kl"][-1])
            for sw in (0, 1):
                swd = hd["sweeps"][sw]
                slots = swd["slots"]

                def emit_smm(i):
                    sl = slots[i]
                    col = sl["qc"] * N
                    if i >= 2:
                        c.w(tt, "ss", slots[i - 2]["exp"])
                    elif sw == 1 and len(hd["sweeps"][0]["slots"]) >= 2 - i:
                        c.w(tt, "ss", hd["sweeps"][0]["slots"][i - 2]["exp"])
                    ins = tt.matmul(c.ps(2 + i % 2, N),
                                    kh[:, sl["kt"] * 128:sl["kt"] * 128 + 128],
                                    qh[:, sw * 1024 + col:sw * 1024 + col + N],
                                    start=True, stop=True)
                    c.inc("tt", ins)

                if slots:
                    emit_smm(0)
                for i, sl in enumerate(slots):
                    kt, qc = sl["kt"], sl["qc"]
                    col = qc * N
                    ktl = _ktlim_chunk(sw, qc, N)
                    if i + 1 < len(slots):
                        emit_smm(i + 1)
                    ptok = ("vv", sl["mask"]) if sl["mask"] else ("ss", sl["exp"])
                    c.w(tt, ptok[0], ptok[1])
                    pb = t["pb0"] if i % 2 == 0 else t["pb1"]
                    if sl["kt"] == 0 and sw == 1:
                        c.w(tt, "vv", hd["sweeps"][0]["lev"])
                    elif sl["kt"] == 0 and sw == 0 and h >= 1:
                        c.w(tt, "vv", heads[h - 1]["sweeps"][1]["lev"])
                    ins = tt.matmul(c.ps(4 + col // 512, 512)[0:1, col % 512:col % 512 + N],
                                    t["ones_bf"][:, 0:1], pb[:, 0:N],
                                    start=sl["lfirst"], stop=(kt == ktl),
                                    skip_group_check=True)
                    c.inc("tt", ins)
                    if sl["kt"] == 0 and sw == 1:
                        c.w(tt, "vv", hd["sweeps"][0]["aev"])
                    elif sl["kt"] == 0 and sw == 0 and h >= 1:
                        c.w(tt, "vv", heads[h - 1]["sweeps"][1]["aev"])
                    ins = tt.matmul(c.ps(col // 512, 512)[:, col % 512:col % 512 + N],
                                    vh[:, kt * 128:kt * 128 + 128], pb[:, 0:N],
                                    start=sl["pvfirst"], stop=(kt == ktl),
                                    skip_group_check=True)
                    c.inc("tt", ins)
                for j in range(2):
                    c.w(tt, "ss", swd["rl"])
                    if j == 1:
                        c.w(tt, "ss", swd["bc"][0][1])
                    elif sw == 1:
                        c.w(tt, "ss", hd["sweeps"][0]["bc"][1][1])
                    elif h >= 1:
                        c.w(tt, "ss", heads[h - 1]["sweeps"][1]["bc"][1][1])
                    ins = tt.matmul(c.ps(6, 512), t["ones_bf"][0:1, 0:128],
                                    rlrow[0:1, j * 512:j * 512 + 512],
                                    start=True, stop=True)
                    c.inc("tt", ins)

    @block.scalar
    def _(s):
        for h in range(NH):
            hd = heads[h]
            N = hd["N"]
            nqh = hd["nqh"]
            for gch in range(4):
                c.w(s, "tt", hd["qmm"][gch])
                c.inc("ss", s.activation(out=qh[:, gch * 512:gch * 512 + 512],
                                         in_=c.ps(7), func=AF.Identity,
                                         bias=t["bias"][:, 64 + h:65 + h], scale=1.0))
            for sw in (0, 1):
                swd = hd["sweeps"][sw]
                for i, sl in enumerate(swd["slots"]):
                    c.w(s, "tt", sl["smm"])
                    if i == 0 and sw == 0:
                        c.w(s, "vv", hd["bt"][-1])
                    if i >= 2:
                        c.w(s, "tt", swd["slots"][i - 2]["pvmm"])
                    elif sw == 1:
                        nsl = hd["sweeps"][0]["slots"]
                        if len(nsl) >= 2 - i:
                            c.w(s, "tt", nsl[i - 2]["pvmm"])
                    pb = t["pb0"] if i % 2 == 0 else t["pb1"]
                    qfull = sw * nqh + sl["qc"]
                    c.inc("ss", s.activation(
                        out=pb[:, 0:N], in_=c.ps(2 + i % 2, N),
                        func=AF.Exp, bias=btab[:, qfull * 32 + sl["kt"]:qfull * 32 + sl["kt"] + 1],
                        scale=SCALE1))
                c.w(s, "vv", swd["lev"])
                lrow = t["scrF"][0:1, 1024:2048]
                lnl = t["scrF"][0:1, 0:1024]
                c.inc("ss", s.activation(out=lnl, in_=lrow, func=AF.Ln,
                                         scale=float(2.0 ** -6)))
                c.w(s, "ss")
                c.inc("ss", s.activation(out=rlrow[0:1, 0:1024], in_=lnl, func=AF.Exp,
                                         scale=-1.0))
                for j in range(2):
                    c.w(s, "tt", swd["bc"][j][0])
                    c.inc("ss", s.copy(out=rlb[:, j * 512:j * 512 + 512], in_=c.ps(6, 512)))

    @block.vector
    def _(v):
        for h in range(NH):
            hd = heads[h]
            N = hd["N"]
            nqh = hd["nqh"]
            c.w(v, "ss", hd["qev"][0])
            kch = t["scrF"][:, 0:32]
            capc = t["scrF"][:, 97:98]
            c.w(v, "vv")
            c.inc("vv", v.tensor_scalar(out=kch, in0=c.m(M_KC, M_KC + 32),
                                        scalar1=c.m(M_SLB + h, M_SLB + h + 1),
                                        scalar2=None, op0=ALU.mult))
            c.w(v, "vv")
            c.inc("vv", v.tensor_scalar(out=capc, in0=c.m(M_SLB + h, M_SLB + h + 1),
                                        scalar1=(N - 1) / 2.0, scalar2=None, op0=ALU.mult))
            for qf in range(NQC[h]):
                slot = int(HQ0[h]) + qf
                sqt = t["scrF"][:, 96:97]
                vm = t["scrF"][:, 32:64]
                pen = t["scrF"][:, 64:96]
                bsl = btab[:, qf * 32:qf * 32 + 32]
                c.w(v, "vv")
                c.inc("vv", v.tensor_scalar(out=sqt, in0=c.m(M_CTB + 16 + slot, M_CTB + 17 + slot),
                                            scalar1=c.m(M_SLB + h, M_SLB + h + 1),
                                            scalar2=None, op0=ALU.mult))
                c.w(v, "vv")
                c.inc("vv", v.tensor_scalar(out=bsl, in0=kch, scalar1=sqt,
                                            scalar2=None, op0=ALU.subtract))
                c.w(v, "vv")
                c.inc("vv", v.tensor_scalar(out=bsl, in0=bsl, scalar1=capc,
                                            scalar2=None, op0=ALU.min))
                c.w(v, "vv")
                c.inc("vv", v.tensor_scalar(out=vm, in0=c.m(M_KTI, M_KTI + 32),
                                            scalar1=c.m(M_CTB + 160 + slot, M_CTB + 161 + slot),
                                            scalar2=None, op0=ALU.subtract))
                c.w(v, "vv")
                c.inc("vv", v.tensor_scalar(out=pen, in0=vm, scalar1=0.0, scalar2=-1e30,
                                            op0=ALU.max, op1=ALU.mult))
                c.w(v, "vv")
                c.inc("vv", v.tensor_tensor(out=bsl, in0=bsl, in1=pen, op=ALU.add))
            for sw in (0, 1):
                swd = hd["sweeps"][sw]
                for i, sl in enumerate(swd["slots"]):
                    if sl["mask"] is not None:
                        pb = t["pb0"] if i % 2 == 0 else t["pb1"]
                        c.w(v, "ss", sl["exp"])
                        c.w(v, "kl", tri_tok)
                        ins = None
                        for ms in sl["mslots"]:
                            c.w(v, "vv")
                            ins = v.tensor_tensor(out=pb[:, 0:N], in0=pb[:, 0:N],
                                                  in1=tri_slice(ms, N), op=ALU.mult)
                        c.inc("vv", ins)
                c.w(v, "tt", swd["slots"][-1]["lmm"])
                lrow = t["scrF"][0:1, 1024:2048]
                for j in range(2):
                    c.w(v, "vv")
                    c.inc("vv", v.tensor_copy(out=lrow[0:1, j * 512:j * 512 + 512],
                                              in_=c.ps(4 + j, 512)[0:1, 0:512]))
                for j in range(2):
                    c.w(v, "ss", swd["bc"][j][1])
                    c.w(v, "tt", swd["slots"][-1]["pvmm"])
                    c.w(v, "vv")
                    c.inc("vv", v.scalar_tensor_tensor(
                        out=c.ab(h, sw * 1024 + j * 512, sw * 1024 + j * 512 + 512),
                        in0=c.ps(j, 512), scalar=float(2.0 ** -6),
                        in1=rlb[:, j * 512:j * 512 + 512],
                        op0=ALU.mult,
                        op1=(ALU.bypass if TAP == "l1a_pv" else ALU.mult)))

    for nm in ("tt", "ss", "vv", "kl"):
        assert c.cnt[nm] == P[nm], (nm, c.cnt[nm], P[nm])

    if TAP in ("l1a_att", "l1a_pv"):
        @block.gpsimd
        def _(g):
            yr = y.rearrange("(kt p) t -> kt p t", p=128)
            c.w(g, "vv")
            for i in range(NKT):
                c.inc("pp", g.dma_start(out=yr[i], in_=c.ab(i)), 16)
        return

    b_tt3 = c.cnt["tt"]
    b_ss3 = c.cnt["ss"]
    b_wl0o, b_wl1o = c.cnt["wl0"], c.cnt["wl1"]

    @block.gpsimd
    def _(g):
        for ci in range(16):
            _stream(c, g, io["wou_t"], ci, ci % 2,
                    guard=("tt", b_tt3 + 4 * (ci - 1)) if ci >= 2 else ("tt", b_tt3))

    @block.tensor
    def _(tt):
        c.w(tt, "vv")
        c.w(tt, "ss", b_ss3)
        gidx = 0
        for ot in range(16):
            c.w(tt, f"wl{ot % 2}", (b_wl0o if ot % 2 == 0 else b_wl1o) + 16 * (ot // 2 + 1))
            buf = t["wb0"] if ot % 2 == 0 else t["wb1"]
            for ch in range(4):
                if gidx >= 4:
                    c.w(tt, "ss", b_ss3 + gidx - 3)
                for kt in range(NKT):
                    tt.matmul(c.ps(gidx % 4), buf[:, kt * 128:kt * 128 + 128],
                              c.ab(kt, ch * 512, ch * 512 + 512),
                              start=(kt == 0), stop=False)
                ins = tt.matmul(c.ps(gidx % 4), t["ident_bf"][:],
                                c.at(ot, ch * 512, ch * 512 + 512),
                                start=False, stop=True)
                c.inc("tt", ins)
                gidx += 1

    @block.scalar
    def _(s):
        for ot in range(16):
            for ch in range(4):
                gk = ot * 4 + ch
                c.w(s, "tt", b_tt3 + gk + 1)
                c.inc("ss", s.activation(out=c.at(ot, ch * 512, ch * 512 + 512),
                                         in_=c.ps(gk % 4), func=AF.Identity,
                                         bias=t["bias"][:, 112 + ot:113 + ot], scale=1.0))

    _post_ln(c, block, "a", "a", final, y)


def _l1_ffn(c, block, io, final, y):
    t = c.t
    rtb = t["qsc"][:, 0:128]
    rmb = t["qsc"][0:8, 128:256]
    atb = t["qb0"][:, 0:2048]
    bwb = t["qb1"][:, 0:2048]
    ohbc = t["qw"][:, 0:2048]
    hm = t["qta"][:, 0:2048]
    lgsb = t["scrF"][0:8, 0:2048]
    lgT = c.m(M_X1, M_X1 + 128)
    mx = c.m(M_X1 + 128, M_X1 + 144)
    ohT = c.m(M_X1 + 144, M_X1 + 272)
    dsc = c.m(M_X1 + 272, M_X1 + 280)
    oh8 = t["scr"][0:8, 4096:6144]

    b_kl = c.cnt["kl"]

    @block.gpsimd
    def _(g):
        c.w(g, "vv")
        c.w(g, "tt")
        c.inc("kl", g.dma_start(out=rtb.rearrange("p (kt m) -> p kt m", m=8),
                                in_=io["rt_t"].transpose([1, 0, 2])), 16)
        c.inc("kl", g.dma_start(out=rmb, in_=io["rmat"]), 16)
        c.inc("kl", g.dma_start(out=atb.rearrange("p (kt m) -> p kt m", m=128),
                                in_=io["at_t"].transpose([1, 0, 2])), 16)
        c.inc("kl", g.dma_start(out=bwb, in_=io["bwt_t"]), 16)

    @block.tensor
    def _(tt):
        c.w(tt, "kl", b_kl + 64)
        c.w(tt, "ss")
        c.w(tt, "vv")
        for ch in range(4):
            for kt in range(NKT):
                ins = tt.matmul(c.ps(ch % 2, 512, 8), rtb[:, kt * 8:kt * 8 + 8],
                                c.at(kt, ch * 512, ch * 512 + 512),
                                start=(kt == 0), stop=(kt == NKT - 1))
            c.inc("tt", ins)
            if ch >= 1:
                c.w(tt, "ss", c.cnt["ss"] - 1)

    b_tt1 = c.cnt["tt"]

    @block.scalar
    def _(s):
        for ch in range(4):
            c.w(s, "tt", b_tt1 - 4 + ch + 1)
            c.inc("ss", s.copy(out=lgsb[0:8, ch * 512:ch * 512 + 512], in_=c.ps(ch % 2, 512, 8)))

    @block.tensor
    def _(tt):
        c.w(tt, "ss")
        for tt_ in range(16):
            ins = tt.transpose(c.ps(2, 128)[0:128, tt_ * 8:tt_ * 8 + 8],
                               lgsb[0:8, tt_ * 128:tt_ * 128 + 128],
                               t["ident_f"][0:8, 0:8])
        c.inc("tt", ins)

    @block.scalar
    def _(s):
        c.w(s, "tt")
        c.inc("ss", s.copy(out=lgT, in_=c.ps(2, 128)))

    @block.vector
    def _(v):
        c.w(v, "ss")
        for tt_ in range(16):
            c.w(v, "vv")
            c.inc("vv", v.reduce_max(out=mx[:, tt_:tt_ + 1], in_=lgT[:, tt_ * 8:tt_ * 8 + 8],
                                     axis=mybir.AxisListType.X))
            c.w(v, "vv")
            c.inc("vv", v.tensor_scalar(out=ohT[:, tt_ * 8:tt_ * 8 + 8],
                                        in0=lgT[:, tt_ * 8:tt_ * 8 + 8],
                                        scalar1=mx[:, tt_:tt_ + 1], scalar2=None,
                                        op0=ALU.subtract))
            c.w(v, "vv")
            c.inc("vv", v.tensor_scalar(out=ohT[:, tt_ * 8:tt_ * 8 + 8],
                                        in0=ohT[:, tt_ * 8:tt_ * 8 + 8],
                                        scalar1=1e30, scalar2=1.0, op0=ALU.mult, op1=ALU.add))
            c.w(v, "vv")
            c.inc("vv", v.tensor_scalar(out=ohT[:, tt_ * 8:tt_ * 8 + 8],
                                        in0=ohT[:, tt_ * 8:tt_ * 8 + 8],
                                        scalar1=0.0, scalar2=1.0, op0=ALU.max, op1=ALU.min))

    b_tt2 = c.cnt["tt"]
    b_ss2 = c.cnt["ss"]

    @block.tensor
    def _(tt):
        c.w(tt, "vv")
        for tt_ in range(16):
            if tt_ >= 1:
                c.w(tt, "ss", b_ss2 + tt_)
            ins = tt.transpose(c.ps(3, 128)[0:8, 0:128], ohT[:, tt_ * 8:tt_ * 8 + 8],
                               t["ident_f"][:])
            c.inc("tt", ins)

    @block.scalar
    def _(s):
        for tt_ in range(16):
            c.w(s, "tt", b_tt2 + tt_ + 1)
            c.inc("ss", s.copy(out=oh8[0:8, tt_ * 128:tt_ * 128 + 128],
                               in_=c.ps(3, 128)[0:8, 0:128]))

    b_tt3 = c.cnt["tt"]
    b_ss3 = c.cnt["ss"]

    @block.tensor
    def _(tt):
        c.w(tt, "ss", b_ss3)
        for ch in range(4):
            if ch >= 2:
                c.w(tt, "ss", b_ss3 + ch - 1)
            ins = tt.matmul(c.ps(ch % 2, 512), rmb[0:8, 0:128],
                            oh8[0:8, ch * 512:ch * 512 + 512], start=True, stop=True)
            c.inc("tt", ins)

    @block.scalar
    def _(s):
        for ch in range(4):
            c.w(s, "tt", b_tt3 + ch + 1)
            c.inc("ss", s.copy(out=ohbc[:, ch * 512:ch * 512 + 512], in_=c.ps(ch % 2, 512)))

    b_tt4 = c.cnt["tt"]
    b_vv4 = c.cnt["vv"]

    @block.tensor
    def _(tt):
        c.w(tt, "ss")
        for ch in range(4):
            if ch >= 2:
                c.w(tt, "vv", b_vv4 + ch - 1)
            for kt in range(NKT):
                ins = tt.matmul(c.ps(2 + ch % 2, 512), atb[:, kt * 128:kt * 128 + 128],
                                c.at(kt, ch * 512, ch * 512 + 512),
                                start=(kt == 0), stop=(kt == NKT - 1))
            c.inc("tt", ins)

    @block.vector
    def _(v):
        for ch in range(4):
            c.w(v, "tt", b_tt4 + ch + 1)
            c.w(v, "vv")
            c.inc("vv", v.scalar_tensor_tensor(
                out=hm[:, ch * 512:ch * 512 + 512], in0=c.ps(2 + ch % 2, 512),
                scalar=2.0, in1=ohbc[:, ch * 512:ch * 512 + 512],
                op0=ALU.mult, op1=ALU.mult))

    b_tt5 = c.cnt["tt"]
    b_ss5 = c.cnt["ss"]
    b_wl0f, b_wl1f = c.cnt["wl0"], c.cnt["wl1"]

    @block.gpsimd
    def _(g):
        for ci in range(16):
            _stream(c, g, io["wb_t"], ci, ci % 2,
                    guard=("tt", b_tt5 + 4 * (ci - 1)) if ci >= 2 else None)

    @block.tensor
    def _(tt):
        c.w(tt, "vv")
        gidx = 0
        for ot in range(16):
            c.w(tt, f"wl{ot % 2}", (b_wl0f if ot % 2 == 0 else b_wl1f) + 16 * (ot // 2 + 1))
            buf = t["wb0"] if ot % 2 == 0 else t["wb1"]
            for ch in range(4):
                if gidx >= 4:
                    c.w(tt, "ss", b_ss5 + gidx - 3)
                for kt in range(NKT):
                    tt.matmul(c.ps(gidx % 4), buf[:, kt * 128:kt * 128 + 128],
                              c.at(kt, ch * 512, ch * 512 + 512),
                              start=(kt == 0), stop=False)
                tt.matmul(c.ps(gidx % 4), bwb[:, ot * 128:ot * 128 + 128],
                          hm[:, ch * 512:ch * 512 + 512], start=False, stop=False)
                ins = tt.matmul(c.ps(gidx % 4), t["ident_bf"][:],
                                c.at(ot, ch * 512, ch * 512 + 512),
                                start=False, stop=True)
                c.inc("tt", ins)
                gidx += 1

    @block.scalar
    def _(s):
        for ot in range(16):
            for ch in range(4):
                gk = ot * 4 + ch
                c.w(s, "tt", b_tt5 + gk + 1)
                c.inc("ss", s.activation(out=c.ab(ot, ch * 512, ch * 512 + 512),
                                         in_=c.ps(gk % 4), func=AF.Identity,
                                         bias=t["bias"][:, 128 + ot:129 + ot], scale=1.0))

    _post_ln(c, block, "b", "a", final, y)



def _np32(a):
    return np.ascontiguousarray(np.asarray(a), dtype=np.float32)


def _host_shared(kw):
    g = {}
    wq, wk, wv, wo = (_np32(kw[k]) for k in
                      ["L0_attn_Wq", "L0_attn_Wk", "L0_attn_Wv", "L0_attn_Wo"])
    bkq = _bt(wq.T)
    bkk = _bt(wk.T)
    bkv = _bt(wv.T)
    wkv = np.empty((32, NKT, 128, 128), np.float32)
    wkv[0::2] = bkk
    wkv[1::2] = bkv
    g["wkv_t"] = wkv
    g["wq_t"] = bkq
    g["wo_t"] = _bt(wo.T)
    wg = _np32(kw["L0_ffn_w_gate_up"])
    perm = np.empty((2 * HF,), np.int64)
    for j in range(HF // 128):
        perm[256 * j:256 * j + 128] = np.arange(128 * j, 128 * j + 128)
        perm[256 * j + 128:256 * j + 256] = np.arange(HF + 128 * j, HF + 128 * j + 128)
    g["wgu_t"] = _bt(np.ascontiguousarray(wg[perm].T))
    wp = _np32(kw["L0_ffn_w_proj"])
    g["wpj_t"] = _bt(np.ascontiguousarray(wp.T))
    win = _np32(kw["L1_attn_in_proj"])
    winT = np.ascontiguousarray(win.T)
    g["win_t"] = _bt(winT[:, 0:2 * D])
    g["wv1_t"] = _bt(winT[:, 2 * D:3 * D], 512)
    g["wou_t"] = _bt(np.ascontiguousarray(_np32(kw["L1_attn_out_w"]).T))
    g["wb_t"] = _bt(np.ascontiguousarray(_np32(kw["L1_ffn_W_base"]).T))
    rt = np.ascontiguousarray(_np32(kw["L1_ffn_router_W"]).T)
    g["rt_t"] = np.ascontiguousarray(rt.reshape(NKT, 128, 8))
    at = np.ascontiguousarray(_np32(kw["L1_ffn_A"]).reshape(128, D).T)
    g["at_t"] = np.ascontiguousarray(at.reshape(NKT, 128, 128))
    g["bwt_t"] = np.ascontiguousarray(
        _np32(kw["L1_ffn_Bw"]).transpose(0, 2, 1).reshape(128, D))
    g["rmat"] = (np.arange(8)[:, None] == (np.arange(128)[None, :] // 16)).astype(np.float32)
    g["biases_cat"] = np.concatenate([
        _np32(kw["L0_attn_bq"]).reshape(16, 128),
        _np32(kw["L0_attn_bk"]).reshape(16, 128),
        _np32(kw["L0_attn_bv"]).reshape(16, 128),
        _np32(kw["L0_attn_bo"]).reshape(16, 128),
        _np32(kw["L1_attn_in_proj_bias"]).reshape(48, 128),
        _np32(kw["L1_attn_out_b"]).reshape(16, 128),
        _np32(kw["L1_ffn_b_base"]).reshape(16, 128)], 0)
    g["slopes"] = _np32(kw["L1_attn_alibi_slopes"]).reshape(1, 16)
    g["vbrow"] = _np32(kw["L1_attn_in_proj_bias"])[2 * D:3 * D].reshape(1, D)
    cls_off = [64, 0, 0, -128, 0, -128, -256, -384]
    tms = {}
    p_ = np.arange(128)[:, None]
    n_ = np.arange(512)[None, :]
    for par in (0, 1):
        tm = np.ones((16, 128, 512), np.float32)
        for sp in (0, 1):
            if sp != par:
                continue
            for ic, off in enumerate(cls_off):
                tm[sp * 8 + ic] = (p_ <= n_ + off).astype(np.float32)
        tms[par] = tm.astype(BF)
    g["trimask"] = tms
    return g


def _host_core(core):
    seq, half = core // 2, core % 2
    q0 = half * TOK
    ct = np.zeros((1, 512), np.float32)
    if half == 1:
        ct[0, core - 1] = 1.0
    for h in range(NH):
        N = QCH[h]
        for qf in range(NQC[h]):
            ct[0, 16 + HQ0[h] + qf] = q0 + qf * N + (N - 1) / 2.0
            ct[0, 160 + HQ0[h] + qf] = (q0 + (qf + 1) * N - 1) // 128
    ci = np.zeros((1, 8), np.int32)
    ci[0, 0] = 2 * seq * SEQ * D
    return ct, ci


def _prep(kw):
    g = _host_shared(kw)
    x = _np32(kw["L0_attn_x"])
    per = []
    for core in range(8):
        seq, half = core // 2, core % 2
        ct, ci = _host_core(core)
        d = {"x_in": np.ascontiguousarray(x[seq, half * TOK:(half + 1) * TOK, :].T),
             "coretab": ct, "corei": ci}
        per.append(d)
    return g, per


_IN_BY_PART = {
    "l0a": ["wkv_t", "wq_t", "wo_t"],
    "l0f": ["wgu_t", "wpj_t"],
    "l1a": ["win_t", "wv1_t", "wou_t", "slopes", "vbrow"],
    "l1f": ["wb_t", "rt_t", "at_t", "bwt_t", "rmat"],
}


def _run(parts, g, per, x_override=None):
    nc = build_program(parts)
    maps = []
    for core in range(8):
        d = dict(per[core])
        if x_override is not None:
            d["x_in"] = np.ascontiguousarray(x_override[core])
        d["biases_cat"] = g["biases_cat"]
        for p in parts:
            for n in _IN_BY_PART[p]:
                d[n] = g[n]
            if p == "l1a":
                d["trimask"] = np.asarray(g["trimask"][core % 2])
        maps.append(d)
    res = run_bass_kernel_spmd(nc, maps, core_ids=list(range(8)))
    return [res.results[i]["y"] for i in range(8)]


def kernel(**kw):
    g, per = _prep(kw)
    if FUSED:
        ys = _run(list(ALL_PARTS), g, per)
    else:
        x_ov = None
        for p in ALL_PARTS:
            ys = _run([p], g, per, x_override=x_ov)
            x_ov = ys
    out = np.empty((4, SEQ, D), np.float32)
    for core in range(8):
        seq, half = core // 2, core % 2
        out[seq, half * TOK:(half + 1) * TOK, :] = ys[core].T
    return out


# version 03
# speedup vs baseline: 1.3132x; 1.0387x over previous
import numpy as np
import ml_dtypes
from contextlib import ExitStack
import concourse.bass as bass
import concourse.mybir as mybir
from concourse.bass_utils import run_bass_kernel_spmd

F32 = mybir.dt.float32
BF16 = mybir.dt.bfloat16
I32 = mybir.dt.int32
AF = mybir.ActivationFunctionType
ALU = mybir.AluOpType
BF = ml_dtypes.bfloat16

D = 2048
TOK = 2048
NKT = 16
SEQ = 4096
NH = 16
HD = 128
SCALE0 = float(HD ** -0.5)
SCALE1 = float(1.0 / np.sqrt(HD))
LN_EPS = 1e-5
HF = 8192
RMS_EPS = float(np.finfo(np.float32).eps)
HB = 6
QCH = [64, 128, 128, 256, 256, 256] + [512] * 10
NQC = [TOK // n for n in QCH]
HQ0 = np.cumsum([0] + NQC)
ALL_PARTS = ("l0a", "l0f", "l1a", "l1f")
FUSED = True
TAP = None
L1A_SKIP = set()

M_CT = 0
M_CTB = 512
M_SLB = 1024
M_KC = 1040
M_KTI = 1072
M_C8 = 1104
M_CP = 1232
M_CS = 1264
M_X1 = 1280
M_SL = 1800


def _bt(w, mc=128):
    K, M = w.shape
    return np.ascontiguousarray(w.reshape(K // 128, 128, M // mc, mc).transpose(2, 0, 1, 3))


class Ctx:
    def __init__(self, nc, stack):
        self.nc, self.stack = nc, stack
        self.cnt, self.sems, self.t = {}, {}, {}

    def sem(self, n):
        self.sems[n] = self.stack.enter_context(self.nc.semaphore(n))
        self.cnt[n] = 0

    def inc(self, n, ins, step=1):
        ins.then_inc(self.sems[n], step)
        self.cnt[n] += step
        return self.cnt[n]

    def w(self, eng, n, val=None):
        v = self.cnt[n] if val is None else val
        if v > 0:
            eng.wait_ge(self.sems[n], v)

    def at(self, i, c0=0, c1=TOK):
        return self.t["arena_a"][:, i * TOK + c0:i * TOK + c1]

    def ab(self, i, c0=0, c1=TOK):
        return self.t["arena_b"][:, i * TOK + c0:i * TOK + c1]

    def ps(self, i, n=512, p=128):
        return self.t[f"ps{i}"][0:p, 0:n]

    def m(self, c0, c1, p=128):
        return self.t["misc"][0:p, c0:c1]

    def srow(self, c0, c1):
        return self.t["misc"][0:1, c0:c1]


SEMS = ["xl", "bl", "wl0", "wl1", "ql0", "ql1", "tt", "ss", "vv", "gg",
        "cc", "so", "sl", "pe", "ev", "qv", "kl", "pp", "sd0", "sd1", "sd2", "sd3"]


def build_program(parts):
    parts = [p for p in ALL_PARTS if p in parts]
    nc = bass.Bass("TRN2", target_bir_lowering=False)
    io = {}

    def di(name, shape, dtype=F32):
        io[name] = nc.dram_tensor(name, shape, dtype, kind="ExternalInput").ap()

    di("x_in", [D, TOK])
    di("biases_cat", [144, 128])
    di("coretab", [1, 512])
    di("corei", [1, 8], I32)
    if "l0a" in parts:
        di("wkv_t", [32, NKT, 128, 128])
        di("wq_t", [16, NKT, 128, 128])
        di("wo_t", [16, NKT, 128, 128])
    if "l0f" in parts:
        di("wgu_t", [128, NKT, 128, 128])
        di("wpj_t", [16, HF // 128, 128, 128])
    if "l1a" in parts:
        di("win_t", [32, NKT, 128, 128])
        di("wv1_t", [4, NKT, 128, 512])
        di("wou_t", [16, NKT, 128, 128])
        di("slopes", [1, 16])
        di("vbrow", [1, D])
        di("trimask", [16, 128, 512], BF16)
    if "l1f" in parts:
        di("wb_t", [16, NKT, 128, 128])
        di("rt_t", [NKT, 128, 8])
        di("at_t", [NKT, 128, 128])
        di("bwt_t", [128, D])
        di("rmat", [8, 128])
    y = nc.dram_tensor("y", [D, TOK], F32, kind="ExternalOutput").ap()

    cs_in = nc.dram_tensor("cs_in", [128, 16], F32)
    cs_out = nc.dram_tensor("cs_out", [8 * 128, 16], F32, addr_space="Shared")
    kv_inK = nc.dram_tensor("kv_inK", [TOK, D], BF16)
    kv_inV = nc.dram_tensor("kv_inV", [TOK, D], BF16)
    kv_outK = nc.dram_tensor("kv_outK", [2 * TOK, D], BF16)
    kv_outV = nc.dram_tensor("kv_outV", [2 * TOK, D], BF16)

    with ExitStack() as st:
        E = st.enter_context
        c = Ctx(nc, st)
        t = c.t
        t["arena_a"] = E(nc.sbuf_tensor("arena_a", [128, NKT * TOK], BF16))
        t["arena_b"] = E(nc.sbuf_tensor("arena_b", [128, NKT * TOK], BF16))
        t["hblk"] = E(nc.sbuf_tensor("hblk", [128, HB * TOK], BF16))
        t["wb0"] = E(nc.sbuf_tensor("wb0", [128, NKT * 128], BF16))
        t["wb1"] = E(nc.sbuf_tensor("wb1", [128, NKT * 128], BF16))
        t["qb0"] = E(nc.sbuf_tensor("qb0", [128, NKT * 128], BF16))
        t["qb1"] = E(nc.sbuf_tensor("qb1", [128, NKT * 128], BF16))
        t["qsc"] = E(nc.sbuf_tensor("qsc", [128, NKT * 128], BF16))
        t["qta"] = E(nc.sbuf_tensor("qta", [128, NKT * 128], BF16))
        t["qw"] = E(nc.sbuf_tensor("qw", [128, NKT * 128], BF16))
        t["scr"] = E(nc.sbuf_tensor("scr", [128, 6144], BF16))
        t["misc"] = E(nc.sbuf_tensor("misc", [128, 2048], F32))
        t["bias"] = E(nc.sbuf_tensor("bias", [128, 144], F32))
        t["ones_bf"] = E(nc.sbuf_tensor("ones_bf", [128, 128], BF16))
        t["ident_bf"] = E(nc.sbuf_tensor("ident_bf", [128, 128], BF16))
        t["ident_f"] = E(nc.sbuf_tensor("ident_f", [128, 128], F32))
        t["selb"] = E(nc.sbuf_tensor("selb", [128, 128], BF16))
        t["onesf"] = E(nc.sbuf_tensor("onesf", [1, 128], F32))
        t["epsc"] = E(nc.sbuf_tensor("epsc", [128, 4], F32))
        t["pb0"] = E(nc.sbuf_tensor("pb0", [128, 512], BF16))
        t["pb1"] = E(nc.sbuf_tensor("pb1", [128, 512], BF16))
        if "l1a" in parts:
            t["kvb"] = E(nc.gpsimd.register("kvb"))
            t["kvt"] = E(nc.gpsimd.register("kvt"))
        for i in range(8):
            t[f"ps{i}"] = E(nc.psum_tensor(f"ps{i}", [128, 512], F32))
        for s in SEMS:
            c.sem(s)
        t["tmpA"] = t["scr"][:, 0:2048]
        t["tmpB"] = t["scr"][:, 2048:4096]
        scf = t["scr"].ap() if hasattr(t["scr"], "ap") else None
        t["scrF"] = t["scr"][:, 0:6144].bitcast(F32)
        t["chunkF"] = t["scr"][:, 4096:6144].bitcast(F32)

        block = E(nc.Block())
        _prolog(c, block, io, parts)
        last = parts[-1]
        if "l0a" in parts:
            _l0_attn(c, block, io, cs_in, cs_out, final=(last == "l0a"), y=y)
        if "l0f" in parts:
            _l0_ffn(c, block, io, final=(last == "l0f"), y=y)
        if "l1a" in parts:
            _l1_attn(c, block, io, kv_inK, kv_inV, kv_outK, kv_outV, final=(last == "l1a"), y=y)
        if "l1f" in parts:
            _l1_ffn(c, block, io, final=True, y=y)
        @block.sync
        def _(s):
            c.w(s, "so")
            c.w(s, "pp")
    return nc


def _prolog(c, block, io, parts):
    t = c.t

    @block.gpsimd
    def _(g):
        xr = io["x_in"].rearrange("(kt p) t -> kt p t", p=128)
        for i in range(NKT):
            c.inc("xl", g.dma_start(out=c.at(i), in_=xr[i]), 16)
        c.inc("gg", g.memset(t["ones_bf"][:], 1.0))
        c.inc("gg", g.memset(t["onesf"][:], 1.0))
        c.inc("gg", g.memset(t["epsc"][:, 0:1], LN_EPS))
        c.inc("gg", g.memset(t["epsc"][:, 1:2], RMS_EPS))
        c.inc("gg", g.memset(t["ident_bf"][:], 1.0))
        c.w(g, "gg")
        c.inc("gg", g.affine_select(out=t["ident_bf"][:], in_=t["ident_bf"][:],
                                    pattern=[[-1, 128]], compare_op=ALU.is_equal,
                                    fill=0.0, base=0, channel_multiplier=1))
        c.inc("gg", g.memset(t["ident_f"][:], 1.0))
        c.w(g, "gg")
        c.inc("gg", g.affine_select(out=t["ident_f"][:], in_=t["ident_f"][:],
                                    pattern=[[-1, 128]], compare_op=ALU.is_equal,
                                    fill=0.0, base=0, channel_multiplier=1))
        c.inc("gg", g.memset(t["selb"][:], 0.0))
        c.w(g, "gg")
        c.inc("gg", g.memset(t["selb"][0:64, 0:64], 1.0 / 64.0))
        c.inc("gg", g.memset(t["selb"][64:128, 64:128], 1.0 / 64.0))
        c.inc("gg", g.iota(c.m(M_KC, M_KC + 32), [[128, 32]], base=0, channel_multiplier=1,
                           allow_small_or_imprecise_dtypes=True))
        c.inc("gg", g.iota(c.m(M_KTI, M_KTI + 32), [[1, 32]], base=0, channel_multiplier=0,
                           allow_small_or_imprecise_dtypes=True))


    @block.sync
    def _(s):
        scrf = t["scrF"]
        c.inc("bl", s.dma_start(out=scrf[:, 0:128], in_=io["biases_cat"][0:128, :]), 16)
        c.inc("bl", s.dma_start(out=scrf[0:16, 128:256], in_=io["biases_cat"][128:144, :]), 16)
        c.inc("sl", s.dma_start(out=c.srow(M_CT, M_CT + 512), in_=io["coretab"]), 16)
        if "l1a" in parts:
            c.inc("sl", s.dma_start(out=c.srow(M_SL, M_SL + 16), in_=io["slopes"]), 16)
            c.inc("sl", s.dma_start(out=t["misc"][0:1, 1840:1848].bitcast(I32), in_=io["corei"]), 16)

    @block.tensor
    def _(tt):
        c.w(tt, "bl"); c.w(tt, "gg"); c.w(tt, "sl")
        tt.transpose(c.ps(0, 128), t["scrF"][:, 0:128], t["ident_f"][:])
        tt.transpose(c.ps(1, 16)[0:128, 0:16], t["scrF"][0:16, 128:256], t["ident_f"][0:16, 0:16])
        ins = tt.matmul(c.ps(2, 512), t["onesf"][0:1, :], c.srow(M_CT, M_CT + 512),
                        start=True, stop=True)
        if "l1a" in parts:
            ins = tt.matmul(c.ps(3, 16), t["onesf"][0:1, :], c.srow(M_SL, M_SL + 16),
                            start=True, stop=True)
        c.inc("tt", ins)

    @block.scalar
    def _(s):
        c.w(s, "tt")
        s.copy(out=t["bias"][:, 0:128], in_=c.ps(0, 128))
        s.copy(out=t["bias"][:, 128:144], in_=c.ps(1, 16)[0:128, 0:16])
        ins = s.copy(out=c.m(M_CTB, M_CTB + 512), in_=c.ps(2, 512))
        if "l1a" in parts:
            ins = s.copy(out=c.m(M_SLB, M_SLB + 16), in_=c.ps(3, 16))
        c.inc("ss", ins)


def _stream(c, g, dram, ci, buf_slot, nkt=NKT, guard=None):
    if guard is not None:
        c.w(g, guard[0], guard[1])
    buf = c.t["wb0"] if buf_slot == 0 else c.t["wb1"]
    c.inc(f"wl{buf_slot}",
          g.dma_start(out=buf[:, 0:nkt * 128].rearrange("p (kt m) -> p kt m", m=128),
                      in_=dram[ci].transpose([1, 0, 2])), 16)
    return buf


def _post_ln(c, block, u, out, final, y):
    t = c.t
    ut = c.at if u == "a" else c.ab
    ot = c.at if out == "a" else c.ab
    usq = t["tmpA"][:, 0:512]
    scf = t["scrF"]
    b_ss, b_vv, b_pe = c.cnt["ss"], c.cnt["vv"], c.cnt["pe"]

    @block.scalar
    def _(s):
        c.w(s, "vv")
        c.w(s, "tt")
        c.w(s, "ss")
        for i in range(NKT):
            for ch in range(4):
                if i * 4 + ch > 0:
                    c.w(s, "pe", b_pe + i * 4 + ch)
                c.inc("ss", s.activation(out=usq, in_=ut(i, ch * 512, ch * 512 + 512),
                                         func=AF.Square))

    @block.tensor
    def _(tt):
        for i in range(NKT):
            for ch in range(4):
                c.w(tt, "ss", b_ss + i * 4 + ch + 1)
                tt.matmul(c.ps(ch), t["ones_bf"][:], ut(i, ch * 512, ch * 512 + 512),
                          start=(i == 0), stop=(i == NKT - 1))
                ins = tt.matmul(c.ps(4 + ch), t["ones_bf"][:], usq,
                                start=(i == 0), stop=(i == NKT - 1))
                if i == NKT - 1 and ch == 3:
                    c.inc("tt", ins)
                else:
                    c.inc("pe", ins)

    b_tt2 = c.cnt["tt"]
    for ch in range(4):
        mloc = scf[:, 1024:1536]
        rloc = t["chunkF"][:, 512:1024]
        mrloc = t["chunkF"][:, 0:512]

        @block.vector
        def _(v):
            if ch == 0:
                c.w(v, "tt", b_tt2)
            else:
                c.w(v, "vv")
            c.inc("vv", v.tensor_scalar(out=mloc, in0=c.ps(ch), scalar1=1.0 / D, scalar2=None,
                            op0=ALU.mult))
            nm2 = t["chunkF"][:, 0:512]
            c.w(v, "vv")
            c.inc("vv", v.scalar_tensor_tensor(out=nm2, in0=mloc, scalar=-1.0, in1=mloc,
                                   op0=ALU.mult, op1=ALU.mult))
            var = t["chunkF"][:, 512:1024]
            c.w(v, "vv")
            c.inc("vv", v.scalar_tensor_tensor(out=var, in0=c.ps(4 + ch), scalar=1.0 / D, in1=nm2,
                                   op0=ALU.mult, op1=ALU.add))

        b_ss2 = c.cnt["ss"]

        @block.scalar
        def _(s):
            c.w(s, "vv")
            var = t["chunkF"][:, 512:1024]
            lnv = t["chunkF"][:, 0:512]
            c.inc("ss", s.activation(out=lnv, in_=var, func=AF.Ln, bias=t["epsc"][:, 0:1], scale=1.0))
            c.w(s, "ss")
            c.inc("ss", s.activation(out=rloc, in_=lnv, func=AF.Exp, scale=-0.5))

        @block.vector
        def _(v):
            c.w(v, "ss", b_ss2 + 2)
            c.inc("vv", v.tensor_mul(out=mrloc, in0=mloc, in1=rloc))
            tmp = scf[:, 1536:2048]
            for i in range(NKT):
                c.w(v, "vv")
                c.inc("vv", v.tensor_mul(out=tmp, in0=ut(i, ch * 512, ch * 512 + 512), in1=rloc))
                c.w(v, "vv")
                c.inc("vv", v.tensor_sub(out=ot(i, ch * 512, ch * 512 + 512), in0=tmp, in1=mrloc))
    if final:
        b_vv3 = c.cnt["vv"]

        @block.gpsimd
        def _(g):
            yr = y.rearrange("(kt p) t -> kt p t", p=128)
            for i in range(NKT):
                c.w(g, "vv", b_vv3 - 2 * (NKT - 1 - i))
                c.inc("pp", g.dma_start(out=yr[i], in_=ot(i)), 16)


def _phi_chain(c, emitters, et, ch, psbank, bias_col, dest, tok_tt):
    raise NotImplementedError


def _l0_attn(c, block, io, cs_in, cs_out, final, y):
    t = c.t
    temp1 = t["chunkF"][:, 0:512]
    temp2 = t["chunkF"][:, 512:1024]

    b_tt = c.cnt["tt"]
    b_wl0, b_wl1 = c.cnt["wl0"], c.cnt["wl1"]

    @block.gpsimd
    def _(g):
        c.w(g, "xl")
        for ci in range(32):
            _stream(c, g, io["wkv_t"], ci, ci % 2,
                    guard=("tt", b_tt + 4 * (ci - 1)) if ci >= 2 else None)

    b_ss_a = c.cnt["ss"]

    @block.tensor
    def _(tt):
        c.w(tt, "ss", b_ss_a)
        gidx = 0
        for et in range(16):
            for kv in range(2):
                ci = 2 * et + kv
                c.w(tt, f"wl{ci % 2}", (b_wl0 if ci % 2 == 0 else b_wl1) + 16 * (ci // 2 + 1))
                buf = t["wb0"] if ci % 2 == 0 else t["wb1"]
                for ch in range(4):
                    bank = gidx % 4
                    if gidx >= 4:
                        gp = gidx - 4
                        ep, rr = gp // 8, gp % 8
                        etok = (b_ss_a + ep * 12 + 2 * (rr % 4) + 1) if rr < 4 else                                (b_ss_a + ep * 12 + 8 + (rr % 4) + 1)
                        c.w(tt, "ss", etok)
                    for kt in range(NKT):
                        ins = tt.matmul(c.ps(bank), buf[:, kt * 128:kt * 128 + 128],
                                        c.at(kt, ch * 512, ch * 512 + 512),
                                        start=(kt == 0), stop=(kt == NKT - 1))
                    c.inc("tt", ins)
                    gidx += 1

    b_ss, b_vv = b_ss_a, c.cnt["vv"]
    A = lambda et, ch: b_ss + et * 12 + 2 * ch + 1
    C = lambda et, ch: b_ss + et * 12 + 2 * ch + 2
    V = lambda et, ch: b_ss + et * 12 + 8 + ch + 1
    Bt = lambda et, ch: b_vv + et * 10 + 2 * ch + 1
    Dt = lambda et, ch: b_vv + et * 10 + 2 * ch + 2
    KVt = lambda et: b_vv + et * 10 + 9
    CSt = lambda et: b_vv + et * 10 + 10

    @block.scalar
    def _(s):
        for et in range(16):
            for ch in range(4):
                gk = et * 8 + ch
                c.w(s, "tt", b_tt + gk + 1)
                if et > 0 and ch == 0:
                    c.w(s, "vv", CSt(et - 1))
                elif ch > 0:
                    c.w(s, "vv", Dt(et, ch - 1))
                ins = s.activation(out=temp1, in_=c.ps(gk % 4), func=AF.Identity,
                                   bias=t["bias"][:, 16 + et:17 + et], scale=1.0)
                c.inc("ss", ins)
                c.w(s, "vv", Bt(et, ch))
                c.inc("ss", s.activation(out=temp2, in_=t["tmpA"][:, ch * 512:ch * 512 + 512],
                                         func=AF.Exp))
            for ch in range(4):
                gv = et * 8 + 4 + ch
                c.w(s, "tt", b_tt + gv + 1)
                ins = s.activation(out=t["tmpB"][:, ch * 512:ch * 512 + 512],
                                   in_=c.ps(gv % 4), func=AF.Identity,
                                   bias=t["bias"][:, 32 + et:33 + et], scale=1.0)
                c.inc("ss", ins)

    @block.vector
    def _(v):
        for et in range(16):
            for ch in range(4):
                c.w(v, "ss", A(et, ch))
                c.inc("vv", v.tensor_scalar(out=t["tmpA"][:, ch * 512:ch * 512 + 512], in0=temp1,
                                scalar1=0.0, scalar2=None, op0=ALU.min))
                c.w(v, "ss", C(et, ch))
                c.inc("vv", v.scalar_tensor_tensor(out=t["tmpA"][:, ch * 512:ch * 512 + 512],
                                       in0=temp1, scalar=0.0, in1=temp2,
                                       op0=ALU.max, op1=ALU.add))
            c.w(v, "ss", V(et, 3))
            c.w(v, "vv", Dt(et, 3))
            c.inc("vv", v.tensor_mul(out=c.ab(et), in0=t["tmpA"], in1=t["tmpB"]))
            c.w(v, "vv", KVt(et))
            c.inc("vv", v.reduce_sum(out=c.m(M_CS + et, M_CS + et + 1), in_=c.ab(et),
                         axis=mybir.AxisListType.X))

    b_so = c.cnt["so"]

    @block.sync
    def _(s):
        c.w(s, "vv")
        c.inc("so", s.dma_start(out=cs_in[:, :], in_=c.m(M_CS, M_CS + 16)), 16)

    b_xl = c.cnt["xl"]

    @block.gpsimd
    def _(g):
        c.w(g, "so", b_so + 16)
        c.inc("cc", g.collective_compute(
            "AllGather", ALU.bypass, replica_groups=[list(range(8))],
            ins=[cs_in.ap().opt()], outs=[cs_out.ap().opt()]))
        c.w(g, "cc")
        c.inc("xl", g.dma_start(
            out=c.m(M_C8, M_C8 + 128).rearrange("p (r t) -> p r t", t=16),
            in_=cs_out.ap().rearrange("(r p) t -> r p t", p=128).transpose([1, 0, 2])), 16)

    b_vv2 = c.cnt["vv"]

    @block.vector
    def _(v):
        c.w(v, "xl", b_xl + 16)
        cp = [c.m(M_CP, M_CP + 16), c.m(M_CP + 16, M_CP + 32)]
        c.inc("vv", v.tensor_scalar(out=cp[0], in0=c.m(M_C8, M_C8 + 16),
                        scalar1=c.m(M_CTB, M_CTB + 1), scalar2=None, op0=ALU.mult))
        for r in range(1, 8):
            c.w(v, "vv")
            c.inc("vv", v.scalar_tensor_tensor(out=cp[r % 2], in0=c.m(M_C8 + 16 * r, M_C8 + 16 * r + 16),
                                   scalar=c.m(M_CTB + r, M_CTB + r + 1),
                                   in1=cp[(r + 1) % 2], op0=ALU.mult, op1=ALU.add))

    b_tt2 = c.cnt["tt"]
    b_ss2 = c.cnt["ss"]
    b_wl0, b_wl1 = c.cnt["wl0"], c.cnt["wl1"]

    @block.gpsimd
    def _(g):
        for ci in range(16):
            _stream(c, g, io["wq_t"], ci, ci % 2,
                    guard=("tt", b_tt2 + 4 * (ci - 1)) if ci >= 2 else ("tt", b_tt2))

    @block.tensor
    def _(tt):
        c.w(tt, "ss", b_ss2)
        gidx = 0
        for et in range(16):
            ci = et
            c.w(tt, f"wl{ci % 2}", (b_wl0 if ci % 2 == 0 else b_wl1) + 16 * (ci // 2 + 1))
            buf = t["wb0"] if ci % 2 == 0 else t["wb1"]
            for ch in range(4):
                bank = gidx % 4
                if gidx >= 4:
                    gp = gidx - 4
                    c.w(tt, "ss", b_ss2 + 2 * (gp % 4) + (gp // 4) * 8 + 1)
                for kt in range(NKT):
                    ins = tt.matmul(c.ps(bank), buf[:, kt * 128:kt * 128 + 128],
                                    c.at(kt, ch * 512, ch * 512 + 512),
                                    start=(kt == 0), stop=(kt == NKT - 1))
                c.inc("tt", ins)
                gidx += 1

    assert c.cnt["ss"] == b_ss2, (c.cnt["ss"], b_ss2)
    b_vv3 = c.cnt["vv"]
    A2 = lambda et, ch: b_ss2 + et * 8 + 2 * ch + 1
    C2 = lambda et, ch: b_ss2 + et * 8 + 2 * ch + 2
    B2 = lambda et, ch: b_vv3 + et * 10 + 2 * ch + 1
    D2 = lambda et, ch: b_vv3 + et * 10 + 2 * ch + 2
    SCt = lambda et: b_vv3 + et * 10 + 9
    CXt = lambda et: b_vv3 + et * 10 + 10

    @block.scalar
    def _(s):
        for et in range(16):
            for ch in range(4):
                gk = et * 4 + ch
                c.w(s, "tt", b_tt2 + gk + 1)
                if et == 0 and ch == 0:
                    c.w(s, "vv", b_vv3)
                elif ch == 0:
                    c.w(s, "vv", CXt(et - 1))
                else:
                    c.w(s, "vv", D2(et, ch - 1))
                ins = s.activation(out=temp1, in_=c.ps(gk % 4), func=AF.Identity,
                                   bias=t["bias"][:, 0 + et:1 + et], scale=1.0)
                c.inc("ss", ins)
                c.w(s, "vv", B2(et, ch))
                c.inc("ss", s.activation(out=temp2, in_=t["tmpA"][:, ch * 512:ch * 512 + 512],
                                         func=AF.Exp))

    @block.vector
    def _(v):
        c.w(v, "vv", b_vv2 + 8)
        for et in range(16):
            for ch in range(4):
                c.w(v, "ss", A2(et, ch))
                c.inc("vv", v.tensor_scalar(out=t["tmpA"][:, ch * 512:ch * 512 + 512], in0=temp1,
                                scalar1=0.0, scalar2=None, op0=ALU.min))
                c.w(v, "ss", C2(et, ch))
                c.inc("vv", v.scalar_tensor_tensor(out=t["tmpA"][:, ch * 512:ch * 512 + 512],
                                       in0=temp1, scalar=0.0, in1=temp2,
                                       op0=ALU.max, op1=ALU.add))
            c.w(v, "vv")
            c.inc("vv", v.tensor_tensor_scan(out=t["tmpB"], data0=c.ab(et), data1=c.ab(et),
                                 initial=c.m(M_CP + 16 + et, M_CP + 17 + et),
                                 op0=ALU.add, op1=ALU.bypass))
            c.w(v, "vv")
            c.inc("vv", v.scalar_tensor_tensor(out=c.ab(et), in0=t["tmpB"], scalar=SCALE0,
                                   in1=t["tmpA"], op0=ALU.mult, op1=ALU.mult))

    b_tt3 = c.cnt["tt"]
    b_ss3 = c.cnt["ss"]
    b_wl0, b_wl1 = c.cnt["wl0"], c.cnt["wl1"]

    @block.gpsimd
    def _(g):
        for ci in range(16):
            _stream(c, g, io["wo_t"], ci, ci % 2,
                    guard=("tt", b_tt3 + 4 * (ci - 1)) if ci >= 2 else ("tt", b_tt3))

    @block.tensor
    def _(tt):
        c.w(tt, "vv")
        c.w(tt, "ss", b_ss3)
        gidx = 0
        for ot in range(16):
            ci = ot
            c.w(tt, f"wl{ci % 2}", (b_wl0 if ci % 2 == 0 else b_wl1) + 16 * (ci // 2 + 1))
            buf = t["wb0"] if ci % 2 == 0 else t["wb1"]
            for ch in range(4):
                bank = gidx % 4
                if gidx >= 4:
                    c.w(tt, "ss", b_ss3 + gidx - 3)
                for kt in range(NKT):
                    tt.matmul(c.ps(bank), buf[:, kt * 128:kt * 128 + 128],
                              c.ab(kt, ch * 512, ch * 512 + 512),
                              start=(kt == 0), stop=False)
                ins = tt.matmul(c.ps(bank), t["ident_bf"][:],
                                c.at(ot, ch * 512, ch * 512 + 512),
                                start=False, stop=True)
                c.inc("tt", ins)
                gidx += 1

    @block.scalar
    def _(s):
        for ot in range(16):
            for ch in range(4):
                gk = ot * 4 + ch
                c.w(s, "tt", b_tt3 + gk + 1)
                ins = s.activation(out=c.at(ot, ch * 512, ch * 512 + 512),
                                   in_=c.ps(gk % 4), func=AF.Identity,
                                   bias=t["bias"][:, 48 + ot:49 + ot], scale=1.0)
                c.inc("ss", ins)

    _post_ln(c, block, "a", "a", final, y)


def _l0_ffn(c, block, io, final, y):
    t = c.t
    gsil = lambda ch: t["scr"][:, 4096 + ch * 512:4096 + ch * 512 + 512]
    qb = [t["qb0"], t["qb1"]]
    blocks = [list(range(i, min(i + HB, 64))) for i in range(0, 64, HB)]

    SC = []
    for bi, blk in enumerate(blocks):
        for local, pr in enumerate(blk):
            for gu in (0, 1):
                SC.append(("wchunk", bi, local, pr, gu))
        for dc in range(16):
            SC.append(("pchunk", bi, dc))

    cnt = {k: c.cnt[k] for k in ("ss", "vv", "tt", "pe", "ql0", "ql1")}
    tok = {}
    qn = 0
    scnt = 0
    gcnt = 0
    qchain = []
    projev = []
    siltok = {}
    hactok = {}
    rmmtok = {}
    acctok = {}

    def plan_quant(ncols):
        rec = {"qn": qn}
        cnt["ss"] += 1; rec["abs"] = cnt["ss"]
        nsb = (ncols + 511) // 512
        rec["smm"] = []
        rec["sev"] = []
        for sb in range(nsb):
            cnt["pe"] += 1; rec["smm"].append(cnt["pe"])
            cnt["ss"] += 1; rec["sev"].append(cnt["ss"])
        for op in ("a", "b", "q", "wh"):
            cnt["vv"] += 1; rec[op] = cnt["vv"]
        rec["nsb"] = nsb
        rec["ncols"] = ncols
        return rec

    plan = []
    for e in SC:
        if e[0] == "wchunk":
            bi, local, pr, gu = e[1:]
            rec = {"kind": "w", "bi": bi, "local": local, "pr": pr, "gu": gu,
                   "ci": 2 * pr + gu}
            cnt[f"ql{qn % 2}"] += 16
            rec["dmatok"] = cnt[f"ql{qn % 2}"]
            rec.update(plan_quant(2048))
            qchain.append(rec)
            rec["qidx"] = len(qchain) - 1
            rec["gmm"] = []
            rec["gev"] = []
            nonlocal_scnt = None
            for ch in range(4):
                cnt["tt"] += 1; rec["gmm"].append(cnt["tt"])
                if gu == 0:
                    cnt["ss"] += 1; rec["gev"].append(("ss", cnt["ss"]))
                    siltok[(pr, ch)] = cnt["ss"]
                else:
                    cnt["vv"] += 1; rec["gev"].append(("vv", cnt["vv"]))
                    hactok[(pr, ch)] = cnt["vv"]
                projev.append(rec["gev"][-1])
            if gu == 1:
                rec["rms"] = []
                for ch in range(4):
                    cnt["ss"] += 1
                    cnt["pe"] += 1
                    rec["rms"].append((cnt["ss"], cnt["pe"]))
                    rmmtok[(pr, ch)] = cnt["pe"]
            plan.append(rec)
        else:
            bi, dc = e[1], e[2]
            blk = blocks[bi]
            L = len(blk)
            rec = {"kind": "p", "bi": bi, "dc": dc, "L": L}
            cnt[f"ql{qn % 2}"] += 16
            rec["dmatok"] = cnt[f"ql{qn % 2}"]
            rec.update(plan_quant(L * 128))
            qchain.append(rec)
            rec["qidx"] = len(qchain) - 1
            rec["pmm"] = []
            rec["pev"] = []
            for ch in range(4):
                cnt["tt"] += 1; rec["pmm"].append(cnt["tt"])
                if bi == 0:
                    cnt["ss"] += 1; rec["pev"].append(("ss", cnt["ss"]))
                else:
                    cnt["vv"] += 1; rec["pev"].append(("vv", cnt["vv"]))
                acctok[(bi, dc, ch)] = rec["pev"][-1][1]
                projev.append(rec["pev"][-1])
            plan.append(rec)
        qn += 1

    si = 0
    gi = 0
    for rec in plan:
        rec["sbank"] = []
        for sb in range(rec["nsb"]):
            rec["sbank"].append(si % 2)
            rec.setdefault("sguard", []).append(si - 2)
            si += 1
        rec["gbank"] = []
        for ch in range(4):
            rec["gbank"].append(2 + gi % 2)
            rec.setdefault("gguard", []).append(gi - 2)
            gi += 1
    sevs = [tk for rec in plan for tk in rec["sev"]]

    b0 = {k: c.cnt[k] for k in ("ss", "vv", "tt", "pe")}

    @block.gpsimd
    def _(g):
        for rec in plan:
            i = rec["qidx"]
            if i >= 2:
                c.w(g, "vv", qchain[i - 2]["wh"])
            slot = i % 2
            if rec["kind"] == "w":
                src_ = io["wgu_t"][rec["ci"]].transpose([1, 0, 2])
                ncol = 2048
            else:
                blk = blocks[rec["bi"]]
                src_ = io["wpj_t"][rec["dc"], blk[0]:blk[0] + rec["L"]].transpose([1, 0, 2])
                ncol = rec["L"] * 128
            c.inc(f"ql{i % 2}",
                  g.dma_start(out=qb[slot][:, 0:ncol].rearrange("p (kt m) -> p kt m", m=128),
                              in_=src_), 16)

    @block.tensor
    def _(tt):
        c.w(tt, "ss", b0["ss"])
        for rec in plan:
            i = rec["qidx"]
            for sb in range(rec["nsb"]):
                n = min(512, rec["ncols"] - sb * 512)
                c.w(tt, "ss", rec["abs"])
                gidx = rec["sguard"][sb]
                if gidx >= 0:
                    c.w(tt, "ss", sevs[gidx])
                bank = rec["sbank"][sb]
                nkts = n // 128
                for k2 in range(nkts):
                    kt = sb * 4 + k2
                    ins = tt.matmul(c.ps(bank)[:, k2 * 128:k2 * 128 + 128], t["selb"][:],
                                    t["qta"][:, kt * 128:kt * 128 + 128],
                                    start=True, stop=True)
                c.inc("pe", ins)
            c.w(tt, "vv", rec["wh"])
            for ch in range(4):
                gidx = rec["gguard"][ch]
                if gidx >= 0:
                    c.w(tt, projev[gidx][0], projev[gidx][1])
                bank = rec["gbank"][ch]
                if rec["kind"] == "w":
                    for kt in range(NKT):
                        ins = tt.matmul(c.ps(bank), t["qw"][:, kt * 128:kt * 128 + 128],
                                        c.at(kt, ch * 512, ch * 512 + 512),
                                        start=(kt == 0), stop=(kt == NKT - 1))
                else:
                    for j in range(rec["L"]):
                        ins = tt.matmul(c.ps(bank), t["qw"][:, j * 128:j * 128 + 128],
                                        t["hblk"][:, j * TOK + ch * 512:j * TOK + ch * 512 + 512],
                                        start=(j == 0), stop=(j == rec["L"] - 1))
                c.inc("tt", ins)
            if rec["kind"] == "w" and rec["gu"] == 1:
                for ch in range(4):
                    c.w(tt, "ss", rec["rms"][ch][0])
                    ins = tt.matmul(c.ps(4 + ch), t["ones_bf"][:], gsil(ch),
                                    start=(rec["pr"] == 0), stop=(rec["pr"] == 63))
                    c.inc("pe", ins)

    @block.scalar
    def _(s):
        for rec in plan:
            i = rec["qidx"]
            slot = i % 2
            ncol = rec["ncols"]
            c.w(s, f"ql{i % 2}", rec["dmatok"])
            if i >= 1:
                c.w(s, "vv", qchain[i - 1]["q"])
            c.inc("ss", s.activation(out=t["qta"][:, 0:ncol], in_=qb[slot][:, 0:ncol],
                                     func=AF.Abs))
            for sb in range(rec["nsb"]):
                n = min(512, ncol - sb * 512)
                c.w(s, "pe", rec["smm"][sb])
                if i >= 1:
                    c.w(s, "vv", qchain[i - 1]["wh"])
                c.inc("ss", s.copy(out=t["qsc"][:, sb * 512:sb * 512 + n],
                                   in_=c.ps(rec["sbank"][sb])[:, 0:n]))
            if rec["kind"] == "w" and rec["gu"] == 0:
                pr = rec["pr"]
                for ch in range(4):
                    c.w(s, "tt", rec["gmm"][ch])
                    if pr > 0:
                        c.w(s, "vv", hactok[(pr - 1, ch)])
                    c.inc("ss", s.activation(out=gsil(ch), in_=c.ps(rec["gbank"][ch]),
                                             func=AF.Silu))
            if rec["kind"] == "w" and rec["gu"] == 1:
                pr, local = rec["pr"], rec["local"]
                for ch in range(4):
                    c.w(s, "vv", hactok[(pr, ch)])
                    if pr > 0:
                        c.w(s, "pe", rmmtok[(pr - 1, ch)])
                    c.inc("ss", s.activation(
                        out=gsil(ch),
                        in_=t["hblk"][:, local * TOK + ch * 512:local * TOK + ch * 512 + 512],
                        func=AF.Square))
            if rec["kind"] == "p" and rec["bi"] == 0:
                for ch in range(4):
                    c.w(s, "tt", rec["pmm"][ch])
                    c.inc("ss", s.copy(out=c.ab(rec["dc"], ch * 512, ch * 512 + 512),
                                       in_=c.ps(rec["gbank"][ch])))

    @block.vector
    def _(v):
        for rec in plan:
            i = rec["qidx"]
            slot = i % 2
            ncol = rec["ncols"]
            c.w(v, "ss", rec["sev"][-1])
            c.w(v, "pe", rec["smm"][-1])
            if TAP == "noq":
                for _r in range(2):
                    c.w(v, "vv")
                    c.inc("vv", v.tensor_copy(out=t["qta"][:, 0:ncol], in_=qb[slot][:, 0:ncol]))
                if i >= 1:
                    prev = qchain[i - 1]
                    c.w(v, "tt", prev["gmm"][-1] if "gmm" in prev else prev["pmm"][-1])
                for _r in range(2):
                    c.w(v, "vv")
                    c.inc("vv", v.tensor_copy(out=t["qw"][:, 0:ncol], in_=qb[slot][:, 0:ncol]))
            else:
                c.w(v, "vv")
                c.inc("vv", v.scalar_tensor_tensor(out=t["qta"][:, 0:ncol], in0=qb[slot][:, 0:ncol],
                                                   scalar=2.0, in1=t["qsc"][:, 0:ncol],
                                                   op0=ALU.mult, op1=ALU.is_gt))
                if i >= 1:
                    prev = qchain[i - 1]
                    c.w(v, "tt", prev["gmm"][-1] if "gmm" in prev else prev["pmm"][-1])
                c.w(v, "vv")
                c.inc("vv", v.scalar_tensor_tensor(out=t["qw"][:, 0:ncol], in0=qb[slot][:, 0:ncol],
                                                   scalar=-2.0, in1=t["qsc"][:, 0:ncol],
                                                   op0=ALU.mult, op1=ALU.is_gt))
                c.w(v, "vv")
                c.inc("vv", v.tensor_sub(out=qb[slot][:, 0:ncol], in0=t["qta"][:, 0:ncol],
                                         in1=t["qw"][:, 0:ncol]))
                c.w(v, "vv")
                c.inc("vv", v.tensor_mul(out=t["qw"][:, 0:ncol], in0=qb[slot][:, 0:ncol],
                                         in1=t["qsc"][:, 0:ncol]))
            if rec["kind"] == "w" and rec["gu"] == 1:
                pr, local = rec["pr"], rec["local"]
                for ch in range(4):
                    c.w(v, "tt", rec["gmm"][ch])
                    c.w(v, "ss", siltok[(pr, ch)])
                    c.w(v, "vv")
                    c.inc("vv", v.scalar_tensor_tensor(
                        out=t["hblk"][:, local * TOK + ch * 512:local * TOK + ch * 512 + 512],
                        in0=c.ps(rec["gbank"][ch]), scalar=1.0, in1=gsil(ch),
                        op0=ALU.mult, op1=ALU.mult))
            if rec["kind"] == "p" and rec["bi"] > 0:
                for ch in range(4):
                    c.w(v, "tt", rec["pmm"][ch])
                    c.w(v, "vv")
                    c.inc("vv", v.scalar_tensor_tensor(
                        out=c.ab(rec["dc"], ch * 512, ch * 512 + 512),
                        in0=c.ps(rec["gbank"][ch]), scalar=1.0,
                        in1=c.ab(rec["dc"], ch * 512, ch * 512 + 512),
                        op0=ALU.mult, op1=ALU.add))

    if TAP in ("accraw", "noq"):
        @block.gpsimd
        def _(g):
            yr = y.rearrange("(kt p) t -> kt p t", p=128)
            c.w(g, "vv")
            c.w(g, "ss")
            for i in range(NKT):
                c.inc("pp", g.dma_start(out=yr[i], in_=c.ab(i)), 16)
        return

    rstdF = t["scrF"][:, 0:2048]
    lntmp = t["chunkF"][:, 512:1024]
    b_ss_r = c.cnt["ss"]

    @block.scalar
    def _(s):
        c.w(s, "ss")
        for ch in range(4):
            c.w(s, "pe", rmmtok[(63, ch)])
            c.inc("ss", s.activation(out=lntmp, in_=c.ps(4 + ch), func=AF.Ln,
                                     bias=t["epsc"][:, 1:2], scale=1.0 / HF))
            c.w(s, "ss")
            c.inc("ss", s.activation(out=rstdF[:, ch * 512:ch * 512 + 512], in_=lntmp,
                                     func=AF.Exp, scale=-0.5))

    @block.vector
    def _(v):
        t1 = t["chunkF"][:, 0:512]
        for dc in range(16):
            for ch in range(4):
                c.w(v, "ss", b_ss_r + 2 * ch + 2)
                c.w(v, "vv")
                c.inc("vv", v.tensor_mul(out=t1, in0=c.ab(dc, ch * 512, ch * 512 + 512),
                                         in1=rstdF[:, ch * 512:ch * 512 + 512]))
                c.w(v, "vv")
                c.inc("vv", v.tensor_add(out=c.ab(dc, ch * 512, ch * 512 + 512),
                                         in0=t1, in1=c.at(dc, ch * 512, ch * 512 + 512)))

    if TAP == "u":
        @block.gpsimd
        def _(g):
            yr = y.rearrange("(kt p) t -> kt p t", p=128)
            c.w(g, "vv")
            for i in range(NKT):
                c.inc("pp", g.dma_start(out=yr[i], in_=c.ab(i)), 16)
        return
    _post_ln(c, block, "b", "a", final, y)


def KTLIM(sw):
    return min(31, (2048 + sw * 1024 + 1023) // 128)


def _ktlim_chunk(sw, qc, N):
    return min(KTLIM(sw), (2048 + sw * 1024 + (qc + 1) * N - 1) // 128)


def _maskslots(sw, qc, kt, N):
    out = []
    for sp in (0, 1):
        j = kt - 16 * sp - (sw * 1024 + qc * N) // 128
        if 0 <= j < max(1, N // 128):
            if N == 64:
                off = (qc * 64) % 128
                out.append(sp * 8 + (0 if off == 64 else 1))
            else:
                out.append(sp * 8 + {128: 1, 256: 2, 512: 4}[N] + j)
    return out


def _l1_attn(c, block, io, kv_inK, kv_inV, kv_outK, kv_outV, final, y):
    t = c.t
    vbufs = [t["qb0"], t["qb1"], t["qsc"], t["qta"]]

    b_tt = c.cnt["tt"]
    b_wl0, b_wl1 = c.cnt["wl0"], c.cnt["wl1"]
    b_ss = c.cnt["ss"]
    b_so = c.cnt["so"]

    @block.gpsimd
    def _(g):
        for ci in range(16):
            _stream(c, g, io["win_t"], 16 + ci, ci % 2,
                    guard=("tt", b_tt + 4 * (ci - 1)) if ci >= 2 else ("tt", b_tt))

    @block.tensor
    def _(tt):
        c.w(tt, "ss", b_ss)
        c.w(tt, "vv")
        c.w(tt, "xl")
        gidx = 0
        for et in range(16):
            c.w(tt, f"wl{et % 2}", (b_wl0 if et % 2 == 0 else b_wl1) + 16 * (et // 2 + 1))
            buf = t["wb0"] if et % 2 == 0 else t["wb1"]
            for ch in range(4):
                if gidx >= 4:
                    c.w(tt, "ss", b_ss + gidx - 3)
                for kt in range(NKT):
                    ins = tt.matmul(c.ps(gidx % 4), buf[:, kt * 128:kt * 128 + 128],
                                    c.at(kt, ch * 512, ch * 512 + 512),
                                    start=(kt == 0), stop=(kt == NKT - 1))
                c.inc("tt", ins)
                gidx += 1

    b_sd = [c.cnt["sd0"], c.cnt["sd1"]]

    @block.scalar
    def _(s):
        for et in range(16):
            ksc = t["tmpA"] if et % 2 == 0 else t["tmpB"]
            for ch in range(4):
                g = et * 4 + ch
                c.w(s, "tt", b_tt + g + 1)
                if et >= 2 and ch == 0:
                    c.w(s, f"sd{et % 2}", b_sd[et % 2] + 16 * ((et - 2) // 2 + 1))
                c.inc("ss", s.activation(out=ksc[:, ch * 512:ch * 512 + 512],
                                         in_=c.ps(g % 4), func=AF.Identity,
                                         bias=t["bias"][:, 80 + et:81 + et], scale=1.0))

    @block.sync
    def _(s):
        for et in range(16):
            ksc = t["tmpA"] if et % 2 == 0 else t["tmpB"]
            c.w(s, "ss", b_ss + et * 4 + 4)
            c.inc(f"sd{et % 2}", s.dma_start(out=kv_inK[et * 128:et * 128 + 128, :], in_=ksc[:]), 16)

    @block.gpsimd
    def _(g):
        c.w(g, "sd0")
        c.w(g, "sd1")
        c.inc("cc", g.collective_compute(
            "AllGather", ALU.bypass, replica_groups=[[0, 1], [2, 3], [4, 5], [6, 7]],
            ins=[kv_inK.ap().opt()], outs=[kv_outK.ap().opt()]))

    b_kl = c.cnt["kl"]
    b_tt_v = c.cnt["tt"]
    b_ss_v = c.cnt["ss"]
    b_so_v = c.cnt["so"]
    vmm_last_plan = {}
    _ttp = c.cnt["tt"]
    for vg in range(4):
        _ttp += 16
        vmm_last_plan[vg] = _ttp

    @block.gpsimd
    def _(g):
        c.w(g, "sd0")
        c.w(g, "sd1")
        vbb = t["scr"][0:1, 2048:4096]
        c.inc("kl", g.dma_start(out=vbb, in_=io["vbrow"]), 16)
        for vg in range(4):
            if vg >= 1:
                c.w(g, "tt", vmm_last_plan[vg - 1])
            for kq in range(4):
                c.inc("kl", g.dma_start(
                    out=vbufs[kq][:, 0:2048].rearrange("p (kt m) -> p kt m", m=512),
                    in_=io["wv1_t"][vg, kq * 4:kq * 4 + 4].transpose([1, 0, 2])), 16)

    @block.tensor
    def _(tt):
        vbb = t["scr"][0:1, 2048:4096]
        i = 0
        for vg in range(4):
            c.w(tt, "kl", b_kl + 16 + 64 * (vg + 1))
            for tt_ in range(16):
                if i >= 2:
                    c.w(tt, "ss", b_ss_v + i - 1)
                bank = 4 + i % 2
                for kt in range(NKT):
                    tt.matmul(c.ps(bank), c.at(kt, tt_ * 128, tt_ * 128 + 128),
                              vbufs[kt // 4][:, (kt % 4) * 512:(kt % 4) * 512 + 512],
                              start=(kt == 0), stop=False)
                ins = tt.matmul(c.ps(bank), t["ones_bf"][0:1, 0:128],
                                vbb[0:1, vg * 512:vg * 512 + 512],
                                start=False, stop=True)
                c.inc("tt", ins)
                i += 1

    b_sd4 = [c.cnt[f"sd{j}"] for j in range(4)]

    @block.scalar
    def _(s):
        i = 0
        for vg in range(4):
            for tt_ in range(16):
                c.w(s, "tt", b_tt_v + i + 1)
                if i >= 4:
                    c.w(s, f"sd{i % 4}", b_sd4[i % 4] + 64 * ((i - 4) // 4 + 1))
                slot = i % 4
                c.inc("ss", s.copy(out=t["tmpA"][:, slot * 512:slot * 512 + 512],
                                   in_=c.ps(4 + i % 2)))
                i += 1

    @block.sync
    def _(s):
        i = 0
        for vg in range(4):
            for tt_ in range(16):
                c.w(s, "ss", b_ss_v + i + 1)
                slot = i % 4
                for sub in range(4):
                    dh_tile = vg * 4 + sub
                    base = dh_tile * 128 + tt_ * 8
                    c.inc(f"sd{i % 4}", s.dma_start(
                        out=kv_inV[base:base + 8, :].rearrange("r (t dh) -> (r t) dh", dh=128),
                        in_=t["tmpA"][:, slot * 512 + sub * 128:slot * 512 + sub * 128 + 128]), 16)
                i += 1

    b_so_ag = c.cnt["so"]
    b_tt_tm = c.cnt["tt"]

    @block.gpsimd
    def _(g):
        for j in range(4):
            c.w(g, f"sd{j}")
        c.inc("cc", g.collective_compute(
            "AllGather", ALU.bypass, replica_groups=[[0, 1], [2, 3], [4, 5], [6, 7]],
            ins=[kv_inV.ap().opt()], outs=[kv_outV.ap().opt()]))
        c.w(g, "tt", b_tt_tm)
        tm = io["trimask"]
        for q4 in range(4):
            c.inc("kl", g.dma_start(
                out=vbufs[q4][:, 0:2048].rearrange("p (s n) -> p s n", n=512),
                in_=tm[q4 * 4:q4 * 4 + 4].transpose([1, 0, 2])), 16)
    tri_tok = c.cnt["kl"]
    b_cc_kv = c.cnt["cc"]

    def tri_slice(slot, N):
        return vbufs[slot // 4][:, (slot % 4) * 512:(slot % 4) * 512 + N]

    qh = t["hblk"][:, 0:2048]
    kh = t["hblk"][:, 2048:6144]
    vh = t["hblk"][:, 6144:10240]
    rlb = t["qw"][:, 0:1024]
    rlrow = t["qw"][0:1, 1024:2048]
    btab = t["chunkF"]

    P = {"tt": c.cnt["tt"], "ss": c.cnt["ss"], "vv": c.cnt["vv"], "kl": c.cnt["kl"]}
    heads = []
    for h in range(NH):
        N = QCH[h]
        nqh = 1024 // N
        hd = {"h": h, "N": N, "nqh": nqh}
        hd["kl"] = [P["kl"] + 16 * (i + 1) for i in range(4)]
        P["kl"] += 64
        hd["qmm"] = []
        hd["qev"] = []
        for gch in range(4):
            P["tt"] += 1; hd["qmm"].append(P["tt"])
            P["ss"] += 1; hd["qev"].append(P["ss"])
        P["vv"] += 2
        hd["kch"] = P["vv"]
        hd["bt"] = []
        for qf in range(NQC[h]):
            for _ in range(6):
                P["vv"] += 1
            hd["bt"].append(P["vv"])
        hd["sweeps"] = []
        for sw in (0, 1):
            swd = {"slots": []}
            seen_l, seen_pv = set(), set()
            for kt in range(KTLIM(sw) + 1):
                for qc in range(nqh):
                    if kt > _ktlim_chunk(sw, qc, N):
                        continue
                    col = qc * N
                    sl = {"kt": kt, "qc": qc,
                          "lfirst": (col // 512) not in seen_l,
                          "pvfirst": (col // 512) not in seen_pv}
                    seen_l.add(col // 512)
                    seen_pv.add(col // 512)
                    P["ss"] += 1; sl["exp"] = P["ss"]
                    ms = _maskslots(sw, qc, kt, N)
                    sl["mask"] = None
                    if ms:
                        P["vv"] += 1; sl["mask"] = P["vv"]
                        sl["mslots"] = ms
                    swd["slots"].append(sl)
            nsl = len(swd["slots"])
            if nsl:
                P["tt"] += 1; swd["slots"][0]["smm"] = P["tt"]
            for i in range(nsl):
                if i + 1 < nsl:
                    P["tt"] += 1; swd["slots"][i + 1]["smm"] = P["tt"]
                P["tt"] += 1; swd["slots"][i]["lmm"] = P["tt"]
                P["tt"] += 1; swd["slots"][i]["pvmm"] = P["tt"]
            P["vv"] += 2; swd["lev"] = P["vv"]
            P["ss"] += 2; swd["rl"] = P["ss"]
            swd["bc"] = []
            for j in range(2):
                P["tt"] += 1
                P["ss"] += 1
                swd["bc"].append((P["tt"], P["ss"]))
            P["vv"] += 2; swd["aev"] = P["vv"]
            hd["sweeps"].append(swd)
        heads.append(hd)

    b_wl0q, b_wl1q = c.cnt["wl0"], c.cnt["wl1"]

    @block.gpsimd
    def _(g):
        c.w(g, "kl", tri_tok)
        c.w(g, "cc", b_cc_kv)
        for h in range(NH):
            hd = heads[h]
            guard = ("tt", heads[h - 2]["qmm"][-1]) if h >= 2 else ("tt", b_tt_tm)
            _stream(c, g, io["win_t"], h, h % 2, guard=guard)
            if h >= 1:
                c.w(g, "tt", heads[h - 1]["sweeps"][1]["slots"][-1]["pvmm"])
            for r in range(2):
                c.inc("kl", g.dma_start(
                    out=kh[:, r * 2048:r * 2048 + 2048],
                    in_=kv_outK[r * TOK + h * 128:r * TOK + h * 128 + 128, :]), 16)
                base = r * TOK + h * 128
                c.inc("kl", g.dma_start(
                    out=vh[:, r * 2048:r * 2048 + 2048].rearrange("p (kt dh) -> p kt dh", dh=128),
                    in_=bass.AP(kv_outV, base * D, [[128, 128], [16384, 16], [1, 128]])), 16)

    @block.tensor
    def _(tt):
        for h in range(NH):
            hd = heads[h]
            N = hd["N"]
            c.w(tt, f"wl{h % 2}", (b_wl0q if h % 2 == 0 else b_wl1q) + 16 * (h // 2 + 1))
            buf = t["wb0"] if h % 2 == 0 else t["wb1"]
            for gch in range(4):
                if gch == 0 and h >= 1:
                    c.w(tt, "ss", heads[h - 1]["qev"][3])
                elif gch >= 1:
                    c.w(tt, "ss", hd["qev"][gch - 1])
                for kt in range(NKT):
                    ins = tt.matmul(c.ps(7), buf[:, kt * 128:kt * 128 + 128],
                                    c.at(kt, gch * 512, gch * 512 + 512),
                                    start=(kt == 0), stop=(kt == NKT - 1))
                c.inc("tt", ins)
            c.w(tt, "kl", hd["kl"][-1])
            for sw in (0, 1):
                swd = hd["sweeps"][sw]
                slots = swd["slots"]

                def emit_smm(i):
                    sl = slots[i]
                    col = sl["qc"] * N
                    if i >= 2:
                        c.w(tt, "ss", slots[i - 2]["exp"])
                    elif sw == 1 and len(hd["sweeps"][0]["slots"]) >= 2 - i:
                        c.w(tt, "ss", hd["sweeps"][0]["slots"][i - 2]["exp"])
                    ins = tt.matmul(c.ps(2 + i % 2, N),
                                    kh[:, sl["kt"] * 128:sl["kt"] * 128 + 128],
                                    qh[:, sw * 1024 + col:sw * 1024 + col + N],
                                    start=True, stop=True)
                    c.inc("tt", ins)

                if slots:
                    emit_smm(0)
                for i, sl in enumerate(slots):
                    kt, qc = sl["kt"], sl["qc"]
                    col = qc * N
                    ktl = _ktlim_chunk(sw, qc, N)
                    if i + 1 < len(slots):
                        emit_smm(i + 1)
                    ptok = ("vv", sl["mask"]) if sl["mask"] else ("ss", sl["exp"])
                    c.w(tt, ptok[0], ptok[1])
                    pb = t["pb0"] if i % 2 == 0 else t["pb1"]
                    if sl["kt"] == 0 and sw == 1:
                        c.w(tt, "vv", hd["sweeps"][0]["lev"])
                    elif sl["kt"] == 0 and sw == 0 and h >= 1:
                        c.w(tt, "vv", heads[h - 1]["sweeps"][1]["lev"])
                    ins = tt.matmul(c.ps(4 + col // 512, 512)[0:1, col % 512:col % 512 + N],
                                    t["ones_bf"][:, 0:1], pb[:, 0:N],
                                    start=sl["lfirst"], stop=(kt == ktl),
                                    skip_group_check=True)
                    c.inc("tt", ins)
                    if sl["kt"] == 0 and sw == 1:
                        c.w(tt, "vv", hd["sweeps"][0]["aev"])
                    elif sl["kt"] == 0 and sw == 0 and h >= 1:
                        c.w(tt, "vv", heads[h - 1]["sweeps"][1]["aev"])
                    ins = tt.matmul(c.ps(col // 512, 512)[:, col % 512:col % 512 + N],
                                    vh[:, kt * 128:kt * 128 + 128], pb[:, 0:N],
                                    start=sl["pvfirst"], stop=(kt == ktl),
                                    skip_group_check=True)
                    c.inc("tt", ins)
                for j in range(2):
                    c.w(tt, "ss", swd["rl"])
                    if j == 1:
                        c.w(tt, "ss", swd["bc"][0][1])
                    elif sw == 1:
                        c.w(tt, "ss", hd["sweeps"][0]["bc"][1][1])
                    elif h >= 1:
                        c.w(tt, "ss", heads[h - 1]["sweeps"][1]["bc"][1][1])
                    ins = tt.matmul(c.ps(6, 512), t["ones_bf"][0:1, 0:128],
                                    rlrow[0:1, j * 512:j * 512 + 512],
                                    start=True, stop=True)
                    c.inc("tt", ins)

    @block.scalar
    def _(s):
        for h in range(NH):
            hd = heads[h]
            N = hd["N"]
            nqh = hd["nqh"]
            for gch in range(4):
                c.w(s, "tt", hd["qmm"][gch])
                c.inc("ss", s.activation(out=qh[:, gch * 512:gch * 512 + 512],
                                         in_=c.ps(7), func=AF.Identity,
                                         bias=t["bias"][:, 64 + h:65 + h], scale=1.0))
            for sw in (0, 1):
                swd = hd["sweeps"][sw]
                for i, sl in enumerate(swd["slots"]):
                    c.w(s, "tt", sl["smm"])
                    if i == 0 and sw == 0:
                        c.w(s, "vv", hd["bt"][-1])
                    if i >= 2:
                        c.w(s, "tt", swd["slots"][i - 2]["pvmm"])
                    elif sw == 1:
                        nsl = hd["sweeps"][0]["slots"]
                        if len(nsl) >= 2 - i:
                            c.w(s, "tt", nsl[i - 2]["pvmm"])
                    pb = t["pb0"] if i % 2 == 0 else t["pb1"]
                    qfull = sw * nqh + sl["qc"]
                    c.inc("ss", s.activation(
                        out=pb[:, 0:N], in_=c.ps(2 + i % 2, N),
                        func=AF.Exp, bias=btab[:, qfull * 32 + sl["kt"]:qfull * 32 + sl["kt"] + 1],
                        scale=SCALE1))
                c.w(s, "vv", swd["lev"])
                lrow = t["scrF"][0:1, 1024:2048]
                lnl = t["scrF"][0:1, 0:1024]
                c.inc("ss", s.activation(out=lnl, in_=lrow, func=AF.Ln,
                                         scale=float(2.0 ** -6)))
                c.w(s, "ss")
                c.inc("ss", s.activation(out=rlrow[0:1, 0:1024], in_=lnl, func=AF.Exp,
                                         scale=-1.0))
                for j in range(2):
                    c.w(s, "tt", swd["bc"][j][0])
                    c.inc("ss", s.copy(out=rlb[:, j * 512:j * 512 + 512], in_=c.ps(6, 512)))

    @block.vector
    def _(v):
        for h in range(NH):
            hd = heads[h]
            N = hd["N"]
            nqh = hd["nqh"]
            c.w(v, "ss", hd["qev"][0])
            kch = t["scrF"][:, 0:32]
            capc = t["scrF"][:, 97:98]
            c.w(v, "vv")
            c.inc("vv", v.tensor_scalar(out=kch, in0=c.m(M_KC, M_KC + 32),
                                        scalar1=c.m(M_SLB + h, M_SLB + h + 1),
                                        scalar2=None, op0=ALU.mult))
            c.w(v, "vv")
            c.inc("vv", v.tensor_scalar(out=capc, in0=c.m(M_SLB + h, M_SLB + h + 1),
                                        scalar1=(N - 1) / 2.0, scalar2=None, op0=ALU.mult))
            for qf in range(NQC[h]):
                slot = int(HQ0[h]) + qf
                sqt = t["scrF"][:, 96:97]
                vm = t["scrF"][:, 32:64]
                pen = t["scrF"][:, 64:96]
                bsl = btab[:, qf * 32:qf * 32 + 32]
                c.w(v, "vv")
                c.inc("vv", v.tensor_scalar(out=sqt, in0=c.m(M_CTB + 16 + slot, M_CTB + 17 + slot),
                                            scalar1=c.m(M_SLB + h, M_SLB + h + 1),
                                            scalar2=None, op0=ALU.mult))
                c.w(v, "vv")
                c.inc("vv", v.tensor_scalar(out=bsl, in0=kch, scalar1=sqt,
                                            scalar2=None, op0=ALU.subtract))
                c.w(v, "vv")
                c.inc("vv", v.tensor_scalar(out=bsl, in0=bsl, scalar1=capc,
                                            scalar2=None, op0=ALU.min))
                c.w(v, "vv")
                c.inc("vv", v.tensor_scalar(out=vm, in0=c.m(M_KTI, M_KTI + 32),
                                            scalar1=c.m(M_CTB + 160 + slot, M_CTB + 161 + slot),
                                            scalar2=None, op0=ALU.subtract))
                c.w(v, "vv")
                c.inc("vv", v.tensor_scalar(out=pen, in0=vm, scalar1=0.0, scalar2=-1e30,
                                            op0=ALU.max, op1=ALU.mult))
                c.w(v, "vv")
                c.inc("vv", v.tensor_tensor(out=bsl, in0=bsl, in1=pen, op=ALU.add))
            for sw in (0, 1):
                swd = hd["sweeps"][sw]
                for i, sl in enumerate(swd["slots"]):
                    if sl["mask"] is not None:
                        pb = t["pb0"] if i % 2 == 0 else t["pb1"]
                        c.w(v, "ss", sl["exp"])
                        c.w(v, "kl", tri_tok)
                        ins = None
                        for ms in sl["mslots"]:
                            c.w(v, "vv")
                            ins = v.tensor_tensor(out=pb[:, 0:N], in0=pb[:, 0:N],
                                                  in1=tri_slice(ms, N), op=ALU.mult)
                        c.inc("vv", ins)
                c.w(v, "tt", swd["slots"][-1]["lmm"])
                lrow = t["scrF"][0:1, 1024:2048]
                for j in range(2):
                    c.w(v, "vv")
                    c.inc("vv", v.tensor_copy(out=lrow[0:1, j * 512:j * 512 + 512],
                                              in_=c.ps(4 + j, 512)[0:1, 0:512]))
                for j in range(2):
                    c.w(v, "ss", swd["bc"][j][1])
                    c.w(v, "tt", swd["slots"][-1]["pvmm"])
                    c.w(v, "vv")
                    c.inc("vv", v.scalar_tensor_tensor(
                        out=c.ab(h, sw * 1024 + j * 512, sw * 1024 + j * 512 + 512),
                        in0=c.ps(j, 512), scalar=float(2.0 ** -6),
                        in1=rlb[:, j * 512:j * 512 + 512],
                        op0=ALU.mult,
                        op1=(ALU.bypass if TAP == "l1a_pv" else ALU.mult)))

    for nm in ("tt", "ss", "vv", "kl"):
        assert c.cnt[nm] == P[nm], (nm, c.cnt[nm], P[nm])

    if TAP in ("l1a_att", "l1a_pv"):
        @block.gpsimd
        def _(g):
            yr = y.rearrange("(kt p) t -> kt p t", p=128)
            c.w(g, "vv")
            for i in range(NKT):
                c.inc("pp", g.dma_start(out=yr[i], in_=c.ab(i)), 16)
        return

    b_tt3 = c.cnt["tt"]
    b_ss3 = c.cnt["ss"]
    b_wl0o, b_wl1o = c.cnt["wl0"], c.cnt["wl1"]

    @block.gpsimd
    def _(g):
        for ci in range(16):
            _stream(c, g, io["wou_t"], ci, ci % 2,
                    guard=("tt", b_tt3 + 4 * (ci - 1)) if ci >= 2 else ("tt", b_tt3))

    @block.tensor
    def _(tt):
        c.w(tt, "vv")
        c.w(tt, "ss", b_ss3)
        gidx = 0
        for ot in range(16):
            c.w(tt, f"wl{ot % 2}", (b_wl0o if ot % 2 == 0 else b_wl1o) + 16 * (ot // 2 + 1))
            buf = t["wb0"] if ot % 2 == 0 else t["wb1"]
            for ch in range(4):
                if gidx >= 4:
                    c.w(tt, "ss", b_ss3 + gidx - 3)
                for kt in range(NKT):
                    tt.matmul(c.ps(gidx % 4), buf[:, kt * 128:kt * 128 + 128],
                              c.ab(kt, ch * 512, ch * 512 + 512),
                              start=(kt == 0), stop=False)
                ins = tt.matmul(c.ps(gidx % 4), t["ident_bf"][:],
                                c.at(ot, ch * 512, ch * 512 + 512),
                                start=False, stop=True)
                c.inc("tt", ins)
                gidx += 1

    @block.scalar
    def _(s):
        for ot in range(16):
            for ch in range(4):
                gk = ot * 4 + ch
                c.w(s, "tt", b_tt3 + gk + 1)
                c.inc("ss", s.activation(out=c.at(ot, ch * 512, ch * 512 + 512),
                                         in_=c.ps(gk % 4), func=AF.Identity,
                                         bias=t["bias"][:, 112 + ot:113 + ot], scale=1.0))

    _post_ln(c, block, "a", "a", final, y)


def _l1_ffn(c, block, io, final, y):
    t = c.t
    rtb = t["qsc"][:, 0:128]
    rmb = t["qsc"][0:8, 128:256]
    atb = t["qb0"][:, 0:2048]
    bwb = t["qb1"][:, 0:2048]
    ohbc = t["qw"][:, 0:2048]
    hm = t["qta"][:, 0:2048]
    lgsb = t["scrF"][0:8, 0:2048]
    lgT = c.m(M_X1, M_X1 + 128)
    mx = c.m(M_X1 + 128, M_X1 + 144)
    ohT = c.m(M_X1 + 144, M_X1 + 272)
    dsc = c.m(M_X1 + 272, M_X1 + 280)
    oh8 = t["scr"][0:8, 4096:6144]

    b_kl = c.cnt["kl"]

    @block.gpsimd
    def _(g):
        c.w(g, "vv")
        c.w(g, "tt")
        c.inc("kl", g.dma_start(out=rtb.rearrange("p (kt m) -> p kt m", m=8),
                                in_=io["rt_t"].transpose([1, 0, 2])), 16)
        c.inc("kl", g.dma_start(out=rmb, in_=io["rmat"]), 16)
        c.inc("kl", g.dma_start(out=atb.rearrange("p (kt m) -> p kt m", m=128),
                                in_=io["at_t"].transpose([1, 0, 2])), 16)
        c.inc("kl", g.dma_start(out=bwb, in_=io["bwt_t"]), 16)

    @block.tensor
    def _(tt):
        c.w(tt, "kl", b_kl + 64)
        c.w(tt, "ss")
        c.w(tt, "vv")
        for ch in range(4):
            for kt in range(NKT):
                ins = tt.matmul(c.ps(ch % 2, 512, 8), rtb[:, kt * 8:kt * 8 + 8],
                                c.at(kt, ch * 512, ch * 512 + 512),
                                start=(kt == 0), stop=(kt == NKT - 1))
            c.inc("tt", ins)
            if ch >= 1:
                c.w(tt, "ss", c.cnt["ss"] - 1)

    b_tt1 = c.cnt["tt"]

    @block.scalar
    def _(s):
        for ch in range(4):
            c.w(s, "tt", b_tt1 - 4 + ch + 1)
            c.inc("ss", s.copy(out=lgsb[0:8, ch * 512:ch * 512 + 512], in_=c.ps(ch % 2, 512, 8)))

    @block.tensor
    def _(tt):
        c.w(tt, "ss")
        for tt_ in range(16):
            ins = tt.transpose(c.ps(2, 128)[0:128, tt_ * 8:tt_ * 8 + 8],
                               lgsb[0:8, tt_ * 128:tt_ * 128 + 128],
                               t["ident_f"][0:8, 0:8])
        c.inc("tt", ins)

    @block.scalar
    def _(s):
        c.w(s, "tt")
        c.inc("ss", s.copy(out=lgT, in_=c.ps(2, 128)))

    @block.vector
    def _(v):
        c.w(v, "ss")
        for tt_ in range(16):
            c.w(v, "vv")
            c.inc("vv", v.reduce_max(out=mx[:, tt_:tt_ + 1], in_=lgT[:, tt_ * 8:tt_ * 8 + 8],
                                     axis=mybir.AxisListType.X))
            c.w(v, "vv")
            c.inc("vv", v.tensor_scalar(out=ohT[:, tt_ * 8:tt_ * 8 + 8],
                                        in0=lgT[:, tt_ * 8:tt_ * 8 + 8],
                                        scalar1=mx[:, tt_:tt_ + 1], scalar2=None,
                                        op0=ALU.subtract))
            c.w(v, "vv")
            c.inc("vv", v.tensor_scalar(out=ohT[:, tt_ * 8:tt_ * 8 + 8],
                                        in0=ohT[:, tt_ * 8:tt_ * 8 + 8],
                                        scalar1=1e30, scalar2=1.0, op0=ALU.mult, op1=ALU.add))
            c.w(v, "vv")
            c.inc("vv", v.tensor_scalar(out=ohT[:, tt_ * 8:tt_ * 8 + 8],
                                        in0=ohT[:, tt_ * 8:tt_ * 8 + 8],
                                        scalar1=0.0, scalar2=1.0, op0=ALU.max, op1=ALU.min))

    b_tt2 = c.cnt["tt"]
    b_ss2 = c.cnt["ss"]

    @block.tensor
    def _(tt):
        c.w(tt, "vv")
        for tt_ in range(16):
            if tt_ >= 1:
                c.w(tt, "ss", b_ss2 + tt_)
            ins = tt.transpose(c.ps(3, 128)[0:8, 0:128], ohT[:, tt_ * 8:tt_ * 8 + 8],
                               t["ident_f"][:])
            c.inc("tt", ins)

    @block.scalar
    def _(s):
        for tt_ in range(16):
            c.w(s, "tt", b_tt2 + tt_ + 1)
            c.inc("ss", s.copy(out=oh8[0:8, tt_ * 128:tt_ * 128 + 128],
                               in_=c.ps(3, 128)[0:8, 0:128]))

    b_tt3 = c.cnt["tt"]
    b_ss3 = c.cnt["ss"]

    @block.tensor
    def _(tt):
        c.w(tt, "ss", b_ss3)
        for ch in range(4):
            if ch >= 2:
                c.w(tt, "ss", b_ss3 + ch - 1)
            ins = tt.matmul(c.ps(ch % 2, 512), rmb[0:8, 0:128],
                            oh8[0:8, ch * 512:ch * 512 + 512], start=True, stop=True)
            c.inc("tt", ins)

    @block.scalar
    def _(s):
        for ch in range(4):
            c.w(s, "tt", b_tt3 + ch + 1)
            c.inc("ss", s.copy(out=ohbc[:, ch * 512:ch * 512 + 512], in_=c.ps(ch % 2, 512)))

    b_tt4 = c.cnt["tt"]
    b_vv4 = c.cnt["vv"]

    @block.tensor
    def _(tt):
        c.w(tt, "ss")
        for ch in range(4):
            if ch >= 2:
                c.w(tt, "vv", b_vv4 + ch - 1)
            for kt in range(NKT):
                ins = tt.matmul(c.ps(2 + ch % 2, 512), atb[:, kt * 128:kt * 128 + 128],
                                c.at(kt, ch * 512, ch * 512 + 512),
                                start=(kt == 0), stop=(kt == NKT - 1))
            c.inc("tt", ins)

    @block.vector
    def _(v):
        for ch in range(4):
            c.w(v, "tt", b_tt4 + ch + 1)
            c.w(v, "vv")
            c.inc("vv", v.scalar_tensor_tensor(
                out=hm[:, ch * 512:ch * 512 + 512], in0=c.ps(2 + ch % 2, 512),
                scalar=2.0, in1=ohbc[:, ch * 512:ch * 512 + 512],
                op0=ALU.mult, op1=ALU.mult))

    b_tt5 = c.cnt["tt"]
    b_ss5 = c.cnt["ss"]
    b_wl0f, b_wl1f = c.cnt["wl0"], c.cnt["wl1"]

    @block.gpsimd
    def _(g):
        for ci in range(16):
            _stream(c, g, io["wb_t"], ci, ci % 2,
                    guard=("tt", b_tt5 + 4 * (ci - 1)) if ci >= 2 else None)

    @block.tensor
    def _(tt):
        c.w(tt, "vv")
        gidx = 0
        for ot in range(16):
            c.w(tt, f"wl{ot % 2}", (b_wl0f if ot % 2 == 0 else b_wl1f) + 16 * (ot // 2 + 1))
            buf = t["wb0"] if ot % 2 == 0 else t["wb1"]
            for ch in range(4):
                if gidx >= 4:
                    c.w(tt, "ss", b_ss5 + gidx - 3)
                for kt in range(NKT):
                    tt.matmul(c.ps(gidx % 4), buf[:, kt * 128:kt * 128 + 128],
                              c.at(kt, ch * 512, ch * 512 + 512),
                              start=(kt == 0), stop=False)
                tt.matmul(c.ps(gidx % 4), bwb[:, ot * 128:ot * 128 + 128],
                          hm[:, ch * 512:ch * 512 + 512], start=False, stop=False)
                ins = tt.matmul(c.ps(gidx % 4), t["ident_bf"][:],
                                c.at(ot, ch * 512, ch * 512 + 512),
                                start=False, stop=True)
                c.inc("tt", ins)
                gidx += 1

    @block.scalar
    def _(s):
        for ot in range(16):
            for ch in range(4):
                gk = ot * 4 + ch
                c.w(s, "tt", b_tt5 + gk + 1)
                c.inc("ss", s.activation(out=c.ab(ot, ch * 512, ch * 512 + 512),
                                         in_=c.ps(gk % 4), func=AF.Identity,
                                         bias=t["bias"][:, 128 + ot:129 + ot], scale=1.0))

    _post_ln(c, block, "b", "a", final, y)



def _np32(a):
    return np.ascontiguousarray(np.asarray(a), dtype=np.float32)


def _host_shared(kw):
    g = {}
    wq, wk, wv, wo = (_np32(kw[k]) for k in
                      ["L0_attn_Wq", "L0_attn_Wk", "L0_attn_Wv", "L0_attn_Wo"])
    bkq = _bt(wq.T)
    bkk = _bt(wk.T)
    bkv = _bt(wv.T)
    wkv = np.empty((32, NKT, 128, 128), np.float32)
    wkv[0::2] = bkk
    wkv[1::2] = bkv
    g["wkv_t"] = wkv
    g["wq_t"] = bkq
    g["wo_t"] = _bt(wo.T)
    wg = _np32(kw["L0_ffn_w_gate_up"])
    perm = np.empty((2 * HF,), np.int64)
    for j in range(HF // 128):
        perm[256 * j:256 * j + 128] = np.arange(128 * j, 128 * j + 128)
        perm[256 * j + 128:256 * j + 256] = np.arange(HF + 128 * j, HF + 128 * j + 128)
    g["wgu_t"] = _bt(np.ascontiguousarray(wg[perm].T))
    wp = _np32(kw["L0_ffn_w_proj"])
    g["wpj_t"] = _bt(np.ascontiguousarray(wp.T))
    win = _np32(kw["L1_attn_in_proj"])
    winT = np.ascontiguousarray(win.T)
    g["win_t"] = _bt(winT[:, 0:2 * D])
    g["wv1_t"] = _bt(winT[:, 2 * D:3 * D], 512)
    g["wou_t"] = _bt(np.ascontiguousarray(_np32(kw["L1_attn_out_w"]).T))
    g["wb_t"] = _bt(np.ascontiguousarray(_np32(kw["L1_ffn_W_base"]).T))
    rt = np.ascontiguousarray(_np32(kw["L1_ffn_router_W"]).T)
    g["rt_t"] = np.ascontiguousarray(rt.reshape(NKT, 128, 8))
    at = np.ascontiguousarray(_np32(kw["L1_ffn_A"]).reshape(128, D).T)
    g["at_t"] = np.ascontiguousarray(at.reshape(NKT, 128, 128))
    g["bwt_t"] = np.ascontiguousarray(
        _np32(kw["L1_ffn_Bw"]).transpose(0, 2, 1).reshape(128, D))
    g["rmat"] = (np.arange(8)[:, None] == (np.arange(128)[None, :] // 16)).astype(np.float32)
    g["biases_cat"] = np.concatenate([
        _np32(kw["L0_attn_bq"]).reshape(16, 128),
        _np32(kw["L0_attn_bk"]).reshape(16, 128),
        _np32(kw["L0_attn_bv"]).reshape(16, 128),
        _np32(kw["L0_attn_bo"]).reshape(16, 128),
        _np32(kw["L1_attn_in_proj_bias"]).reshape(48, 128),
        _np32(kw["L1_attn_out_b"]).reshape(16, 128),
        _np32(kw["L1_ffn_b_base"]).reshape(16, 128)], 0)
    g["slopes"] = _np32(kw["L1_attn_alibi_slopes"]).reshape(1, 16)
    g["vbrow"] = _np32(kw["L1_attn_in_proj_bias"])[2 * D:3 * D].reshape(1, D)
    cls_off = [64, 0, 0, -128, 0, -128, -256, -384]
    tms = {}
    p_ = np.arange(128)[:, None]
    n_ = np.arange(512)[None, :]
    for par in (0, 1):
        tm = np.ones((16, 128, 512), np.float32)
        for sp in (0, 1):
            if sp != par:
                continue
            for ic, off in enumerate(cls_off):
                tm[sp * 8 + ic] = (p_ <= n_ + off).astype(np.float32)
        tms[par] = tm.astype(BF)
    g["trimask"] = tms
    return g


def _host_core(core):
    seq, half = core // 2, core % 2
    q0 = half * TOK
    ct = np.zeros((1, 512), np.float32)
    if half == 1:
        ct[0, core - 1] = 1.0
    for h in range(NH):
        N = QCH[h]
        for qf in range(NQC[h]):
            ct[0, 16 + HQ0[h] + qf] = q0 + qf * N + (N - 1) / 2.0
            ct[0, 160 + HQ0[h] + qf] = (q0 + (qf + 1) * N - 1) // 128
    ci = np.zeros((1, 8), np.int32)
    ci[0, 0] = 2 * seq * SEQ * D
    return ct, ci


def _prep(kw):
    g = _host_shared(kw)
    x = _np32(kw["L0_attn_x"])
    per = []
    for core in range(8):
        seq, half = core // 2, core % 2
        ct, ci = _host_core(core)
        d = {"x_in": np.ascontiguousarray(x[seq, half * TOK:(half + 1) * TOK, :].T),
             "coretab": ct, "corei": ci}
        per.append(d)
    return g, per


_IN_BY_PART = {
    "l0a": ["wkv_t", "wq_t", "wo_t"],
    "l0f": ["wgu_t", "wpj_t"],
    "l1a": ["win_t", "wv1_t", "wou_t", "slopes", "vbrow"],
    "l1f": ["wb_t", "rt_t", "at_t", "bwt_t", "rmat"],
}


def _run(parts, g, per, x_override=None):
    nc = build_program(parts)
    maps = []
    for core in range(8):
        d = dict(per[core])
        if x_override is not None:
            d["x_in"] = np.ascontiguousarray(x_override[core])
        d["biases_cat"] = g["biases_cat"]
        for p in parts:
            for n in _IN_BY_PART[p]:
                d[n] = g[n]
            if p == "l1a":
                d["trimask"] = np.asarray(g["trimask"][core % 2])
        maps.append(d)
    res = run_bass_kernel_spmd(nc, maps, core_ids=list(range(8)))
    return [res.results[i]["y"] for i in range(8)]


def kernel(**kw):
    g, per = _prep(kw)
    if FUSED:
        ys = _run(list(ALL_PARTS), g, per)
    else:
        x_ov = None
        for p in ALL_PARTS:
            ys = _run([p], g, per, x_override=x_ov)
            x_ov = ys
    out = np.empty((4, SEQ, D), np.float32)
    for core in range(8):
        seq, half = core // 2, core % 2
        out[seq, half * TOK:(half + 1) * TOK, :] = ys[core].T
    return out
